# version 16
# baseline (speedup 1.0000x reference)
import os
import numpy as np
import ml_dtypes

DBG_OUT = [s for s in os.environ.get("KDBG", "").split(",") if s]
from contextlib import ExitStack

import concourse.bass as bass
import concourse.mybir as mybir
from concourse.bass_utils import run_bass_kernel_spmd
from concourse.ap import AP as RawAP

RACE_DETECT = False
I32 = mybir.dt.int32


def sym_rows(handle, row_sv, nrows, ncols):
    return RawAP(tensor=handle, offset=row_sv * ncols, ap=[[1, nrows * ncols]])


def sym_elem(handle, row_sv, col, ncols):
    return RawAP(tensor=handle, offset=row_sv * ncols + col, ap=[[1, 1], [1, 1]])

F32 = mybir.dt.float32
BF16 = mybir.dt.bfloat16
I8 = mybir.dt.int8
AF = mybir.ActivationFunctionType
ALU = mybir.AluOpType
AX = mybir.AxisListType

MAGIC = 12582912.0
ATANH_HALF = 0.5493061443340549
LN_EPS = 1e-5
RMS_EPS = 1e-8
AQ_CLIP = 1e-4
T_EPS = 1e-8


def full_cfg():
    return derive(dict(B=4, S=4096, D=2048, NH0=32, NKV0=8, HD0=64, INTER=8192,
                       NH1=16, HD1=128, rope_theta=10000.0))


def derive(c):
    c = dict(c)
    c["T"] = c["S"] // 2
    c["GR"] = c["NH0"] // c["NKV0"]
    c["KVD"] = c["NKV0"] * c["HD0"]
    c["DC"] = c["D"] // 128
    c["KVC"] = max(1, c["KVD"] // 128)
    c["TT"] = c["T"] // 128
    c["NG"] = min(512, c["T"])
    c["NTG"] = c["T"] // c["NG"]
    c["IC"] = c["INTER"] // 128
    c["JG"] = c["INTER"] // 512
    c["HC"] = c["NH1"] // 2
    c["AD"] = c["HC"] * c["HD1"]
    c["ADC"] = c["AD"] // 128
    c["KC"] = min(512, c["S"])
    c["SC"] = c["S"] // 128
    c["DG"] = c["D"] // 512
    c["GSZ"] = min(2, c["TT"])
    c["HPC"] = 2 * c["GR"]
    assert c["T"] % c["NG"] == 0 and c["INTER"] % 512 == 0 and c["D"] % 512 == 0
    return c



def prep_per_core(ins, cfg):
    c = cfg
    B, S, D, I = c["B"], c["S"], c["D"], c["INTER"]
    T, AD = c["T"], c["AD"]
    A = np.ascontiguousarray

    x = np.asarray(ins["L0_attn_hidden_states"])
    wq0 = np.asarray(ins["L0_attn_Wq"]); wk0 = np.asarray(ins["L0_attn_Wk"])
    wv0 = np.asarray(ins["L0_attn_Wv"]); wphi0 = np.asarray(ins["L0_attn_Wphi"])
    wo0 = np.asarray(ins["L0_attn_Wo"])
    wg = np.asarray(ins["L0_ffn_w_gate"]); wu = np.asarray(ins["L0_ffn_w_up"])
    wd = np.asarray(ins["L0_ffn_w_down"])
    sg = np.asarray(ins["L0_ffn_s_gate"]); su = np.asarray(ins["L0_ffn_s_up"])
    sd = np.asarray(ins["L0_ffn_s_down"])
    wq1 = np.asarray(ins["L1_attn_wq"]); wk1 = np.asarray(ins["L1_attn_wk"])
    wv1 = np.asarray(ins["L1_attn_wv"]); wo1 = np.asarray(ins["L1_attn_wo"])
    w1 = np.asarray(ins["L1_ffn_w1"]); g1 = np.asarray(ins["L1_ffn_g1"])
    w2 = np.asarray(ins["L1_ffn_w2"]); g2 = np.asarray(ins["L1_ffn_g2"])
    w3 = np.asarray(ins["L1_ffn_w3"]); g3 = np.asarray(ins["L1_ffn_g3"])

    IC, DC = c["IC"], c["DC"]
    shared = dict(
        wq0=A(wq0), wk0=A(wk0), wv0=A(wv0), wphi0=A(wphi0), wo0=A(wo0),
        wgt=A(wg.reshape(D, IC, 128).transpose(1, 0, 2)),
        wut=A(wu.reshape(D, IC, 128).transpose(1, 0, 2)),
        wdt=A(wd.reshape(I, DC, 128).transpose(1, 0, 2)),
        sgs=A(sg.reshape(IC, 128).T), sus=A(su.reshape(IC, 128).T),
        sds=A(sd.reshape(DC, 128).T),
        w1T=A(w1.T), w2T=A(w2.T), w3T=A(w3.T),
        g1v=A(g1), g2v=A(g2), g3v=A(g3),
    )

    HD0 = c["HD0"]
    inv = (1.0 / cfg["rope_theta"] ** (np.arange(0, HD0, 2) / HD0)).astype(np.float64)
    pos = np.arange(S, dtype=np.float64)
    ang = pos[None, :] * inv[:, None]
    cos64 = np.cos(np.concatenate([ang, ang], 0))
    sin64 = np.sin(np.concatenate([ang, ang], 0))
    cos128 = np.concatenate([cos64, cos64], 0).astype(np.float32)
    sin128 = np.concatenate([sin64, sin64], 0).astype(np.float32)

    per_core = []
    for core in range(2 * B):
        b, role = core // 2, core % 2
        hs = slice(role * AD, role * AD + AD)
        d = dict(shared)
        d["x_in"] = A(x[b, role * T:(role + 1) * T, :].astype(np.float32))
        d["cosfm"] = A(cos128[:, role * T:(role + 1) * T])
        d["sinfm"] = A(sin128[:, role * T:(role + 1) * T])
        d["wqTs"] = A(wq1.T[:, hs]); d["wkTs"] = A(wk1.T[:, hs])
        d["wvTs"] = A(wv1.T[:, hs]); d["woTs"] = A(wo1.T[hs, :])
        per_core.append(d)
    return per_core


def inline_consts(nc, cfg):
    KC = cfg["KC"]
    ident = np.eye(128, dtype=ml_dtypes.bfloat16)
    identf = np.eye(128, dtype=np.float32)
    pr = np.zeros((128, 128), np.float32)
    for dd in range(128):
        if dd % 64 < 32:
            pr[dd, dd + 32] = -1.0
        else:
            pr[dd, dd - 32] = 1.0
    protT = pr.T.astype(ml_dtypes.bfloat16)
    masks = np.zeros((128, 4 * KC), np.float32)
    for m in range(4):
        i = np.arange(128)[:, None]
        j = np.arange(KC)[None, :]
        masks[:, m * KC:(m + 1) * KC] = np.where(j <= 128 * m + i, 0.0, -1e9)
    cvals = np.zeros((128, 8), np.float32)
    cvals[:, 0] = LN_EPS
    cvals[:, 1] = 1.0
    masks_t = np.where(np.arange(128)[None, :] >= np.arange(128)[:, None], 0.0, -1e9).astype(np.float32)
    onesb = np.ones((128, 1), ml_dtypes.bfloat16)
    onesf_row = np.ones((1, 128), np.float32)
    onesf_col = np.ones((128, 1), np.float32)
    return dict(
        ident=nc.inline_tensor(ident, name="c_ident"),
        identf=nc.inline_tensor(identf, name="c_identf"),
        protT=nc.inline_tensor(protT, name="c_protT"),
        masks=nc.inline_tensor(masks, name="c_masks"),
        maskt=nc.inline_tensor(masks_t, name="c_maskt"),
        onesb=nc.inline_tensor(onesb, name="c_onesb"),
        onesf_row=nc.inline_tensor(onesf_row, name="c_onesfr"),
        cvals=nc.inline_tensor(cvals, name="c_cvals"),
        onesf_col=nc.inline_tensor(onesf_col, name="c_onesfc"),
    )



def dyn_tensors(c):
    T, D, I, S, AD = c["T"], c["D"], c["INTER"], c["S"], c["AD"]
    NH0, TT = c["NH0"], c["TT"]
    return {
        "xnfm0": ([D, T], BF16),
        "qkfm": ([D, T], BF16),
        "kkfm": ([c["KVC"] * 128, T], BF16),
        "kktm": ([T, c["KVC"] * 128], BF16),
        "vtm0": ([T, c["KVC"] * 128], BF16),
        "qgp": ([128, c["DC"]], F32),
        "c1out": ([256, c["DC"]], F32),
        "lhm": ([NH0, T], F32),
        "mcp": ([NH0, 1], F32),
        "c2out": ([2 * NH0, 1], F32),
        "zp": ([NH0, 1], F32),
        "c3aout": ([2 * NH0, 1], F32),
        "up": ([64, NH0 * 64], F32),
        "c3bout": ([128, NH0 * 64], F32),
        "fuse": ([I, T], BF16),
        "g0tm": ([T, D], F32),
        "ctx": ([D, T], BF16),
        "x1": ([T, D], F32),
        "t1i": ([c["JG"], D, 512], I8),
        "t2i": ([c["JG"], D, 512], I8),
        "t3i": ([c["DG"], I, 512], I8),
        "terns": ([1, 8], F32),
        "xn1fm": ([D, T], BF16),
        "ln1st": ([128, 2 * TT], F32),
        "c4out": ([2 * D, T], BF16),
        "xq1fm": ([D, T], BF16),
        "xq2fm": ([D, T], BF16),
        "k12": ([128, 2 * TT], F32),
        "hdr": ([T, I], F32),
        "g1tm": ([T, D], F32),
        "qfm1": ([AD, S], BF16),
        "kfm1": ([AD, S], BF16),
        "vtm1": ([S, AD], BF16),
        "atfm": ([AD, S], BF16),
        "s1p": ([S, D], F32),
        "c5out": ([T, D], F32),
        "s1pA": ([T, D], F32),
        "s1pB": ([T, D], F32),
        "out": ([T, D], F32),
    }


def static_tensors(c):
    T, D, I, AD = c["T"], c["D"], c["INTER"], c["AD"]
    return {
        "nonce": ([2, 32], I32),
        "xw_c1": ([128, 1], I32), "xw_c2": ([128, 1], I32), "xw_c3a": ([128, 1], I32),
        "xw_c3b": ([128, 1], I32),
        "xw_c4": ([128, (c["D"] + 127) // 128], I32),
        "xw_c5": ([128, (c["T"] + 127) // 128], I32),
        "xr_c5": ([128, (c["T"] + 127) // 128], I32),
        "xf_w": ([2, 16], I32), "xf_r": ([2, 16], I32),
        "x_in": ([T, D], F32),
        "wq0": ([D, D], F32), "wk0": ([D, c["KVD"]], F32), "wv0": ([D, c["KVD"]], F32),
        "wphi0": ([D, D], F32), "wo0": ([D, D], F32),
        "wgt": ([c["IC"], D, 128], I8), "wut": ([c["IC"], D, 128], I8),
        "wdt": ([c["DC"], I, 128], I8),
        "sgs": ([128, c["IC"]], F32), "sus": ([128, c["IC"]], F32),
        "sds": ([128, c["DC"]], F32),
        "w1T": ([D, I], F32), "w2T": ([D, I], F32), "w3T": ([I, D], F32),
        "g1v": ([D], F32), "g2v": ([D], F32), "g3v": ([I], F32),
        "cosfm": ([128, T], F32), "sinfm": ([128, T], F32),
        "wqTs": ([D, AD], F32), "wkTs": ([D, AD], F32), "wvTs": ([D, AD], F32),
        "woTs": ([AD, D], F32),
    }



class M:

    def __init__(self, nc, cfg, fused):
        self.nc = nc
        self.c = cfg
        self.fused = fused
        self.counts = {}
        self.sems = {}
        self.gstack = ExitStack()
        self.consts = None
        self.dram = {}
        self._const_sb = None
        self.eids = {}
        self.polreg = None
        self.ixcache = {}
        self.xbounce = None

    def sem(self, name):
        if name not in self.sems:
            self.sems[name] = self.gstack.enter_context(self.nc.semaphore(name))
            self.counts[name] = 0
        return self.sems[name]

    def i(self, inst, s, n=1):
        h = self.sem(s)
        inst.then_inc(h, n)
        self.counts[s] += n
        return self.counts[s]

    def w(self, eng, s, tok=None):
        h = self.sem(s)
        eng.wait_ge(h, self.counts[s] if tok is None else tok)

    def dma(self, q, eng, out, in_, fence=True, stream=None, **kw):
        s = q if stream is None else f"{q}s{stream}"
        tok = self.i(eng.dma_start(out=out, in_=in_, **kw), s, 16)
        if fence:
            self.w(eng, s, tok)
        return (s, tok)

    def wt(self, eng, tok):
        self.w(eng, tok[0], tok[1])


PHASES = {}


def phase(name, statics=(), reads=(), writes=()):
    def deco(fn):
        PHASES[name] = dict(statics=list(statics), reads=list(reads),
                            writes=list(writes), emit=fn, name=name)
        return fn
    return deco


COLLS = {}


def coll(name, src, dst, kind, op, host):
    COLLS[name] = dict(name=name, src=src, dst=dst, kind=kind, op=op, host=host)



def _gate(m, eng, key):
    if key in m.free:
        s, tok = m.free[key]
        m.w(eng, s, tok)


def _mark(m, key, s):
    m.free[key] = (s, m.counts[s])


def _markt(m, key, tok):
    m.free[key] = tok


def fm_ap(ap, p=128):
    return ap.rearrange("(c p) n -> p c n", p=p)


def sb3(ap, nch):
    return ap.rearrange("p (c n) -> p c n", c=nch)



def emit_ln(m, blk, ps, src, dstfm, dstst=None):
    nc, c = m.nc, m.c
    D, DC, TT = c["D"], c["DC"], c["TT"]
    xt = [ps.enter_context(nc.sbuf_tensor(f"ln_x{i}", [128, D], F32)) for i in range(2)]
    sq = ps.enter_context(nc.sbuf_tensor("ln_sq", [128, D], F32))
    st = ps.enter_context(nc.sbuf_tensor("ln_st", [128, 8], F32))
    xnt = [ps.enter_context(nc.sbuf_tensor(f"ln_xnt{i}", [128, D], BF16)) for i in range(2)]
    stg = [ps.enter_context(nc.sbuf_tensor(f"ln_stg{i}", [128, D], BF16)) for i in range(2)]
    stats = ps.enter_context(nc.sbuf_tensor("ln_stats", [128, 2 * TT], F32))
    pbf = [ps.enter_context(nc.psum_tensor(f"ln_p{i}", [128, 128], BF16)) for i in range(2)]
    xsrc = m.dram[src]
    xdst = m.dram[dstfm]

    for tt in range(TT):
        bx = xt[tt % 2]
        bn = xnt[tt % 2]
        bs = stg[tt % 2]

        def qload(q, tt=tt, bx=bx):
            _gate(m, q, f"ln_x{tt % 2}")
            m.dma("qsy", q, out=bx[:], in_=xsrc[tt * 128:(tt + 1) * 128, :])
        blk.sync(qload)

        def v1(v, tt=tt, bx=bx):
            m.w(v, "qsy")
            m.i(v.reduce_sum(st[:, 0:1], bx[:], axis=AX.X), "dve")
        blk.vector(v1)

        def s1(s, tt=tt, bx=bx):
            m.w(s, "dve")
            _gate(m, s, "ln_sq")
            m.i(s.activation(sq[:], bx[:], AF.Square, accum_out=st[:, 1:2]), "act")
        blk.scalar(s1)

        def v2(v, tt=tt):
            m.w(v, "act")
            mcol = stats[:, 2 * tt:2 * tt + 1]
            m.i(v.tensor_scalar(mcol, st[:, 0:1], 1.0 / D, None, ALU.mult), "dve")
            m.w(v, "dve")
            m.i(v.tensor_tensor(st[:, 3:4], mcol, mcol, ALU.mult), "dve")
            m.w(v, "dve")
            m.i(v.scalar_tensor_tensor(st[:, 4:5], st[:, 1:2], 1.0 / D, st[:, 3:4],
                                       ALU.mult, ALU.subtract), "dve")
            _mark(m, "ln_sq", "dve")
        blk.vector(v2)

        def s2(s, tt=tt):
            m.w(s, "dve")
            m.i(s.activation(st[:, 5:6], st[:, 4:5], AF.Sqrt,
                             bias=m.sbc["cvals"][:, 0:1]), "act")
        blk.scalar(s2)

        def v3(v, tt=tt, bx=bx, bn=bn):
            m.w(v, "act")
            rcol = stats[:, 2 * tt + 1:2 * tt + 2]
            mcol = stats[:, 2 * tt:2 * tt + 1]
            m.i(v.reciprocal(rcol, st[:, 5:6]), "dve")
            m.w(v, "dve")
            _gate(m, v, f"ln_xnt{tt % 2}")
            m.i(v.tensor_scalar(bn[:], bx[:], mcol, rcol, ALU.subtract, ALU.mult), "dve")
            _mark(m, f"ln_x{tt % 2}", "dve")
        blk.vector(v3)

        for cc in range(DC):
            def tb(t, tt=tt, cc=cc, bn=bn):
                m.w(t, "dve")
                _gate(m, t, f"ln_p{cc % 2}")
                m.i(t.transpose(pbf[cc % 2][:], bn[:, cc * 128:(cc + 1) * 128],
                                m.sbc["ident"][:]), "pe")
            blk.tensor(tb)

            def ve(v, tt=tt, cc=cc, bs=bs):
                m.w(v, "pe")
                if cc == 0:
                    _gate(m, v, f"ln_stg{tt % 2}")
                m.i(v.tensor_copy(bs[:, cc * 128:(cc + 1) * 128], pbf[cc % 2][:]), "dve")
                _mark(m, f"ln_p{cc % 2}", "dve")
                if cc == DC - 1:
                    _mark(m, f"ln_xnt{tt % 2}", "dve")
            blk.vector(ve)

        def qst(q, tt=tt, bs=bs):
            m.w(q, "dve")
            m.dma("qsy", q, out=fm_ap(xdst[:, tt * 128:(tt + 1) * 128]),
                  in_=sb3(bs[:], DC))
            _mark(m, f"ln_stg{tt % 2}", "qsy")
        blk.sync(qst)

    if dstst is not None:
        def qst2(q):
            m.w(q, "dve")
            m.dma("qsy", q, out=m.dram[dstst][:], in_=stats[:])
        blk.sync(qst2)


PHASES["ln0"] = dict(statics=["x_in"], reads=[], writes=["xnfm0"], name="ln0",
                     emit=lambda m, blk, ps: emit_ln(m, blk, ps, "x_in", "xnfm0"))
PHASES["ln1"] = dict(statics=[], reads=["x1"], writes=["xn1fm", "ln1st"], name="ln1",
                     emit=lambda m, blk, ps: emit_ln(m, blk, ps, "x1", "xn1fm", "ln1st"))



COLLS["c1"] = dict(src="qgp", dst="c1out", kind="AllGather", op=ALU.bypass)
COLLS["c2"] = dict(src="mcp", dst="c2out", kind="AllGather", op=ALU.bypass)
COLLS["c3a"] = dict(src="zp", dst="c3aout", kind="AllGather", op=ALU.bypass)
COLLS["c3b"] = dict(src="up", dst="c3bout", kind="AllGather", op=ALU.bypass)
COLLS["c4"] = dict(src="xn1fm", dst="c4out", kind="AllGather", op=ALU.bypass)
COLLS["c5"] = dict(src="s1p", dst="c5out", kind="ReduceScatter", op=ALU.add)


def coll_host(name, env, cfg):
    cl = COLLS[name]
    ncores = 2 * cfg["B"]
    for c in range(ncores):
        p0, p1 = (c // 2) * 2, (c // 2) * 2 + 1
        if cl["kind"] == "AllGather":
            env[c][cl["dst"]] = np.concatenate(
                [env[p0][cl["src"]], env[p1][cl["src"]]], axis=0)
        else:
            T = cfg["T"]
            role = c % 2
            env[c]["s1pA"] = np.ascontiguousarray(env[p0]["s1p"][role * T:(role + 1) * T])
            env[c]["s1pB"] = np.ascontiguousarray(env[p1]["s1p"][role * T:(role + 1) * T])


def _idx_sb(m, ps, name, cols, rows=128):
    nc = m.nc
    t = ps.enter_context(nc.sbuf_tensor(f"ix_{name}", [rows, cols], I32))
    with nc.Block() as blk:
        def ld(g):
            m.dma("qgp", g, out=t[:], in_=m.dram[name][:])
        blk.gpsimd(ld)
    return t


def _xbounce(m, ps, cols, dt):
    if m.xbounce is None:
        m.xbounce = ps.enter_context(m.nc.sbuf_tensor("xbounce", [128, 2048], F32))
    if dt == F32:
        return m.xbounce[:, 0:cols]
    return m.xbounce[:].bitcast(dt)[:, 0:cols]


def _scatter_rows(m, g, dst_h, src_ap, rows, cols, idx_t):
    IOA = bass.IndirectOffsetOnAxis
    bt = _xbounce(m, m.collps, cols, src_ap.dtype)
    nch = (rows + 127) // 128
    for ch in range(nch):
        r0 = ch * 128
        r1 = min(rows, r0 + 128)
        n = r1 - r0
        m.dma("qgp", g, out=bt[0:n, :], in_=src_ap[r0:r1, :])
        inst = g.indirect_dma_start(
            out=dst_h[:], out_offset=IOA(idx_t[0:n, ch:ch + 1], 0),
            in_=bt[0:n, :], in_offset=None)
        m.i(inst, "qgp", 16)
        m.w(g, "qgp")


def _gather_rows(m, g, dst_ap, src_h, rows, cols, idx_t):
    IOA = bass.IndirectOffsetOnAxis
    bt = _xbounce(m, m.collps, cols, dst_ap.dtype)
    nch = (rows + 127) // 128
    for ch in range(nch):
        r0 = ch * 128
        r1 = min(rows, r0 + 128)
        n = r1 - r0
        inst = g.indirect_dma_start(
            out=bt[0:n, :], out_offset=None,
            in_=src_h[:], in_offset=IOA(idx_t[0:n, ch:ch + 1], 0))
        m.i(inst, "qgp", 16)
        m.w(g, "qgp")
        m.dma("qgp", g, out=dst_ap[r0:r1, :], in_=bt[0:n, :])


def emit_coll(m, name):
    with ExitStack() as collps:
        _emit_coll_inner(m, name, collps)


def _emit_coll_inner(m, name, collps):
    cl = COLLS[name]
    m.collps = collps
    m.xbounce = None
    nc, c = m.nc, m.c
    Pool = mybir.EngineType.Pool
    dyn = dyn_tensors(c)
    eid = m.eids.setdefault(name, len(m.eids))
    src_h, dst_h = m.dram[cl["src"]], m.dram[cl["dst"] if cl["kind"] == "AllGather" else "s1pA"]

    if "shflags" not in m.dram:
        m.dram["shflags"] = nc.dram_tensor("shflags", [32, 2], I32, addr_space="Shared")
        m.sh_nonce = m.gstack.enter_context(nc.sbuf_tensor("sh_nonce", [2, 32], I32))
        m.sh_fsb = m.gstack.enter_context(nc.sbuf_tensor("sh_fsb", [2, 32], I32))
        with nc.Block() as blk0:
            def ld0(g):
                m.dma("qgp", g, out=m.sh_nonce[:], in_=m.dram["nonce"][:])
            blk0.gpsimd(ld0)
    shf = m.dram["shflags"]
    fw = _idx_sb(m, collps, "xf_w", 16, rows=2)
    fr = _idx_sb(m, collps, "xf_r", 16, rows=2)
    m.fr_tile = fr

    if cl["kind"] == "AllGather":
        rows, cols = dyn[cl["src"]][0]
        sh = nc.dram_tensor(f"shx_{name}", [2 * rows, cols], dyn[cl["src"]][1],
                            addr_space="Shared")
        wix = _idx_sb(m, collps, "xw_" + name, (rows + 127) // 128)
        with nc.Block() as blk:
            def gw(g):
                m.w(g, "qsy"); m.w(g, "qsc"); m.w(g, "qgp")
                _scatter_rows(m, g, sh, src_h[:], rows, cols, wix)
                inst = g.indirect_dma_start(
                    out=shf[:],
                    out_offset=bass.IndirectOffsetOnAxis(fw[0:2, eid:eid + 1], 0),
                    in_=m.sh_nonce[0:2, 2 * eid:2 * eid + 2], in_offset=None)
                m.i(inst, "qgp", 16)
                m.w(g, "qgp")
            blk.gpsimd(gw)
        _emit_poll(m, name, eid)
        with nc.Block() as blk2:
            def gr(g):
                m.dma("qgp", g, out=dst_h[:], in_=sh[:])
            blk2.gpsimd(gr)
    else:
        T, D = c["T"], c["D"]
        sh = nc.dram_tensor(f"shx_{name}", [2 * T, D], F32, addr_space="Shared")
        wix = _idx_sb(m, collps, "xw_c5", (T + 127) // 128)
        rix = _idx_sb(m, collps, "xr_c5", (T + 127) // 128)
        bteid = nc.dram_tensor("c5_bounce", [T, D], F32, kind="Internal")
        with nc.Block() as blkb:
            def gb(g):
                m.w(g, "qsy"); m.w(g, "qsc"); m.w(g, "qgp")
                _gather_rows(m, g, bteid[:], m.dram["s1p"], T, D, wix)
                _scatter_rows(m, g, sh, bteid[:], T, D, wix)
                inst = g.indirect_dma_start(
                    out=shf[:],
                    out_offset=bass.IndirectOffsetOnAxis(fw[0:2, eid:eid + 1], 0),
                    in_=m.sh_nonce[0:2, 2 * eid:2 * eid + 2], in_offset=None)
                m.i(inst, "qgp", 16)
                m.w(g, "qgp")
            blkb.gpsimd(gb)
        _emit_poll(m, name, eid)
        with nc.Block() as blk2:
            def gr(g):
                _gather_rows(m, g, m.dram["s1pA"][:], m.dram["s1p"], T, D, rix)
                _gather_rows(m, g, m.dram["s1pB"][:], sh, T, D, rix)
            blk2.gpsimd(gr)
    nc.all_engine_barrier()


def _emit_poll(m, name, eid):
    nc = m.nc
    Pool = mybir.EngineType.Pool
    eng = [Pool]
    psem = m.sem(f"poll_{name}")
    qtot = m.counts["qgp"]
    fr = m.fr_tile
    if m.polreg is None:
        m.polreg = nc.alloc_registers("pollreg", engines=eng)
        m.flagreg = nc.alloc_registers("flagreg", engines=eng)
        m.noncereg = nc.alloc_registers("noncereg", engines=eng)
    nc.regs_mov(m.polreg, 0)
    nc.regs_load(m.noncereg, m.sh_nonce[0:1, 2 * eid:2 * eid + 1])
    loop, done = f"poll_{name}_loop", f"poll_{name}_done"
    nc.br(loop, engines=eng)
    with nc.body(loop, valid_engines=eng):
        gp = nc.gpsimd
        gp.wait_ge(m.sems["qgp"], qtot)
        gp.indirect_dma_start(
            out=m.sh_fsb[0:2, 0:2], out_offset=None,
            in_=m.dram["shflags"][:],
            in_offset=bass.IndirectOffsetOnAxis(fr[0:2, eid:eid + 1], 0)).then_inc(psem, 16)
        nc.regs_alu(m.polreg, m.polreg, 16, op=ALU.add)
        gp.wait_ge(psem, m.polreg)
        nc.regs_load(m.flagreg, m.sh_fsb[0:1, 0:1])
        nc.br_cmp(m.flagreg, m.noncereg, on_true=done, on_false=loop,
                  comp_op="IS_EQ", engines=eng)
    nc.switch_bb(done)



def build_launch(cfg, group, fused):
    nc = bass.Bass("TRN2", target_bir_lowering=False, debug=False,
                   detect_race_conditions=RACE_DETECT)
    _orig_sb, _orig_ps = nc.sbuf_tensor, nc.psum_tensor
    def _sb(*a, **kw):
        kw.setdefault("allow_name_mangling", True)
        return _orig_sb(*a, **kw)
    def _pp(*a, **kw):
        kw.setdefault("allow_name_mangling", True)
        return _orig_ps(*a, **kw)
    nc.sbuf_tensor, nc.psum_tensor = _sb, _pp
    m = M(nc, cfg, fused)
    m.free = {}
    stat_reg = static_tensors(cfg)
    dyn_reg = dyn_tensors(cfg)
    statics, reads, writes = [], [], []
    on_dev_coll = fused or len(group) > 1
    if on_dev_coll and any(nm in COLLS for nm in group):
        statics.extend(["nonce", "xw_c1", "xw_c2", "xw_c3a", "xw_c3b", "xw_c4",
                        "xw_c5", "xr_c5", "xf_w", "xf_r"])
    for nm in group:
        if nm in COLLS:
            cl = COLLS[nm]
            if on_dev_coll:
                if cl["src"] not in writes and cl["src"] not in reads:
                    reads.append(cl["src"])
                dsts = ["s1pA", "s1pB"] if nm == "c5" else [cl["dst"]]
                for dd in dsts:
                    if dd not in writes:
                        writes.append(dd)
            continue
        ph = PHASES[nm]
        for s in ph["statics"]:
            if s not in statics:
                statics.append(s)
        for r in ph["reads"]:
            if r not in writes and r not in reads:
                reads.append(r)
        for wn in ph["writes"]:
            if wn not in writes:
                writes.append(wn)
    for s in statics:
        shp, dt = stat_reg[s]
        m.dram[s] = nc.dram_tensor(s, shp, dt, kind="ExternalInput")
    if fused:
        reads = [r for r in reads if r not in ("s1pA", "s1pB")]
    for r in reads:
        shp, dt = dyn_reg[r]
        m.dram[r] = nc.dram_tensor(r, shp, dt, kind="ExternalInput")
    for wn in writes:
        shp, dt = dyn_reg[wn]
        kind = "ExternalOutput" if (not fused or wn == "out" or wn in DBG_OUT) else "Internal"
        m.dram[wn] = nc.dram_tensor(wn, shp, dt, kind=kind)
    for k, h in inline_consts(nc, cfg).items():
        m.dram[k] = h

    KC = cfg["KC"]
    m.sbc = {
        "ident": m.gstack.enter_context(nc.sbuf_tensor("sb_ident", [128, 128], BF16)),
        "identf": m.gstack.enter_context(nc.sbuf_tensor("sb_identf", [128, 128], F32)),
        "protT": m.gstack.enter_context(nc.sbuf_tensor("sb_protT", [128, 128], BF16)),
        "masks": m.gstack.enter_context(nc.sbuf_tensor("sb_masks", [128, 4 * KC], F32)),
        "maskt": m.gstack.enter_context(nc.sbuf_tensor("sb_maskt", [128, 128], F32)),
        "onesb": m.gstack.enter_context(nc.sbuf_tensor("sb_onesb", [128, 1], BF16)),
        "onesf_row": m.gstack.enter_context(nc.sbuf_tensor("sb_onesfr", [1, 128], F32)),
        "cvals": m.gstack.enter_context(nc.sbuf_tensor("sb_cvals", [128, 8], F32)),
        "onesf_col": m.gstack.enter_context(nc.sbuf_tensor("sb_onesfc", [128, 1], F32)),
    }
    with nc.Block() as blk:
        def pre(q):
            for k in m.sbc:
                m.dma("qsy", q, out=m.sbc[k][:], in_=m.dram[k][:])
            m.w(q, "qsy")
        blk.sync(pre)
    nc.all_engine_barrier()

    def _dma_drain():
        with nc.Block() as dblk:
            def dsy(q):
                for s in list(m.sems):
                    if s.startswith("qsy"):
                        m.w(q, s)
            dblk.sync(dsy)
            def dsc(s_):
                for s in list(m.sems):
                    if s.startswith("qsc"):
                        m.w(s_, s)
            dblk.scalar(dsc)
            def dgp(g):
                for s in list(m.sems):
                    if s.startswith("qgp"):
                        m.w(g, s)
            dblk.gpsimd(dgp)

    for nm in group:
        if nm in COLLS:
            if on_dev_coll:
                with nc.named_scope("X_" + nm):
                    emit_coll(m, nm)
            continue
        with nc.named_scope("P_" + nm):
            with ExitStack() as ps:
                with nc.Block() as blk:
                    PHASES[nm]["emit"](m, blk, ps)
            _dma_drain()
            nc.all_engine_barrier()

    with nc.Block() as blk:
        def fsy(q):
            for s in list(m.sems):
                if s.startswith("qsy"):
                    m.w(q, s)
        blk.sync(fsy)
        def fsc(s_):
            for s in list(m.sems):
                if s.startswith("qsc"):
                    m.w(s_, s)
        blk.scalar(fsc)
        def fgp(g):
            for s in list(m.sems):
                if s.startswith("qgp") or s.startswith("poll") or s == "qcl":
                    m.w(g, s)
        blk.gpsimd(fgp)
    m.gstack.close()
    io = dict(statics=statics, reads=reads,
              outputs=[wn for wn in writes if (not fused or wn == "out" or wn in DBG_OUT)])
    return nc, io



PHASE_ORDER = [
    "tern", "ln0", "qkv0", "c1", "logits", "c2", "zrow", "c3a", "uacc", "c3b",
    "ffn0a", "ffn0b", "ctx0", "out0", "ln1", "c4", "quant1", "uh1", "hdown",
    "proj1", "flash", "s1proj", "c5", "comb1",
]
LAUNCH_PLAN_UNFUSED = [[p] for p in PHASE_ORDER]
LAUNCH_PLAN_FUSED = [PHASE_ORDER]
LAUNCH_PLAN = LAUNCH_PLAN_FUSED


def kernel_impl(inputs, cfg, runner, plan=None, fused=None, env0=None):
    plan = plan if plan is not None else LAUNCH_PLAN
    per_core = prep_per_core(inputs, cfg)
    base = int(np.random.default_rng().integers(1, 2 ** 30))
    nonce = np.repeat(base + np.arange(16, dtype=np.int32), 2).reshape(1, 32).astype(np.int32)
    nonce = np.concatenate([nonce, nonce], axis=0)

    def colmajor(vals):
        n = len(vals)
        nch = (n + 127) // 128
        a = np.zeros((128, nch), np.int32)
        for i, v in enumerate(vals):
            a[i % 128, i // 128] = v
        return a

    dynr = dyn_tensors(cfg)
    for ci, d in enumerate(per_core):
        rank = ci % 2
        prank = 1 - rank
        d["nonce"] = nonce
        for nm2 in ("c1", "c2", "c3a", "c3b", "c4"):
            rows = dynr[COLLS[nm2]["src"]][0][0]
            d["xw_" + nm2] = colmajor([rank * rows + i for i in range(rows)])
        T = cfg["T"]
        d["xw_c5"] = colmajor([prank * T + i for i in range(T)])
        d["xr_c5"] = colmajor([rank * T + i for i in range(T)])
        fwv = np.array([2 * e + rank for e in range(16)], np.int32)
        frv = np.array([2 * e + prank for e in range(16)], np.int32)
        d["xf_w"] = np.stack([fwv, fwv])
        d["xf_r"] = np.stack([frv, frv])
    ncores = 2 * cfg["B"]
    env = [dict(e) for e in env0] if env0 is not None else [dict() for _ in range(ncores)]
    if fused is None:
        fused = len(plan) == 1 and len(plan[0]) > 4
    for group in plan:
        dev = [g for g in group if g not in COLLS]
        if not dev and not fused:
            for g in group:
                coll_host(g, env, cfg)
            continue
        nc, io = build_launch(cfg, group, fused)
        in_maps = []
        for c in range(ncores):
            im = {k: per_core[c][k] for k in io["statics"]}
            im.update({k: env[c][k] for k in io["reads"]})
            in_maps.append(im)
        res = runner(nc, in_maps, core_ids=list(range(ncores)))
        for c in range(ncores):
            for k in io["outputs"]:
                env[c][k] = np.asarray(res.results[c][k])
        if not fused and len(group) == 1:
            for g in group:
                if g in COLLS:
                    coll_host(g, env, cfg)
    B, T, D, S = cfg["B"], cfg["T"], cfg["D"], cfg["S"]
    if not all("out" in e for e in env):
        return None, env
    out = np.zeros((B, S, D), np.float32)
    for c in range(ncores):
        b, role = c // 2, c % 2
        out[b, role * T:(role + 1) * T, :] = env[c]["out"]
    return out, env


def kernel(**inputs):
    out, _ = kernel_impl(inputs, full_cfg(), run_bass_kernel_spmd)
    return out



def emit_qkv0(m, blk, ps):
    nc, c = m.nc, m.c
    D, DC, T, NG, NTG, TT = c["D"], c["DC"], c["T"], c["NG"], c["NTG"], c["TT"]
    KVD, KVC = c["KVD"], c["KVC"]
    xn = ps.enter_context(nc.sbuf_tensor("qk_xn", [128, DC * T], BF16))
    wsb = ps.enter_context(nc.sbuf_tensor("qk_w", [128, DC * D], BF16))
    cosb = ps.enter_context(nc.sbuf_tensor("qk_cos", [128, T], F32))
    sinb = ps.enter_context(nc.sbuf_tensor("qk_sin", [128, T], F32))
    eA = ps.enter_context(nc.sbuf_tensor("qk_eA", [128, NG], F32))
    eB = ps.enter_context(nc.sbuf_tensor("qk_eB", [128, NG], F32))
    eC = ps.enter_context(nc.sbuf_tensor("qk_eC", [128, NG], F32))
    qpre = ps.enter_context(nc.sbuf_tensor("qk_qpre", [128, NG], BF16))
    stg = [ps.enter_context(nc.sbuf_tensor(f"qk_stg{i}", [128, T], BF16)) for i in range(2)]
    tstg = ps.enter_context(nc.sbuf_tensor("qk_tstg", [128, 128], BF16))
    qg = ps.enter_context(nc.sbuf_tensor("qk_qg", [128, DC], F32))
    qgt = ps.enter_context(nc.sbuf_tensor("qk_qgt", [128, 1], F32))
    vev = ps.enter_context(nc.sbuf_tensor("qk_vev", [128, max(1, m.c["KVD"])], BF16))
    pA = [ps.enter_context(nc.psum_tensor(f"qk_pA{i}", [128, NG], F32)) for i in range(2)]
    pR = [ps.enter_context(nc.psum_tensor(f"qk_pR{i}", [128, NG], F32)) for i in range(2)]
    pT = [ps.enter_context(nc.psum_tensor(f"qk_pT{i}", [128, 128], BF16)) for i in range(2)]

    def pre(q):
        m.dma("qsy", q, out=sb3(xn[:], DC), in_=fm_ap(m.dram["xnfm0"][:]))
        m.dma("qsy", q, out=cosb[:], in_=m.dram["cosfm"][:])
        m.dma("qsy", q, out=sinb[:], in_=m.dram["sinfm"][:])
    blk.sync(pre)

    def wload(g, wname, cols):
        def f(gp):
            m.w(gp, "pe")
            m.dma("qgp", gp, out=sb3(wsb[:, 0:DC * cols], DC),
                  in_=fm_ap(m.dram[wname][:]))
        blk.gpsimd(f)

    def proj_rope(outdram, nchunks, cols, do_qg):
        for co in range(nchunks):
            bs = stg[co % 2]
            for tg in range(NTG):
                i = (co * NTG + tg) % 2

                def b1(t, co=co, tg=tg, i=i):
                    m.w(t, "qsy"); m.w(t, "qgp")
                    _gate(m, t, f"qk_pA{i}")
                    for cc in range(DC):
                        inst = t.matmul(pA[i][:], wsb[:, cc * cols + co * 128: cc * cols + co * 128 + 128],
                                        xn[:, cc * T + tg * NG: cc * T + (tg + 1) * NG],
                                        start=(cc == 0), stop=(cc == DC - 1))
                    m.i(inst, "pe")
                blk.tensor(b1)

                def b2(v, co=co, tg=tg, i=i):
                    m.w(v, "pe")
                    _gate(m, v, "qk_qpre")
                    m.i(v.tensor_copy(qpre[:], pA[i][:]), "dve")
                    _gate(m, v, "qk_eA")
                    m.i(v.tensor_tensor(eA[:], pA[i][:], cosb[:, tg * NG:(tg + 1) * NG], ALU.mult), "dve")
                    _mark(m, f"qk_pA{i}", "dve")
                blk.vector(b2)

                def b3(t, co=co, tg=tg, i=i):
                    m.w(t, "dve")
                    _gate(m, t, f"qk_pR{i}")
                    m.i(t.matmul(pR[i][:], m.sbc["protT"][:], qpre[:], start=True, stop=True), "pe")
                blk.tensor(b3)

                def b4(v, co=co, tg=tg, i=i):
                    m.w(v, "pe")
                    _mark(m, "qk_qpre", "pe")
                    _gate(m, v, "qk_eB")
                    m.i(v.tensor_tensor(eB[:], pR[i][:], sinb[:, tg * NG:(tg + 1) * NG], ALU.mult), "dve")
                    _mark(m, f"qk_pR{i}", "dve")
                    m.w(v, "dve")
                    _gate(m, v, "qk_eC")
                    m.i(v.tensor_add(eC[:], eA[:], eB[:]), "dve")
                    _mark(m, "qk_eA", "dve")
                    m.w(v, "dve")
                    m.i(v.tensor_scalar_min(eB[:], eC[:], 0.0), "dve")
                blk.vector(b4)

                def b5(s, co=co, tg=tg):
                    m.w(s, "dve")
                    m.i(s.activation(eA[:], eB[:], AF.Exp), "act")
                blk.scalar(b5)

                def b6(v, co=co, tg=tg, bs=bs, do_qg=do_qg):
                    m.w(v, "act")
                    _mark(m, "qk_eB", "act")
                    if tg == 0:
                        _gate(m, v, f"qk_stg{co % 2}")
                    m.i(v.scalar_tensor_tensor(bs[:, tg * NG:(tg + 1) * NG], eC[:], 0.0, eA[:],
                                               ALU.max, ALU.add), "dve")
                    _mark(m, "qk_eC", "dve")
                    _mark(m, "qk_eA", "dve")
                    if do_qg:
                        m.w(v, "dve")
                        if tg == 0:
                            m.i(v.reduce_sum(qg[:, co:co + 1], bs[:, 0:NG], axis=AX.X), "dve")
                        else:
                            m.i(v.reduce_sum(qgt[:], bs[:, tg * NG:(tg + 1) * NG], axis=AX.X), "dve")
                            m.w(v, "dve")
                            m.i(v.tensor_add(qg[:, co:co + 1], qg[:, co:co + 1], qgt[:]), "dve")
                blk.vector(b6)

            def bst(q, co=co, bs=bs, outdram=outdram):
                m.w(q, "dve")
                m.dma("qsy", q, out=m.dram[outdram][co * 128:(co + 1) * 128, :], in_=bs[:])
                _mark(m, f"qk_stg{co % 2}", "qsy")
            blk.sync(bst)

            if outdram == "kkfm":
                for tt in range(TT):
                    def btr(t, co=co, tt=tt, bs=bs):
                        m.w(t, "dve")
                        _gate(m, t, f"qk_pT{tt % 2}")
                        m.i(t.transpose(pT[tt % 2][:], bs[:, tt * 128:(tt + 1) * 128],
                                        m.sbc["ident"][:]), "pe")
                    blk.tensor(btr)

                    def bev(v, co=co, tt=tt):
                        m.w(v, "pe")
                        _gate(m, v, "qk_tstg")
                        m.i(v.tensor_copy(tstg[:], pT[tt % 2][:]), "dve")
                        _mark(m, f"qk_pT{tt % 2}", "dve")
                    blk.vector(bev)

                    def bsv(q, co=co, tt=tt):
                        m.w(q, "dve")
                        m.dma("qsy", q, out=m.dram["kktm"][tt * 128:(tt + 1) * 128,
                                                          co * 128:(co + 1) * 128], in_=tstg[:])
                        _mark(m, "qk_tstg", "qsy")
                    blk.sync(bsv)
                _mark(m, f"qk_stg{co % 2}", "pe")

    wload(blk, "wq0", D)
    proj_rope("qkfm", DC, D, True)
    wload(blk, "wk0", KVD)
    proj_rope("kkfm", KVC, KVD, False)
    wload(blk, "wv0", KVD)
    for tt in range(TT):
        i = tt % 2

        def bv1(t, tt=tt, i=i):
            m.w(t, "qgp"); m.w(t, "qsy")
            _gate(m, t, f"qk_pA{i}")
            for cc in range(DC):
                inst = t.matmul(pA[i][:, 0:KVD], xn[:, cc * T + tt * 128: cc * T + tt * 128 + 128],
                                wsb[:, cc * KVD:(cc + 1) * KVD],
                                start=(cc == 0), stop=(cc == DC - 1))
            m.i(inst, "pe")
        blk.tensor(bv1)

        def bv2(v, tt=tt, i=i):
            m.w(v, "pe")
            _gate(m, v, "qk_vev")
            m.i(v.tensor_copy(vev[:], pA[i][:, 0:KVD]), "dve")
            _mark(m, f"qk_pA{i}", "dve")
        blk.vector(bv2)

        def bv3(q, tt=tt):
            m.w(q, "dve")
            m.dma("qsy", q, out=m.dram["vtm0"][tt * 128:(tt + 1) * 128, :], in_=vev[:])
            _mark(m, "qk_vev", "qsy")
        blk.sync(bv3)

    def bqg(q):
        m.w(q, "dve")
        m.dma("qsy", q, out=m.dram["qgp"][:], in_=qg[:])
    blk.sync(bqg)


PHASES["qkv0"] = dict(statics=["wq0", "wk0", "wv0", "cosfm", "sinfm"],
                      reads=["xnfm0"], writes=["qkfm", "kkfm", "kktm", "vtm0", "qgp"],
                      name="qkv0", emit=emit_qkv0)



def emit_logits(m, blk, ps):
    nc, c = m.nc, m.c
    DC, T, NG, NTG = c["DC"], c["T"], c["NG"], c["NTG"]
    NH0, KVC, GR, S = c["NH0"], c["KVC"], c["GR"], c["S"]
    qgt = ps.enter_context(nc.sbuf_tensor("lg_qgt", [128, DC], F32))
    qg2 = ps.enter_context(nc.sbuf_tensor("lg_qg2", [128, DC], F32))
    bsb = ps.enter_context(nc.sbuf_tensor("lg_b", [128, KVC * NH0], BF16))
    kk = ps.enter_context(nc.sbuf_tensor("lg_kk", [128, KVC * T], BF16))
    lg = ps.enter_context(nc.sbuf_tensor("lg_lg", [NH0, T], F32))
    mc = ps.enter_context(nc.sbuf_tensor("lg_mc", [NH0, 1], F32))
    pL = ps.enter_context(nc.psum_tensor("lg_pL", [NH0, NG], F32))

    def pre(q):
        m.dma("qsy", q, out=qgt[:], in_=m.dram["c1out"][0:128, :])
        m.dma("qsy", q, out=qg2[:], in_=m.dram["c1out"][128:256, :])
        m.dma("qsy", q, out=sb3(kk[:], KVC), in_=fm_ap(m.dram["kkfm"][:]))
    blk.sync(pre)

    def v1(v):
        m.w(v, "qsy")
        m.i(v.tensor_add(qgt[:], qgt[:], qg2[:]), "dve")
        m.w(v, "dve")
        m.i(v.tensor_scalar_mul(qgt[:], qgt[:], 1.0 / S), "dve")
        m.w(v, "dve")
        m.i(v.memset(bsb[:], 0.0), "dve")
        for n in range(NH0):
            m.w(v, "dve")
            cb = (n // GR) // 2
            base = 64 * ((n // GR) % 2)
            m.i(v.tensor_copy(bsb[base:base + 64, cb * NH0 + n: cb * NH0 + n + 1],
                              qgt[64 * (n % 2):64 * (n % 2) + 64, n // 2:n // 2 + 1]), "dve")
    blk.vector(v1)

    for tg in range(NTG):
        def t1(t, tg=tg):
            m.w(t, "dve")
            _gate(m, t, "lg_pL")
            for cb in range(KVC):
                inst = t.matmul(pL[:], bsb[:, cb * NH0:(cb + 1) * NH0],
                                kk[:, cb * T + tg * NG: cb * T + (tg + 1) * NG],
                                start=(cb == 0), stop=(cb == KVC - 1))
            m.i(inst, "pe")
        blk.tensor(t1)

        def v2(v, tg=tg):
            m.w(v, "pe")
            m.i(v.tensor_copy(lg[:, tg * NG:(tg + 1) * NG], pL[:]), "dve")
            _mark(m, "lg_pL", "dve")
        blk.vector(v2)

    def v3(v):
        m.w(v, "dve")
        m.i(v.reduce_max(mc[:], lg[:], axis=AX.X), "dve")
    blk.vector(v3)

    def st(q):
        m.w(q, "dve")
        m.dma("qsy", q, out=m.dram["lhm"][:], in_=lg[:])
        m.dma("qsy", q, out=m.dram["mcp"][:], in_=mc[:])
    blk.sync(st)


PHASES["logits"] = dict(statics=[], reads=["c1out", "kkfm"], writes=["lhm", "mcp"],
                        name="logits", emit=emit_logits)


def emit_zrow(m, blk, ps):
    nc, c = m.nc, m.c
    NH0, T = c["NH0"], c["T"]
    ma = ps.enter_context(nc.sbuf_tensor("zr_ma", [NH0, 1], F32))
    mb = ps.enter_context(nc.sbuf_tensor("zr_mb", [NH0, 1], F32))
    lg = ps.enter_context(nc.sbuf_tensor("zr_lg", [NH0, T], F32))
    ed = ps.enter_context(nc.sbuf_tensor("zr_ed", [NH0, T], F32))
    z = ps.enter_context(nc.sbuf_tensor("zr_z", [NH0, 1], F32))

    def pre(q):
        m.dma("qsy", q, out=ma[:], in_=m.dram["c2out"][0:NH0, :])
        m.dma("qsy", q, out=mb[:], in_=m.dram["c2out"][NH0:2 * NH0, :])
        m.dma("qsy", q, out=lg[:], in_=m.dram["lhm"][:])
    blk.sync(pre)

    def v1(v):
        m.w(v, "qsy")
        m.i(v.tensor_max(ma[:], ma[:], mb[:]), "dve")
        m.w(v, "dve")
        m.i(v.tensor_scalar_mul(mb[:], ma[:], -1.0), "dve")
    blk.vector(v1)

    def s1(s):
        m.w(s, "dve")
        m.i(s.activation(ed[:], lg[:], AF.Exp, bias=mb[:], accum_out=z[:]), "act")
    blk.scalar(s1)

    def st(q):
        m.w(q, "act")
        m.dma("qsy", q, out=m.dram["zp"][:], in_=z[:])
    blk.sync(st)


PHASES["zrow"] = dict(statics=[], reads=["c2out", "lhm"], writes=["zp"],
                      name="zrow", emit=emit_zrow)


def emit_uacc(m, blk, ps):
    nc, c = m.nc, m.c
    NH0, T, TT, KVC, KVD, GR, S = c["NH0"], c["T"], c["TT"], c["KVC"], c["KVD"], c["GR"], c["S"]
    ma = ps.enter_context(nc.sbuf_tensor("ua_ma", [NH0, 1], F32))
    mb = ps.enter_context(nc.sbuf_tensor("ua_mb", [NH0, 1], F32))
    za = ps.enter_context(nc.sbuf_tensor("ua_za", [NH0, 1], F32))
    zb = ps.enter_context(nc.sbuf_tensor("ua_zb", [NH0, 1], F32))
    cv = ps.enter_context(nc.sbuf_tensor("ua_cv", [NH0, 1], F32))
    lg = ps.enter_context(nc.sbuf_tensor("ua_lg", [NH0, T], F32))
    ed = ps.enter_context(nc.sbuf_tensor("ua_ed", [NH0, T], F32))
    ebf = ps.enter_context(nc.sbuf_tensor("ua_ebf", [NH0, T], BF16))
    etm = ps.enter_context(nc.sbuf_tensor("ua_etm", [128, TT * NH0], F32))
    kt = ps.enter_context(nc.sbuf_tensor("ua_kt", [128, TT * KVD], BF16))
    vt = ps.enter_context(nc.sbuf_tensor("ua_vt", [128, TT * KVD], BF16))
    skb = [ps.enter_context(nc.sbuf_tensor(f"ua_skb{i}", [128, 64], BF16)) for i in range(2)]
    usb = ps.enter_context(nc.sbuf_tensor("ua_usb", [64, NH0 * 64], F32))
    pE = [ps.enter_context(nc.psum_tensor(f"ua_pE{i}", [128, NH0], BF16)) for i in range(2)]
    pU = ps.enter_context(nc.psum_tensor("ua_pU", [64, NH0 * 64], F32))

    def pre(q):
        m.dma("qsy", q, out=ma[:], in_=m.dram["c2out"][0:NH0, :])
        m.dma("qsy", q, out=mb[:], in_=m.dram["c2out"][NH0:2 * NH0, :])
        m.dma("qsy", q, out=za[:], in_=m.dram["c3aout"][0:NH0, :])
        m.dma("qsy", q, out=zb[:], in_=m.dram["c3aout"][NH0:2 * NH0, :])
        m.dma("qsy", q, out=lg[:], in_=m.dram["lhm"][:])
        m.dma("qsy", q, out=sb3(kt[:], TT), in_=m.dram["kktm"][:].rearrange("(t p) n -> p t n", p=128))
        m.dma("qsy", q, out=sb3(vt[:], TT), in_=m.dram["vtm0"][:].rearrange("(t p) n -> p t n", p=128))
    blk.sync(pre)

    def v1(v):
        m.w(v, "qsy")
        m.i(v.tensor_max(ma[:], ma[:], mb[:]), "dve")
        m.w(v, "dve")
        m.i(v.tensor_scalar_mul(mb[:], ma[:], -1.0), "dve")
        m.w(v, "dve")
        m.i(v.tensor_add(za[:], za[:], zb[:]), "dve")
        m.w(v, "dve")
        m.i(v.reciprocal(cv[:], za[:]), "dve")
        m.w(v, "dve")
        m.i(v.tensor_scalar_mul(cv[:], cv[:], float(S)), "dve")
    blk.vector(v1)

    def s1(s):
        m.w(s, "dve")
        m.i(s.activation(ed[:], lg[:], AF.Exp, bias=mb[:]), "act")
    blk.scalar(s1)

    def v2(v):
        m.w(v, "act")
        m.i(v.tensor_single_scalar(ebf[:], ed[:], cv[:], ALU.mult), "dve")
    blk.vector(v2)

    for tt in range(TT):
        def t1(t, tt=tt):
            m.w(t, "dve")
            _gate(m, t, f"ua_pE{tt % 2}")
            m.i(t.transpose(pE[tt % 2][:, 0:NH0], ebf[:, tt * 128:(tt + 1) * 128],
                            m.sbc["ident"][0:NH0, 0:NH0]), "pe")
        blk.tensor(t1)

        def v3(v, tt=tt):
            m.w(v, "pe")
            m.i(v.tensor_copy(etm[:, tt * NH0:(tt + 1) * NH0], pE[tt % 2][:, 0:NH0]), "dve")
            _mark(m, f"ua_pE{tt % 2}", "dve")
        blk.vector(v3)

    for n in range(NH0):
        pe_base = m.counts["pe"]
        dtoks = []

        def vb(v, n=n, pe_base=pe_base):
            kv = n // GR
            for tc in range(TT):
                if pe_base + tc - 1 > 0:
                    m.w(v, "pe", pe_base + tc - 1)
                m.w(v, "dve")
                dtoks.append(m.i(v.tensor_single_scalar(
                    skb[tc % 2][:], kt[:, tc * KVD + 64 * kv: tc * KVD + 64 * kv + 64],
                    etm[:, tc * NH0 + n: tc * NH0 + n + 1], ALU.mult), "dve"))
        blk.vector(vb)

        def tb(t, n=n):
            kv = n // GR
            for tc in range(TT):
                m.w(t, "dve", dtoks[tc])
                m.i(t.matmul(pU[:, n * 64:(n + 1) * 64], skb[tc % 2][:],
                             vt[:, tc * KVD + 64 * kv: tc * KVD + 64 * kv + 64],
                             start=(tc == 0), stop=(tc == TT - 1)), "pe")
        blk.tensor(tb)

    def v4(v):
        m.w(v, "pe")
        m.i(v.tensor_copy(usb[:], pU[:]), "dve")
    blk.vector(v4)

    def st(q):
        m.w(q, "dve")
        m.dma("qsy", q, out=m.dram["up"][:], in_=usb[:])
    blk.sync(st)


PHASES["uacc"] = dict(statics=[], reads=["c2out", "c3aout", "lhm", "kktm", "vtm0"],
                      writes=["up"], name="uacc", emit=emit_uacc)



def emit_ffn0a(m, blk, ps):
    nc, c = m.nc, m.c
    D, DC, T, NG, NTG, IC = c["D"], c["DC"], c["T"], c["NG"], c["NTG"], c["IC"]
    xn = ps.enter_context(nc.sbuf_tensor("fa_xn", [128, DC * T], BF16))
    sgs = ps.enter_context(nc.sbuf_tensor("fa_sgs", [128, IC], F32))
    sus = ps.enter_context(nc.sbuf_tensor("fa_sus", [128, IC], F32))
    wgj = [ps.enter_context(nc.sbuf_tensor(f"fa_wg{i}", [128, DC * 128], BF16)) for i in range(2)]
    wuj = [ps.enter_context(nc.sbuf_tensor(f"fa_wu{i}", [128, DC * 128], BF16)) for i in range(2)]
    ga = ps.enter_context(nc.sbuf_tensor("fa_ga", [128, T], F32))
    fu = ps.enter_context(nc.sbuf_tensor("fa_fu", [128, T], BF16))
    pG = ps.enter_context(nc.psum_tensor("fa_pG", [128, T], F32))
    pU = ps.enter_context(nc.psum_tensor("fa_pU", [128, T], F32))

    def pre(q):
        m.dma("qsy", q, out=sb3(xn[:], DC), in_=fm_ap(m.dram["xnfm0"][:]))
        m.dma("qsy", q, out=sgs[:], in_=m.dram["sgs"][:])
        m.dma("qsy", q, out=sus[:], in_=m.dram["sus"][:])
    blk.sync(pre)

    for j in range(IC):
        i = j % 2

        def ld(g, j=j, i=i):
            _gate(m, g, f"fa_wg{i}")
            m.dma("qgp", g, out=sb3(wgj[i][:], DC), in_=fm_ap(m.dram["wgt"][j]),
                  fence=False, stream=i)
            t = m.dma("qgp", g, out=sb3(wuj[i][:], DC), in_=fm_ap(m.dram["wut"][j]),
                      fence=False, stream=i)
            _markt(m, f"fa_w{i}_rdy", t)
        blk.gpsimd(ld)

        def tb(t, j=j, i=i):
            if j == 0:
                m.w(t, "qsy")
            _gate(m, t, f"fa_w{i}_rdy")
            _gate(m, t, "fa_pG")
            for tg in range(NTG):
                for cc in range(DC):
                    t.matmul(pG[:, tg * NG:(tg + 1) * NG],
                             wgj[i][:, cc * 128:(cc + 1) * 128],
                             xn[:, cc * T + tg * NG: cc * T + (tg + 1) * NG],
                             start=(cc == 0), stop=(cc == DC - 1))
                    inst = t.matmul(pU[:, tg * NG:(tg + 1) * NG],
                                    wuj[i][:, cc * 128:(cc + 1) * 128],
                                    xn[:, cc * T + tg * NG: cc * T + (tg + 1) * NG],
                                    start=(cc == 0), stop=(cc == DC - 1))
            m.i(inst, "pe")
            _mark(m, f"fa_wg{i}", "pe")
        blk.tensor(tb)

        tokM0 = ("pe", m.counts["pe"])

        def sb_(s, j=j, tokM0=None):
            m.wt(s, tokM0 or ("pe", m.counts["pe"]))
            _gate(m, s, "fa_ga")
            m.i(s.activation(ga[:], pG[:], AF.Silu, scale=sgs[:, j:j + 1]), "act")
        blk.scalar(sb_)

        def vb(v, j=j):
            m.w(v, "act")
            _gate(m, v, "fa_fu")
            m.i(v.scalar_tensor_tensor(fu[:], pU[:], sus[:, j:j + 1], ga[:],
                                       ALU.mult, ALU.mult), "dve")
            _mark(m, "fa_pG", "dve")
            _mark(m, "fa_ga", "dve")
        blk.vector(vb)

        def st(q, j=j):
            m.w(q, "dve")
            t = m.dma("qsy", q, out=m.dram["fuse"][j * 128:(j + 1) * 128, :], in_=fu[:],
                      fence=False, stream=j % 2)
            _markt(m, "fa_fu", t)
        blk.sync(st)


PHASES["ffn0a"] = dict(statics=["wgt", "wut", "sgs", "sus"], reads=["xnfm0"],
                       writes=["fuse"], name="ffn0a", emit=emit_ffn0a)


def emit_ffn0b(m, blk, ps):
    nc, c = m.nc, m.c
    D, DC, T, NG, NTG, IC = c["D"], c["DC"], c["T"], c["NG"], c["NTG"], c["IC"]
    NSUB = NG // 128
    fus = ps.enter_context(nc.sbuf_tensor("fb_fu", [128, IC * NG], BF16))
    sds = ps.enter_context(nc.sbuf_tensor("fb_sds", [128, DC], F32))
    wdj = [ps.enter_context(nc.sbuf_tensor(f"fb_wd{i}", [128, IC * 128], BF16)) for i in range(2)]
    g0f = ps.enter_context(nc.sbuf_tensor("fb_g0f", [128, NG], F32))
    g0s = ps.enter_context(nc.sbuf_tensor("fb_g0s", [128, NSUB * D], F32))
    pC = [ps.enter_context(nc.psum_tensor(f"fb_pC{i}", [128, NG], F32)) for i in range(2)]
    pT = [ps.enter_context(nc.psum_tensor(f"fb_pT{i}", [128, 128], F32)) for i in range(2)]

    def pre(q):
        m.dma("qsy", q, out=sds[:], in_=m.dram["sds"][:])
    blk.sync(pre)

    for tgrp in range(NTG):
        def lf(q, tgrp=tgrp):
            _gate(m, q, "fb_fu")
            t = m.dma("qsy", q, out=sb3(fus[:], IC),
                      in_=fm_ap(m.dram["fuse"][:, tgrp * NG:(tgrp + 1) * NG]),
                      fence=False, stream=3)
            _markt(m, "fb_fu_rdy", t)
        blk.sync(lf)

        for j in range(DC):
            i = j % 2

            def ld(g, j=j, i=i):
                _gate(m, g, f"fb_wd{i}")
                t = m.dma("qgp", g, out=sb3(wdj[i][:], IC), in_=fm_ap(m.dram["wdt"][j]),
                          fence=False, stream=4 + i)
                _markt(m, f"fb_wd{i}_rdy", t)
            blk.gpsimd(ld)

            def tb(t, j=j, i=i, tgrp=tgrp):
                _gate(m, t, f"fb_wd{i}_rdy")
                _gate(m, t, "fb_fu_rdy")
                _gate(m, t, f"fb_pC{i}")
                for cc in range(IC):
                    inst = t.matmul(pC[i][:], wdj[i][:, cc * 128:(cc + 1) * 128],
                                    fus[:, cc * NG:(cc + 1) * NG],
                                    start=(cc == 0), stop=(cc == IC - 1))
                m.i(inst, "pe")
                _mark(m, f"fb_wd{i}", "pe")
                if j == DC - 1:
                    _mark(m, "fb_fu", "pe")
            blk.tensor(tb)

            tokC = ("pe", m.counts["pe"])

            def sb_(s, j=j, i=i, tokC=tokC):
                m.wt(s, tokC)
                _gate(m, s, "fb_g0f")
                m.i(s.activation(g0f[:], pC[i][:], AF.Silu, scale=sds[:, j:j + 1]), "act")
            blk.scalar(sb_)

            def vr(v, j=j, i=i):
                m.w(v, "act")
                m.i(v.engine_nop(), "dve")
                _mark(m, f"fb_pC{i}", "dve")
            blk.vector(vr)

            for ts in range(NSUB):
                def tt_(t, j=j, ts=ts):
                    m.w(t, "dve")
                    _gate(m, t, f"fb_pT{ts % 2}")
                    m.i(t.transpose(pT[ts % 2][:], g0f[:, ts * 128:(ts + 1) * 128],
                                    m.sbc["identf"][:]), "pe")
                blk.tensor(tt_)

                def ve(v, j=j, ts=ts):
                    m.w(v, "pe")
                    if j == 0 and ts == 0:
                        _gate(m, v, "fb_g0s")
                    m.i(v.tensor_copy(g0s[:, ts * D + j * 128: ts * D + (j + 1) * 128],
                                      pT[ts % 2][:]), "dve")
                    _mark(m, f"fb_pT{ts % 2}", "dve")
                    if ts == NSUB - 1:
                        _mark(m, "fb_g0f", "dve")
                blk.vector(ve)
        def st(q, tgrp=tgrp):
            m.w(q, "dve")
            for ts in range(NSUB):
                t = m.dma("qsy", q, out=m.dram["g0tm"][tgrp * NG + ts * 128: tgrp * NG + (ts + 1) * 128, :],
                          in_=g0s[:, ts * D:(ts + 1) * D], fence=False, stream=6)
            _markt(m, "fb_g0s", t)
        blk.sync(st)


PHASES["ffn0b"] = dict(statics=["wdt", "sds"], reads=["fuse"], writes=["g0tm"],
                       name="ffn0b", emit=emit_ffn0b)



def emit_ctx0(m, blk, ps):
    nc, c = m.nc, m.c
    D, DC, T, NG, NTG, NH0 = c["D"], c["DC"], c["T"], c["NG"], c["NTG"], c["NH0"]
    ua = ps.enter_context(nc.sbuf_tensor("cx_ua", [64, NH0 * 64], F32))
    ub = ps.enter_context(nc.sbuf_tensor("cx_ub", [64, NH0 * 64], F32))
    bd = ps.enter_context(nc.sbuf_tensor("cx_bd", [128, DC * 128], BF16))
    xn = ps.enter_context(nc.sbuf_tensor("cx_xn", [128, DC * T], BF16))
    wp = ps.enter_context(nc.sbuf_tensor("cx_wp", [128, DC * D], BF16))
    qkg = ps.enter_context(nc.sbuf_tensor("cx_qkg", [128, DC * NG], BF16))
    xph = ps.enter_context(nc.sbuf_tensor("cx_xph", [128, NG], F32))
    ctxp = ps.enter_context(nc.sbuf_tensor("cx_ctxp", [128, NG], BF16))
    pA = [ps.enter_context(nc.psum_tensor(f"cx_pA{i}", [128, NG], F32)) for i in range(2)]
    pB = [ps.enter_context(nc.psum_tensor(f"cx_pB{i}", [128, NG], F32)) for i in range(2)]

    def pre(q):
        m.dma("qsy", q, out=ua[:], in_=m.dram["c3bout"][0:64, :])
        m.dma("qsy", q, out=ub[:], in_=m.dram["c3bout"][64:128, :])
        m.dma("qsy", q, out=sb3(xn[:], DC), in_=fm_ap(m.dram["xnfm0"][:]))
    blk.sync(pre)

    def wl(g):
        m.dma("qgp", g, out=sb3(wp[:], DC), in_=fm_ap(m.dram["wphi0"][:]))
    blk.gpsimd(wl)

    def v1(v):
        m.w(v, "qsy")
        m.i(v.tensor_add(ua[:], ua[:], ub[:]), "dve")
        m.w(v, "dve")
        m.i(v.memset(bd[:], 0.0), "dve")
        for cb in range(DC):
            m.w(v, "dve")
            m.i(v.tensor_copy(bd[0:64, cb * 128: cb * 128 + 64],
                              ua[:, (2 * cb) * 64:(2 * cb + 1) * 64]), "dve")
            m.w(v, "dve")
            m.i(v.tensor_copy(bd[64:128, cb * 128 + 64: cb * 128 + 128],
                              ua[:, (2 * cb + 1) * 64:(2 * cb + 2) * 64]), "dve")
    blk.vector(v1)

    for tgrp in range(NTG):
        def lq(q, tgrp=tgrp):
            m.w(q, "dve")
            _gate(m, q, "cx_qkg")
            m.dma("qsy", q, out=sb3(qkg[:], DC),
                  in_=fm_ap(m.dram["qkfm"][:, tgrp * NG:(tgrp + 1) * NG]))
        blk.sync(lq)

        for co in range(DC):
            i = co % 2

            def tb(t, tgrp=tgrp, co=co, i=i):
                m.w(t, "qsy"); m.w(t, "qgp")
                _gate(m, t, f"cx_pA{i}")
                for cc in range(DC):
                    inst = t.matmul(pA[i][:], wp[:, cc * D + co * 128: cc * D + co * 128 + 128],
                                    xn[:, cc * T + tgrp * NG: cc * T + (tgrp + 1) * NG],
                                    start=(cc == 0), stop=(cc == DC - 1))
                m.i(inst, "pe")
                _gate(m, t, f"cx_pB{i}")
                m.i(t.matmul(pB[i][:], bd[:, co * 128:(co + 1) * 128],
                             qkg[:, co * NG:(co + 1) * NG], start=True, stop=True), "pe")
                if co == DC - 1:
                    _mark(m, "cx_qkg", "pe")
            blk.tensor(tb)

            def sb_(s, co=co, i=i):
                m.w(s, "pe")
                _gate(m, s, "cx_xph")
                m.i(s.copy(xph[:], pA[i][:]), "act")
            blk.scalar(sb_)

            def vb(v, co=co, i=i):
                m.w(v, "act")
                _gate(m, v, "cx_ctxp")
                m.i(v.tensor_tensor(ctxp[:], pB[i][:], xph[:], ALU.mult), "dve")
                _mark(m, f"cx_pA{i}", "dve")
                _mark(m, f"cx_pB{i}", "dve")
                _mark(m, "cx_xph", "dve")
            blk.vector(vb)

            def st(q, tgrp=tgrp, co=co):
                m.w(q, "dve")
                m.dma("qsy", q, out=m.dram["ctx"][co * 128:(co + 1) * 128,
                                                  tgrp * NG:(tgrp + 1) * NG], in_=ctxp[:])
                _mark(m, "cx_ctxp", "qsy")
            blk.sync(st)


PHASES["ctx0"] = dict(statics=["wphi0"], reads=["c3bout", "xnfm0", "qkfm"],
                      writes=["ctx"], name="ctx0", emit=emit_ctx0)


def emit_out0(m, blk, ps):
    nc, c = m.nc, m.c
    D, DC, T, NG, TT = c["D"], c["DC"], c["T"], c["NG"], c["TT"]
    wo = ps.enter_context(nc.sbuf_tensor("o0_wo", [128, DC * D], BF16))
    ctxt = ps.enter_context(nc.sbuf_tensor("o0_ctxt", [128, DC * 128], BF16))
    xt = ps.enter_context(nc.sbuf_tensor("o0_xt", [128, D], F32))
    g0t = ps.enter_context(nc.sbuf_tensor("o0_g0t", [128, D], F32))
    t1 = ps.enter_context(nc.sbuf_tensor("o0_t1", [128, D], F32))
    x1t = ps.enter_context(nc.sbuf_tensor("o0_x1t", [128, D], F32))
    pD = ps.enter_context(nc.psum_tensor("o0_pD", [128, min(2048, D)], F32))

    def wl(g):
        m.dma("qgp", g, out=sb3(wo[:], DC), in_=fm_ap(m.dram["wo0"][:]))
    blk.gpsimd(wl)

    for tt in range(TT):
        def lt(q, tt=tt):
            _gate(m, q, "o0_ctxt")
            m.dma("qsy", q, out=sb3(ctxt[:], DC),
                  in_=fm_ap(m.dram["ctx"][:, tt * 128:(tt + 1) * 128]))
            _gate(m, q, "o0_xt")
            m.dma("qsy", q, out=xt[:], in_=m.dram["x_in"][tt * 128:(tt + 1) * 128, :])
            _gate(m, q, "o0_g0t")
            m.dma("qsy", q, out=g0t[:], in_=m.dram["g0tm"][tt * 128:(tt + 1) * 128, :])
        blk.sync(lt)

        def tb(t, tt=tt):
            m.w(t, "qsy"); m.w(t, "qgp")
            _gate(m, t, "o0_pD")
            for dg in range(D // NG):
                for cc in range(DC):
                    inst = t.matmul(pD[:, dg * NG:(dg + 1) * NG],
                                    ctxt[:, cc * 128:(cc + 1) * 128],
                                    wo[:, cc * D + dg * NG: cc * D + (dg + 1) * NG],
                                    start=(cc == 0), stop=(cc == DC - 1))
            m.i(inst, "pe")
            _mark(m, "o0_ctxt", "pe")
        blk.tensor(tb)

        def vb(v, tt=tt):
            m.w(v, "pe")
            _gate(m, v, "o0_t1")
            m.i(v.tensor_tensor(t1[:], pD[:], g0t[:], ALU.mult), "dve")
            _mark(m, "o0_pD", "dve")
            _mark(m, "o0_g0t", "dve")
            m.w(v, "dve")
            _gate(m, v, "o0_x1t")
            m.i(v.tensor_add(x1t[:], xt[:], t1[:]), "dve")
            _mark(m, "o0_t1", "dve")
            _mark(m, "o0_xt", "dve")
        blk.vector(vb)

        def st(q, tt=tt):
            m.w(q, "dve")
            m.dma("qsy", q, out=m.dram["x1"][tt * 128:(tt + 1) * 128, :], in_=x1t[:])
            _mark(m, "o0_x1t", "qsy")
        blk.sync(st)


PHASES["out0"] = dict(statics=["wo0", "x_in"], reads=["ctx", "g0tm"],
                      writes=["x1"], name="out0", emit=emit_out0)



def emit_tern(m, blk, ps):
    nc, c = m.nc, m.c
    D, I, DC, IC, JG, DG = c["D"], c["INTER"], c["DC"], c["IC"], c["JG"], c["DG"]
    W = max(D, I)
    slab = [ps.enter_context(nc.sbuf_tensor(f"tn_sl{i}", [128, W], F32)) for i in range(2)]
    t8 = ps.enter_context(nc.sbuf_tensor("tn_t8", [128, W], I8))
    t8b = ps.enter_context(nc.sbuf_tensor("tn_t8b", [128, W], I8))
    btmp = ps.enter_context(nc.sbuf_tensor("tn_bt", [128, W], F32))
    btmpb = ps.enter_context(nc.sbuf_tensor("tn_btb", [128, W], F32))
    acc = ps.enter_context(nc.sbuf_tensor("tn_acc", [128, max(DC, IC)], F32))
    accs = ps.enter_context(nc.sbuf_tensor("tn_accs", [128, 1], F32))
    sm = ps.enter_context(nc.sbuf_tensor("tn_sm", [1, 8], F32))
    tauv = ps.enter_context(nc.sbuf_tensor("tn_tauv", [128, 1], F32))
    ntv = ps.enter_context(nc.sbuf_tensor("tn_ntv", [128, 1], F32))
    asb = ps.enter_context(nc.sbuf_tensor("tn_asb", [1, 8], F32))
    p1 = ps.enter_context(nc.psum_tensor("tn_p1", [128, 8], F32))

    def zero_asb(v):
        m.i(v.memset(asb[:], 0.0), "dve")
    blk.vector(zero_asb)

    for mi_, (wname, tname, aslot, rows, cols, ntile) in enumerate([
            ("w1T", "t1i", 0, D, I, JG), ("w2T", "t2i", 1, D, I, JG),
            ("w3T", "t3i", 2, I, D, DG)]):
        nsl = rows // 128
        numel = rows * cols

        if "pool" in m.counts:
            def wsync(v):
                m.w(v, "pool")
            blk.vector(wsync)
        for sl in range(nsl):
            i = sl % 2

            def ld(q, sl=sl, i=i, wname=wname, cols=cols):
                _gate(m, q, f"tn_sl{i}")
                t = m.dma("qsy", q, out=slab[i][:, 0:cols],
                          in_=m.dram[wname][sl * 128:(sl + 1) * 128, :],
                          fence=False, stream=i)
                _markt(m, f"tn_sl{i}_rdy", t)
            blk.sync(ld)

            def vr(v, sl=sl, i=i, cols=cols):
                _gate(m, v, f"tn_sl{i}_rdy")
                m.i(v.reduce_sum(acc[:, sl:sl + 1], slab[i][:, 0:cols], axis=AX.X,
                                 apply_absolute_value=True), "dve")
                _mark(m, f"tn_sl{i}", "dve")
            blk.vector(vr)

        def vtot(v, nsl=nsl):
            m.w(v, "dve")
            if "pool" in m.counts:
                m.w(v, "pool")
            m.i(v.reduce_sum(accs[:], acc[:, 0:nsl], axis=AX.X), "dve")
        blk.vector(vtot)

        def ttot(t):
            m.w(t, "dve")
            _gate(m, t, "tn_p1")
            m.i(t.matmul(p1[0:1, 0:1], accs[:], m.sbc["onesf_col"][:],
                         start=True, stop=True), "pe")
        blk.tensor(ttot)

        def vsc(v, numel=numel, aslot=aslot):
            m.w(v, "pe")
            m.i(v.tensor_scalar_mul(sm[0:1, 0:1], p1[0:1, 0:1], 1.0 / numel), "dve")
            _mark(m, "tn_p1", "dve")
            m.w(v, "dve")
            m.i(v.tensor_scalar(sm[0:1, 1:2], sm[0:1, 0:1], T_EPS, ATANH_HALF,
                                ALU.add, ALU.mult), "dve")
        blk.vector(vsc)

        def sa(s):
            m.w(s, "dve")
            m.i(s.activation(sm[0:1, 2:3], sm[0:1, 0:1], AF.Ln,
                             bias=m.sbc["cvals"][0:1, 1:2]), "act")
            m.w(s, "act")
            m.i(s.activation(sm[0:1, 3:4], sm[0:1, 0:1], AF.Ln,
                             bias=m.sbc["cvals"][0:1, 1:2], scale=-1.0), "act")
        blk.scalar(sa)

        def va(v, aslot=aslot):
            m.w(v, "act")
            m.i(v.tensor_sub(sm[0:1, 4:5], sm[0:1, 2:3], sm[0:1, 3:4]), "dve")
            m.w(v, "dve")
            m.i(v.tensor_scalar_mul(asb[0:1, aslot:aslot + 1], sm[0:1, 4:5], 0.5), "dve")
        blk.vector(va)

        def tbc(t):
            m.w(t, "dve")
            _gate(m, t, "tn_p1")
            m.i(t.matmul(p1[:, 0:1], m.sbc["onesf_row"][:], sm[0:1, 1:2],
                         start=True, stop=True), "pe")
        blk.tensor(tbc)

        def vbc(v):
            m.w(v, "pe")
            m.i(v.tensor_copy(tauv[:], p1[:, 0:1]), "dve")
            _mark(m, "tn_p1", "dve")
            m.w(v, "dve")
            m.i(v.tensor_scalar_mul(ntv[:], tauv[:], -1.0), "dve")
        blk.vector(vbc)

        for sl in range(nsl):
            i = sl % 2

            def ld2(q, sl=sl, i=i, wname=wname, cols=cols):
                _gate(m, q, f"tn_sl{i}")
                t = m.dma("qsy", q, out=slab[i][:, 0:cols],
                          in_=m.dram[wname][sl * 128:(sl + 1) * 128, :],
                          fence=False, stream=i)
                _markt(m, f"tn_sl{i}_rdy", t)
            blk.sync(ld2)

            eng_s = "dve"
            t82 = t8 if sl % 2 == 0 else t8b

            def vq(v, sl=sl, i=i, cols=cols, t82=t82):
                _gate(m, v, f"tn_sl{i}_rdy"); m.w(v, "dve")
                m.i(v.tensor_single_scalar(btmp[:, 0:cols], slab[i][:, 0:cols], ntv[:],
                                           ALU.is_le), "dve")
                m.w(v, "dve")
                _gate(m, v, f"tn_t8{sl % 2}")
                m.i(v.scalar_tensor_tensor(t82[:, 0:cols], slab[i][:, 0:cols], tauv[:],
                                           btmp[:, 0:cols], ALU.is_ge, ALU.subtract), "dve")
                _mark(m, f"tn_sl{i}", "dve")
            blk.vector(vq)

            def st(q, sl=sl, tname=tname, ntile=ntile, cols=cols, eng_s=eng_s, t82=t82):
                m.w(q, eng_s)
                t = m.dma("qsy", q, out=m.dram[tname][:, sl * 128:(sl + 1) * 128, :]
                          .rearrange("j p n -> p j n"),
                          in_=sb3(t82[:, 0:cols], ntile), fence=False, stream=2 + sl % 2)
                _markt(m, f"tn_t8{sl % 2}", t)
            blk.sync(st)

    def sta(q):
        m.w(q, "dve")
        m.dma("qsy", q, out=m.dram["terns"][:], in_=asb[:])
    blk.sync(sta)


PHASES["tern"] = dict(statics=["w1T", "w2T", "w3T"], reads=[],
                      writes=["t1i", "t2i", "t3i", "terns"], name="tern", emit=emit_tern)



def emit_quant1(m, blk, ps):
    nc, c = m.nc, m.c
    D, DC, T, TT = c["D"], c["DC"], c["T"], c["TT"]
    x1t = ps.enter_context(nc.sbuf_tensor("q1_x1t", [128, D], F32))
    xn1 = ps.enter_context(nc.sbuf_tensor("q1_xn1", [128, D], F32))
    sq = ps.enter_context(nc.sbuf_tensor("q1_sq", [128, D], F32))
    xr = ps.enter_context(nc.sbuf_tensor("q1_xr", [128, D], F32))
    q8 = ps.enter_context(nc.sbuf_tensor("q1_q8", [128, D], I8))
    qb = ps.enter_context(nc.sbuf_tensor("q1_qb", [128, D], BF16))
    stg = ps.enter_context(nc.sbuf_tensor("q1_stg", [128, DC * 128], BF16))
    stats = ps.enter_context(nc.sbuf_tensor("q1_stats", [128, 2 * TT], F32))
    k12 = ps.enter_context(nc.sbuf_tensor("q1_k12", [128, 2 * TT], F32))
    sm = ps.enter_context(nc.sbuf_tensor("q1_sm", [128, 8], F32))
    grow = ps.enter_context(nc.sbuf_tensor("q1_grow", [1, D], F32))
    g1rep = ps.enter_context(nc.sbuf_tensor("q1_g1rep", [128, D], F32))
    g2rep = ps.enter_context(nc.sbuf_tensor("q1_g2rep", [128, D], F32))
    arow = ps.enter_context(nc.sbuf_tensor("q1_arow", [1, 8], F32))
    avec = ps.enter_context(nc.sbuf_tensor("q1_avec", [128, 2], F32))
    pb = ps.enter_context(nc.psum_tensor("q1_pb", [128, 512], F32))
    pt = [ps.enter_context(nc.psum_tensor(f"q1_pt{i}", [128, 128], BF16)) for i in range(2)]

    for gi, (gname, dst) in enumerate([("g1v", g1rep), ("g2v", g2rep)]):
        def lg(q, gname=gname):
            _gate(m, q, "q1_grow")
            m.dma("qsy", q, out=grow[0:1, :], in_=m.dram[gname][:].rearrange("(a d) -> a d", a=1))
        blk.sync(lg)
        for gg in range(D // 512):
            def tb(t, gg=gg):
                m.w(t, "qsy")
                _gate(m, t, "q1_pb")
                m.i(t.matmul(pb[:], m.sbc["onesf_row"][:], grow[0:1, gg * 512:(gg + 1) * 512],
                             start=True, stop=True), "pe")
                _mark(m, "q1_grow", "pe")
            blk.tensor(tb)

            def vb(v, gg=gg, dst=dst):
                m.w(v, "pe")
                m.i(v.tensor_copy(dst[:, gg * 512:(gg + 1) * 512], pb[:]), "dve")
                _mark(m, "q1_pb", "dve")
            blk.vector(vb)

    def la(q):
        m.dma("qsy", q, out=arow[:], in_=m.dram["terns"][:])
        m.dma("qsy", q, out=stats[:], in_=m.dram["ln1st"][:])
    blk.sync(la)
    for gi in range(2):
        def tb2(t, gi=gi):
            m.w(t, "qsy")
            _gate(m, t, "q1_pb")
            m.i(t.matmul(pb[:, 0:1], m.sbc["onesf_row"][:], arow[0:1, gi:gi + 1],
                         start=True, stop=True), "pe")
        blk.tensor(tb2)

        def vb2(v, gi=gi):
            m.w(v, "pe")
            m.i(v.tensor_copy(avec[:, gi:gi + 1], pb[:, 0:1]), "dve")
            _mark(m, "q1_pb", "dve")
        blk.vector(vb2)

    for tt in range(TT):
        def lx(q, tt=tt):
            _gate(m, q, "q1_x1t")
            m.dma("qsy", q, out=x1t[:], in_=m.dram["x1"][tt * 128:(tt + 1) * 128, :])
        blk.sync(lx)

        def v0(v, tt=tt):
            m.w(v, "qsy")
            _gate(m, v, "q1_xn1")
            m.i(v.tensor_scalar(xn1[:], x1t[:], stats[:, 2 * tt:2 * tt + 1],
                                stats[:, 2 * tt + 1:2 * tt + 2], ALU.subtract, ALU.mult), "dve")
            _mark(m, "q1_x1t", "dve")
        blk.vector(v0)

        for G in (0, 1):
            grep = g1rep if G == 0 else g2rep
            dstfm = "xq1fm" if G == 0 else "xq2fm"

            def s0(s, tt=tt, G=G):
                m.w(s, "dve")
                _gate(m, s, "q1_sq")
                m.i(s.activation(sq[:], xn1[:], AF.Square, accum_out=sm[:, 0:1]), "act")
            blk.scalar(s0)

            def v1(v, tt=tt, G=G):
                m.w(v, "act")
                m.i(v.tensor_scalar(sm[:, 1:2], sm[:, 0:1], 1.0 / D, RMS_EPS,
                                    ALU.mult, ALU.add), "dve")
            blk.vector(v1)

            def s1(s, tt=tt, G=G):
                m.w(s, "dve")
                m.i(s.activation(sm[:, 2:3], sm[:, 1:2], AF.Sqrt), "act")
            blk.scalar(s1)

            def v2(v, tt=tt, G=G, grep=grep):
                m.w(v, "act")
                m.i(v.reciprocal(sm[:, 3:4], sm[:, 2:3]), "dve")
                m.w(v, "dve")
                _gate(m, v, "q1_xr")
                m.i(v.scalar_tensor_tensor(xr[:], xn1[:], sm[:, 3:4], grep[:],
                                           ALU.mult, ALU.mult), "dve")
                m.w(v, "dve")
                m.i(v.reduce_max(sm[:, 4:5], xr[:], axis=AX.X,
                                 apply_absolute_value=True), "dve")
                m.w(v, "dve")
                m.i(v.tensor_scalar_max(sm[:, 4:5], sm[:, 4:5], AQ_CLIP), "dve")
                m.w(v, "dve")
                m.i(v.reciprocal(sm[:, 5:6], sm[:, 4:5]), "dve")
                m.w(v, "dve")
                m.i(v.tensor_scalar_mul(sm[:, 5:6], sm[:, 5:6], 127.0), "dve")
                m.w(v, "dve")
                _gate(m, v, "q1_sq2")
                m.i(v.tensor_scalar(sq[:], xr[:], sm[:, 5:6], MAGIC, ALU.mult, ALU.add), "dve")
                _mark(m, "q1_xr", "dve")
                m.w(v, "dve")
                _gate(m, v, "q1_q8")
                m.i(v.tensor_scalar_sub(q8[:], sq[:], MAGIC), "dve")
                _mark(m, "q1_sq", "dve")
                _mark(m, "q1_sq2", "dve")
                m.w(v, "dve")
                m.i(v.reciprocal(sm[:, 6:7], sm[:, 5:6]), "dve")
                m.w(v, "dve")
                m.i(v.tensor_tensor(k12[:, G * TT + tt:G * TT + tt + 1], sm[:, 6:7],
                                    avec[:, G:G + 1], ALU.mult), "dve")
            blk.vector(v2)

            def s2(s, tt=tt, G=G):
                m.w(s, "dve")
                _gate(m, s, "q1_qb")
                m.i(s.copy(qb[:], q8[:]), "act")
                _mark(m, "q1_q8", "act")
            blk.scalar(s2)

            def vr(v):
                m.w(v, "act")
                m.i(v.engine_nop(), "dve")
            blk.vector(vr)

            for cc in range(DC):
                def tb3(t, cc=cc):
                    m.w(t, "dve")
                    _gate(m, t, f"q1_pt{cc % 2}")
                    m.i(t.transpose(pt[cc % 2][:], qb[:, cc * 128:(cc + 1) * 128],
                                    m.sbc["ident"][:]), "pe")
                blk.tensor(tb3)

                def ve(v, cc=cc):
                    m.w(v, "pe")
                    if cc == 0:
                        _gate(m, v, "q1_stg")
                    m.i(v.tensor_copy(stg[:, cc * 128:(cc + 1) * 128], pt[cc % 2][:]), "dve")
                    _mark(m, f"q1_pt{cc % 2}", "dve")
                    if cc == DC - 1:
                        _mark(m, "q1_qb", "dve")
                blk.vector(ve)

            def st(q, tt=tt, dstfm=dstfm, G=G):
                m.w(q, "dve")
                m.dma("qsy", q, out=fm_ap(m.dram[dstfm][:, tt * 128:(tt + 1) * 128]),
                      in_=sb3(stg[:], DC))
                _mark(m, "q1_stg", "qsy")
                if G == 1:
                    _mark(m, "q1_xn1", "qsy")
            blk.sync(st)

    def stk(q):
        m.w(q, "dve")
        m.dma("qsy", q, out=m.dram["k12"][:], in_=k12[:])
    blk.sync(stk)


PHASES["quant1"] = dict(statics=["g1v", "g2v"], reads=["x1", "ln1st", "terns"],
                        writes=["xq1fm", "xq2fm", "k12"], name="quant1", emit=emit_quant1)



def emit_uh1(m, blk, ps):
    nc, c = m.nc, m.c
    D, DC, T, TT, JG = c["D"], c["DC"], c["T"], c["TT"], c["JG"]
    xq1 = ps.enter_context(nc.sbuf_tensor("uh_xq1", [128, DC * T], BF16))
    xq2 = ps.enter_context(nc.sbuf_tensor("uh_xq2", [128, DC * T], BF16))
    k12 = ps.enter_context(nc.sbuf_tensor("uh_k12", [128, 2 * TT], F32))
    t1s = [ps.enter_context(nc.sbuf_tensor(f"uh_t1{i}", [128, DC * 512], BF16)) for i in range(2)]
    t2s = [ps.enter_context(nc.sbuf_tensor(f"uh_t2{i}", [128, DC * 512], BF16)) for i in range(2)]
    ga = [ps.enter_context(nc.sbuf_tensor("uh_ga0", [128, 512], F32))] * 2
    hb = [ps.enter_context(nc.sbuf_tensor("uh_hb0", [128, 512], F32))] * 2
    pU1 = [ps.enter_context(nc.psum_tensor(f"uh_p1{i}", [128, 512], F32)) for i in range(2)]
    pU2 = [ps.enter_context(nc.psum_tensor(f"uh_p2{i}", [128, 512], F32)) for i in range(2)]

    def pre(q):
        m.dma("qsy", q, out=sb3(xq1[:], DC), in_=fm_ap(m.dram["xq1fm"][:]))
        m.dma("qsy", q, out=sb3(xq2[:], DC), in_=fm_ap(m.dram["xq2fm"][:]))
        m.dma("qsy", q, out=k12[:], in_=m.dram["k12"][:])
    blk.sync(pre)
    pre_tok = ("qsy", m.counts["qsy"])

    for jg in range(JG):
        i = jg % 2

        def ld(g, jg=jg, i=i):
            _gate(m, g, f"uh_t1{i}")
            m.dma("qgp", g, out=sb3(t1s[i][:], DC),
                  in_=m.dram["t1i"][jg].rearrange("(c p) n -> p c n", p=128),
                  fence=False, stream=2 * i)
            t = m.dma("qgp", g, out=sb3(t2s[i][:], DC),
                      in_=m.dram["t2i"][jg].rearrange("(c p) n -> p c n", p=128),
                      fence=False, stream=2 * i)
            _markt(m, f"uh_t{i}_rdy", t)
        blk.gpsimd(ld)

        for tt in range(TT):
            ii = tt % 2

            def tb(t, jg=jg, i=i, tt=tt, ii=ii):
                if jg == 0 and tt == 0:
                    m.wt(t, pre_tok)
                _gate(m, t, f"uh_t{i}_rdy")
                _gate(m, t, f"uh_p1{ii}")
                _gate(m, t, f"uh_p2{ii}")
                for cc in range(DC):
                    t.matmul(pU1[ii][:], xq1[:, cc * T + tt * 128: cc * T + (tt + 1) * 128],
                             t1s[i][:, cc * 512:(cc + 1) * 512],
                             start=(cc == 0), stop=(cc == DC - 1))
                    inst = t.matmul(pU2[ii][:], xq2[:, cc * T + tt * 128: cc * T + (tt + 1) * 128],
                                    t2s[i][:, cc * 512:(cc + 1) * 512],
                                    start=(cc == 0), stop=(cc == DC - 1))
                m.i(inst, "pe")
                if tt == TT - 1:
                    _mark(m, f"uh_t1{i}", "pe")
            blk.tensor(tb)
            tokM = ("pe", m.counts["pe"])

            def sb_(s, jg=jg, tt=tt, ii=ii, tokM=tokM):
                m.wt(s, tokM)
                _gate(m, s, "uh_ga0")
                m.i(s.activation(ga[ii][:], pU1[ii][:], AF.Silu,
                                 scale=k12[:, tt:tt + 1]), "act")
                _mark(m, f"uh_p1{ii}", "act")
            blk.scalar(sb_)
            tokA = ("act", m.counts["act"])

            def vb(v, jg=jg, tt=tt, ii=ii, tokA=tokA):
                m.wt(v, tokA)
                _gate(m, v, "uh_hb0")
                m.i(v.scalar_tensor_tensor(hb[ii][:], pU2[ii][:], k12[:, TT + tt:TT + tt + 1],
                                           ga[ii][:], ALU.mult, ALU.mult), "dve")
                _mark(m, f"uh_p2{ii}", "dve")
                _mark(m, "uh_ga0", "dve")
            blk.vector(vb)
            tokV = ("dve", m.counts["dve"])

            def st(q, jg=jg, tt=tt, ii=ii, tokV=tokV):
                m.wt(q, tokV)
                t = m.dma("qsy", q, out=m.dram["hdr"][tt * 128:(tt + 1) * 128,
                                                      jg * 512:(jg + 1) * 512], in_=hb[ii][:],
                          fence=False, stream=ii)
                _markt(m, "uh_hb0", t)
            blk.sync(st)


PHASES["uh1"] = dict(statics=[], reads=["xq1fm", "xq2fm", "k12", "t1i", "t2i"],
                     writes=["hdr"], name="uh1", emit=emit_uh1)



def emit_hdown(m, blk, ps):
    nc, c = m.nc, m.c
    D, T, TT, I, IC, DG, GSZ = c["D"], c["T"], c["TT"], c["INTER"], c["IC"], c["DG"], c["GSZ"]
    GT = GSZ * 128
    ICH = IC // 2
    hbuf2 = [ps.enter_context(nc.sbuf_tensor(f"hd_h{i}", [128, I], F32)) for i in range(2)]
    q8 = ps.enter_context(nc.sbuf_tensor("hd_q8", [128, I], I8))
    qb = ps.enter_context(nc.sbuf_tensor("hd_qb", [128, I], BF16))
    hq = ps.enter_context(nc.sbuf_tensor("hd_hq", [128, IC * GT], BF16))
    t3h = ps.enter_context(nc.sbuf_tensor("hd_t3h", [128, ICH * 512], BF16))
    gout = ps.enter_context(nc.sbuf_tensor("hd_gout", [128, 512], F32))
    gsc = ps.enter_context(nc.sbuf_tensor("hd_gsc", [128, 512], F32))
    gdn = [ps.enter_context(nc.sbuf_tensor(f"hd_gdn{i}", [128, 512], F32)) for i in range(2)]
    sm = ps.enter_context(nc.sbuf_tensor("hd_sm", [128, 24], F32))
    k3 = ps.enter_context(nc.sbuf_tensor("hd_k3", [128, TT], F32))
    grow = ps.enter_context(nc.sbuf_tensor("hd_grow", [1, 512], F32))
    g3rep = ps.enter_context(nc.sbuf_tensor("hd_g3rep", [128, I], F32))
    arow = ps.enter_context(nc.sbuf_tensor("hd_arow", [1, 8], F32))
    avec = ps.enter_context(nc.sbuf_tensor("hd_avec", [128, 1], F32))
    pb = ps.enter_context(nc.psum_tensor("hd_pb", [128, 512], F32))
    pt = [ps.enter_context(nc.psum_tensor(f"hd_pt{i}", [128, 512], BF16)) for i in range(2)]
    pD = [ps.enter_context(nc.psum_tensor(f"hd_pD{i}", [128, 512], F32)) for i in range(2)]

    for gg in range(I // 512):
        def lg(q, gg=gg):
            _gate(m, q, "hd_grow")
            m.dma("qsy", q, out=grow[0:1, :],
                  in_=m.dram["g3v"][gg * 512:(gg + 1) * 512].rearrange("(a d) -> a d", a=1))
        blk.sync(lg)

        def tb(t, gg=gg):
            m.w(t, "qsy")
            _gate(m, t, "hd_pb")
            m.i(t.matmul(pb[:], m.sbc["onesf_row"][:], grow[0:1, :],
                         start=True, stop=True), "pe")
            _mark(m, "hd_grow", "pe")
        blk.tensor(tb)

        def vb(v, gg=gg):
            m.w(v, "pe")
            m.i(v.tensor_copy(g3rep[:, gg * 512:(gg + 1) * 512], pb[:]), "dve")
            _mark(m, "hd_pb", "dve")
        blk.vector(vb)

    def la(q):
        m.dma("qsy", q, out=arow[:], in_=m.dram["terns"][:])
    blk.sync(la)

    def tb2(t):
        m.w(t, "qsy")
        _gate(m, t, "hd_pb")
        m.i(t.matmul(pb[:, 0:1], m.sbc["onesf_row"][:], arow[0:1, 2:3],
                     start=True, stop=True), "pe")
    blk.tensor(tb2)

    def vb2(v):
        m.w(v, "pe")
        m.i(v.tensor_copy(avec[:], pb[:, 0:1]), "dve")
        _mark(m, "hd_pb", "dve")
    blk.vector(vb2)

    def emit_quant(grp):
        tokQ = {}
        for lt in range(GSZ):
            tt = grp * GSZ + lt
            hbuf = hbuf2[tt % 2]

            def lh(q, tt=tt, hbuf=hbuf):
                _gate(m, q, f"hd_h{tt % 2}")
                t = m.dma("qsy", q, out=hbuf[:], in_=m.dram["hdr"][tt * 128:(tt + 1) * 128, :],
                          fence=False, stream=tt % 2)
                _markt(m, f"hd_h{tt % 2}_rdy", t)
            blk.sync(lh)

            for ch in range(I // 512):
                def sA(s, ch=ch, tt=tt):
                    if ch == 0:
                        _gate(m, s, f"hd_h{tt % 2}_rdy")
                        _gate(m, s, "hd_smA")
                    m.i(s.activation(gsc[:], hbuf[:, ch * 512:(ch + 1) * 512], AF.Square,
                                     accum_out=sm[:, ch:ch + 1]), "act")
                blk.scalar(sA)

            for ch in range(I // 512):
                def vA(v, ch=ch, tt=tt):
                    if ch == 0:
                        _gate(m, v, f"hd_h{tt % 2}_rdy")
                    m.i(v.tensor_tensor(gout[:], hbuf[:, ch * 512:(ch + 1) * 512],
                                        g3rep[:, ch * 512:(ch + 1) * 512], ALU.mult), "dve")
                    m.w(v, "dve")
                    m.i(v.reduce_max(sm[:, 18:19], gout[:], axis=AX.X,
                                     apply_absolute_value=True), "dve")
                    m.w(v, "dve")
                    if ch == 0:
                        m.i(v.tensor_copy(sm[:, 17:18], sm[:, 18:19]), "dve")
                    else:
                        m.i(v.tensor_max(sm[:, 17:18], sm[:, 17:18], sm[:, 18:19]), "dve")
                blk.vector(vA)

            def v1(v, tt=tt):
                m.w(v, "act")
                m.i(v.reduce_sum(sm[:, 16:17], sm[:, 0:I // 512], axis=AX.X), "dve")
                m.w(v, "dve")
                m.i(v.tensor_scalar(sm[:, 19:20], sm[:, 16:17], 1.0 / I, RMS_EPS,
                                    ALU.mult, ALU.add), "dve")
                _mark(m, "hd_smA", "dve")
            blk.vector(v1)

            def s1(s, tt=tt):
                m.w(s, "dve")
                m.i(s.activation(sm[:, 20:21], sm[:, 19:20], AF.Sqrt), "act")
            blk.scalar(s1)

            def v2(v, tt=tt):
                m.w(v, "act")
                m.i(v.reciprocal(sm[:, 19:20], sm[:, 20:21]), "dve")
                m.w(v, "dve")
                m.i(v.tensor_tensor(sm[:, 21:22], sm[:, 19:20], sm[:, 17:18], ALU.mult), "dve")
                m.w(v, "dve")
                m.i(v.tensor_scalar_max(sm[:, 21:22], sm[:, 21:22], AQ_CLIP), "dve")
                m.w(v, "dve")
                m.i(v.reciprocal(sm[:, 22:23], sm[:, 21:22]), "dve")
                m.w(v, "dve")
                m.i(v.tensor_scalar_mul(sm[:, 22:23], sm[:, 22:23], 127.0), "dve")
                m.w(v, "dve")
                m.i(v.tensor_tensor(sm[:, 20:21], sm[:, 19:20], sm[:, 22:23], ALU.mult), "dve")
                m.w(v, "dve")
                m.i(v.reciprocal(sm[:, 21:22], sm[:, 22:23]), "dve")
                m.w(v, "dve")
                m.i(v.tensor_tensor(k3[:, tt:tt + 1], sm[:, 21:22], avec[:], ALU.mult), "dve")
            blk.vector(v2)

            for ch in range(I // 512):
                def vB(v, ch=ch, tt=tt, hbuf=hbuf):
                    if ch == 0:
                        _gate(m, v, "hd_q8")
                    m.i(v.scalar_tensor_tensor(gout[:], hbuf[:, ch * 512:(ch + 1) * 512],
                                               sm[:, 20:21], g3rep[:, ch * 512:(ch + 1) * 512],
                                               ALU.mult, ALU.mult), "dve")
                    m.w(v, "dve")
                    m.i(v.tensor_scalar(q8[:, ch * 512:(ch + 1) * 512], gout[:],
                                        MAGIC, MAGIC, ALU.add, ALU.subtract), "dve")
                    if ch == I // 512 - 1:
                        _mark(m, f"hd_h{tt % 2}", "dve")
                blk.vector(vB)

            if lt == 0:
                def s2(s):
                    m.w(s, "dve")
                    _gate(m, s, "hd_qb")
                    m.i(s.copy(qb[:], q8[:]), "act")
                    _mark(m, "hd_q8", "act")
                blk.scalar(s2)

                def vr(v):
                    m.w(v, "act")
                    m.i(v.engine_nop(), "dve")
                blk.vector(vr)
                tokQ[lt] = ("dve", m.counts["dve"])
        return tokQ

    def emit_transp(grp, tokQ):
        for lt in range(GSZ):
            if lt > 0:
                def s2b(s):
                    m.w(s, "dve")
                    _gate(m, s, "hd_qb")
                    m.i(s.copy(qb[:], q8[:]), "act")
                    _mark(m, "hd_q8", "act")
                blk.scalar(s2b)

                def vrb(v):
                    m.w(v, "act")
                    m.i(v.engine_nop(), "dve")
                blk.vector(vrb)
                tokQ[lt] = ("dve", m.counts["dve"])
            for icg in range(0, IC, 4):
                g2 = (icg // 4) % 2

                def tb3(t, icg=icg, g2=g2, lt=lt):
                    if icg == 0:
                        m.wt(t, tokQ[lt])
                    _gate(m, t, f"hd_pt{g2}")
                    for s in range(4):
                        m.i(t.transpose(pt[g2][:, s * 128:(s + 1) * 128],
                                        qb[:, (icg + s) * 128:(icg + s + 1) * 128],
                                        m.sbc["ident"][:]), "pe")
                blk.tensor(tb3)

                def ve(v, icg=icg, g2=g2, lt=lt):
                    m.w(v, "pe")
                    if lt == 0 and icg == 0:
                        _gate(m, v, "hd_hq")
                    for s in range(4):
                        ic = icg + s
                        m.i(v.tensor_copy(hq[:, ic * GT + lt * 128: ic * GT + (lt + 1) * 128],
                                          pt[g2][:, s * 128:(s + 1) * 128]), "dve")
                    _mark(m, f"hd_pt{g2}", "dve")
                    if icg + 4 == IC:
                        _mark(m, "hd_qb", "dve")
                blk.vector(ve)
        return ("dve", m.counts["dve"])

    def emit_down(grp, tokHQ):
        for dg in range(DG):
            for half in range(2):
                def ld(g, dg=dg, half=half):
                    _gate(m, g, "hd_t3h")
                    t = m.dma("qgp", g, out=sb3(t3h[:], ICH),
                              in_=m.dram["t3i"][dg, half * (I // 2):(half + 1) * (I // 2), :]
                              .rearrange("(c p) n -> p c n", p=128),
                              fence=False, stream=7)
                    _markt(m, "hd_t3h_rdy", t)
                blk.gpsimd(ld)

                for lt in range(GSZ):
                    tt = grp * GSZ + lt
                    ii = lt % 2

                    def tb4(t, dg=dg, half=half, lt=lt, ii=ii):
                        _gate(m, t, "hd_t3h_rdy"); m.wt(t, tokHQ)
                        if half == 0:
                            _gate(m, t, f"hd_pD{ii}")
                        for icl in range(ICH):
                            ic = half * ICH + icl
                            inst = t.matmul(pD[ii][:], hq[:, ic * GT + lt * 128: ic * GT + (lt + 1) * 128],
                                            t3h[:, icl * 512:(icl + 1) * 512],
                                            start=(ic == 0), stop=(ic == IC - 1))
                        m.i(inst, "pe")
                        if lt == GSZ - 1:
                            _mark(m, "hd_t3h", "pe")
                        if dg == DG - 1 and half == 1 and lt == GSZ - 1:
                            _mark(m, "hd_hq", "pe")
                    blk.tensor(tb4)

                    if half == 1:
                        gdni = gdn[ii][:]
                        tokD = ("pe", m.counts["pe"])

                        def sb_(s, dg=dg, tt=tt, ii=ii, gdni=gdni, tokD=tokD):
                            m.wt(s, tokD)
                            _gate(m, s, f"hd_gdn{ii}")
                            m.i(s.activation(gdni, pD[ii][:], AF.Silu,
                                             scale=k3[:, tt:tt + 1]), "act")
                        blk.scalar(sb_)

                        def vrd(v, ii=ii):
                            m.w(v, "act")
                            m.i(v.engine_nop(), "dve")
                            _mark(m, f"hd_pD{ii}", "dve")
                        blk.vector(vrd)
                        tokR = ("act", m.counts["act"])

                        def st(q, dg=dg, tt=tt, ii=ii, gdni=gdni, tokR=tokR):
                            m.wt(q, tokR)
                            t = m.dma("qsy", q, out=m.dram["g1tm"][tt * 128:(tt + 1) * 128,
                                                                   dg * 512:(dg + 1) * 512],
                                      in_=gdni, fence=False, stream=5 + ii)
                            _markt(m, f"hd_gdn{ii}", t)
                        blk.sync(st)

    NGRP = TT // GSZ
    tokHQ = None
    for grp in range(NGRP):
        tokQ = emit_quant(grp)
        if grp > 0:
            emit_down(grp - 1, tokHQ)
        tokHQ = emit_transp(grp, tokQ)
    emit_down(NGRP - 1, tokHQ)


PHASES["hdown"] = dict(statics=["g3v"], reads=["hdr", "t3i", "terns"],
                       writes=["g1tm"], name="hdown", emit=emit_hdown)



def emit_proj1(m, blk, ps):
    nc, c = m.nc, m.c
    D, DC, T, S, AD, ADC, NG, NTG = c["D"], c["DC"], c["T"], c["S"], c["AD"], c["ADC"], c["NG"], c["NTG"]
    wq = ps.enter_context(nc.sbuf_tensor("p1_wq", [128, DC * AD], BF16))
    wk = ps.enter_context(nc.sbuf_tensor("p1_wk", [128, DC * AD], BF16))
    wv = ps.enter_context(nc.sbuf_tensor("p1_wv", [128, DC * AD], BF16))
    xb = ps.enter_context(nc.sbuf_tensor("p1_xb", [128, DC * T], BF16))
    stg = ps.enter_context(nc.sbuf_tensor("p1_stg", [128, T], BF16))
    vst = ps.enter_context(nc.sbuf_tensor("p1_vst", [128, AD], BF16))
    pA = [ps.enter_context(nc.psum_tensor(f"p1_pA{i}", [128, NG], F32)) for i in range(2)]

    def wl(g):
        m.dma("qgp", g, out=sb3(wq[:], DC), in_=fm_ap(m.dram["wqTs"][:]))
        m.dma("qgp", g, out=sb3(wk[:], DC), in_=fm_ap(m.dram["wkTs"][:]))
        m.dma("qgp", g, out=sb3(wv[:], DC), in_=fm_ap(m.dram["wvTs"][:]))
    blk.gpsimd(wl)

    for h in range(2):
        def lx(q, h=h):
            _gate(m, q, "p1_xb")
            m.dma("qsy", q, out=sb3(xb[:], DC),
                  in_=fm_ap(m.dram["c4out"][h * D:(h + 1) * D, :]))
        blk.sync(lx)

        for wsb, dst in ((wq, "qfm1"), (wk, "kfm1")):
            for co in range(ADC):
                for tg in range(NTG):
                    i = (co * NTG + tg) % 2

                    def tb(t, h=h, wsb=wsb, co=co, tg=tg, i=i):
                        m.w(t, "qsy"); m.w(t, "qgp")
                        _gate(m, t, f"p1_pA{i}")
                        for cc in range(DC):
                            inst = t.matmul(pA[i][:], wsb[:, cc * AD + co * 128: cc * AD + co * 128 + 128],
                                            xb[:, cc * T + tg * NG: cc * T + (tg + 1) * NG],
                                            start=(cc == 0), stop=(cc == DC - 1))
                        m.i(inst, "pe")
                        if wsb is wk and co == ADC - 1 and tg == NTG - 1:
                            pass
                    blk.tensor(tb)

                    def vb(v, co=co, tg=tg, i=i):
                        m.w(v, "pe")
                        if tg == 0:
                            _gate(m, v, "p1_stg")
                        m.i(v.tensor_copy(stg[:, tg * NG:(tg + 1) * NG], pA[i][:]), "dve")
                        _mark(m, f"p1_pA{i}", "dve")
                    blk.vector(vb)

                def st(q, h=h, dst=dst, co=co):
                    m.w(q, "dve")
                    m.dma("qsy", q, out=m.dram[dst][co * 128:(co + 1) * 128,
                                                    h * T:(h + 1) * T], in_=stg[:])
                    _mark(m, "p1_stg", "qsy")
                blk.sync(st)

        for tt in range(c["TT"]):
            for vg in range(max(1, AD // 512)):
                ncols = min(512, AD)
                i = (tt + vg) % 2

                def tv(t, h=h, tt=tt, vg=vg, i=i, ncols=ncols):
                    m.w(t, "qsy"); m.w(t, "qgp")
                    _gate(m, t, f"p1_pA{i}")
                    for cc in range(DC):
                        inst = t.matmul(pA[i][:, 0:ncols],
                                        xb[:, cc * T + tt * 128: cc * T + (tt + 1) * 128],
                                        wv[:, cc * AD + vg * 512: cc * AD + vg * 512 + ncols],
                                        start=(cc == 0), stop=(cc == DC - 1))
                    m.i(inst, "pe")
                blk.tensor(tv)

                def vv(v, tt=tt, vg=vg, i=i, ncols=ncols):
                    m.w(v, "pe")
                    if vg == 0:
                        _gate(m, v, "p1_vst")
                    m.i(v.tensor_copy(vst[:, vg * 512:vg * 512 + ncols], pA[i][:, 0:ncols]), "dve")
                    _mark(m, f"p1_pA{i}", "dve")
                blk.vector(vv)

            def sv(q, h=h, tt=tt):
                m.w(q, "dve")
                m.dma("qsy", q, out=m.dram["vtm1"][h * T + tt * 128: h * T + (tt + 1) * 128, :],
                      in_=vst[:])
                _mark(m, "p1_vst", "qsy")
            blk.sync(sv)
        if h == 0:
            def gx(v):
                m.w(v, "pe")
                m.i(v.engine_nop(), "dve")
                _mark(m, "p1_xb", "dve")
            blk.vector(gx)


PHASES["proj1"] = dict(statics=["wqTs", "wkTs", "wvTs"], reads=["c4out"],
                       writes=["qfm1", "kfm1", "vtm1"], name="proj1", emit=emit_proj1)


def emit_flash(m, blk, ps):
    nc, c = m.nc, m.c
    S, SC, AD, ADC, HC, HD1 = c["S"], c["SC"], c["AD"], c["ADC"], c["HC"], c["HD1"]
    iscale = 1.0 / float(np.sqrt(HD1))
    kf = ps.enter_context(nc.sbuf_tensor("fl_kf", [128, ADC * S], BF16))
    qh = [ps.enter_context(nc.sbuf_tensor(f"fl_qh{i}", [128, S], BF16)) for i in range(2)]
    vh = [ps.enter_context(nc.sbuf_tensor(f"fl_vh{i}", [128, SC * 128], BF16)) for i in range(2)]
    ssb = ps.enter_context(nc.sbuf_tensor("fl_ssb", [128, 128], F32))
    psb = [ps.enter_context(nc.sbuf_tensor(f"fl_psb{i}", [128, 1024], BF16)) for i in range(2)]
    atb = [ps.enter_context(nc.sbuf_tensor(f"fl_atb{i}", [128, 128], BF16)) for i in range(2)]
    rec = [ps.enter_context(nc.sbuf_tensor(f"fl_rec{i}", [128, 1], F32)) for i in range(2)]
    afm = [ps.enter_context(nc.sbuf_tensor(f"fl_afm{i}", [128, S], BF16)) for i in range(2)]
    pS = [ps.enter_context(nc.psum_tensor(f"fl_pS{i}", [128, 1024], F32)) for i in range(2)]
    pA = [ps.enter_context(nc.psum_tensor(f"fl_pA{i}", [128, 128], BF16)) for i in range(2)]
    pO = [ps.enter_context(nc.psum_tensor(f"fl_pO{i}", [128, 132], F32)) for i in range(2)]

    def lk(q):
        m.dma("qsy", q, out=sb3(kf[:], ADC), in_=fm_ap(m.dram["kfm1"][:]))
    blk.sync(lk)

    for hd in range(HC):
        hb = hd % 2

        def lq(q, hd=hd, hb=hb):
            _gate(m, q, f"fl_qh{hb}")
            m.dma("qsy", q, out=qh[hb][:], in_=m.dram["qfm1"][hd * 128:(hd + 1) * 128, :],
                  fence=False, stream=hb)
            t2 = m.dma("qsy", q, out=sb3(vh[hb][:], SC),
                       in_=m.dram["vtm1"][:, hd * 128:(hd + 1) * 128]
                       .rearrange("(t p) n -> p t n", p=128), fence=False, stream=hb)
            _markt(m, f"fl_qv{hb}_rdy", t2)
        blk.sync(lq)

        for qt in range(SC):
            po = pO[qt % 2]
            ep_tok = m.free.get(f"fl_pO{qt % 2}", None)
            nsub = qt + 1
            ngrp = (nsub + 7) // 8
            for grp in range(ngrp):
                s0g = grp * 8
                s1g = min(nsub, s0g + 8)
                nsg = s1g - s0g
                i = grp % 2

                def ts_(t, hd=hd, hb=hb, qt=qt, s0g=s0g, s1g=s1g, i=i):
                    _gate(m, t, f"fl_qv{hb}_rdy")
                    _gate(m, t, f"fl_pS{i}")
                    for s in range(s0g, s1g):
                        m.i(t.matmul(pS[i][:, (s - s0g) * 128:(s - s0g + 1) * 128],
                                     kf[:, hd * S + s * 128: hd * S + (s + 1) * 128],
                                     qh[hb][:, qt * 128:(qt + 1) * 128],
                                     start=True, stop=True), "pe")
                blk.tensor(ts_)
                tokS = ("pe", m.counts["pe"])

                has_diag = (s1g == nsub)
                if has_diag:
                    def vm(v, i=i, nsg=nsg, tokS=tokS):
                        m.wt(v, tokS)
                        _gate(m, v, "fl_ssb")
                        m.i(v.tensor_add(ssb[:], pS[i][:, (nsg - 1) * 128:nsg * 128],
                                         m.sbc["maskt"][:]), "dve")
                    blk.vector(vm)
                    tokM = ("dve", m.counts["dve"])

                    def se(s, i=i, nsg=nsg, tokM=tokM, tokS=tokS):
                        m.wt(s, tokS)
                        _gate(m, s, f"fl_psb{i}")
                        for off in range(0, (nsg - 1) * 128, 512):
                            ch = min(512, (nsg - 1) * 128 - off)
                            m.i(s.activation(psb[i][:, off:off + ch],
                                             pS[i][:, off:off + ch], AF.Exp,
                                             scale=iscale), "act")
                        m.wt(s, tokM)
                        m.i(s.activation(psb[i][:, (nsg - 1) * 128:nsg * 128], ssb[:],
                                         AF.Exp, scale=iscale), "act")
                        _mark(m, "fl_ssb", "act")
                        _mark(m, f"fl_pS{i}", "act")
                    blk.scalar(se)
                else:
                    def se2(s, i=i, nsg=nsg, tokS=tokS):
                        m.wt(s, tokS)
                        _gate(m, s, f"fl_psb{i}")
                        for off in range(0, nsg * 128, 512):
                            ch = min(512, nsg * 128 - off)
                            m.i(s.activation(psb[i][:, off:off + ch],
                                             pS[i][:, off:off + ch],
                                             AF.Exp, scale=iscale), "act")
                        _mark(m, f"fl_pS{i}", "act")
                    blk.scalar(se2)
                tokE = ("act", m.counts["act"])

                def vr(v, tokE=tokE):
                    m.wt(v, tokE)
                    m.i(v.engine_nop(), "dve")
                blk.vector(vr)
                tokP = ("dve", m.counts["dve"])

                def tv(t, hd=hd, hb=hb, qt=qt, s0g=s0g, s1g=s1g, i=i, nsub=nsub,
                       tokP=tokP, ep_tok=ep_tok, po=po):
                    m.wt(t, tokP)
                    if s0g == 0 and ep_tok is not None:
                        m.wt(t, ep_tok)
                    for s in range(s0g, s1g):
                        first = (s == 0)
                        last = (s == nsub - 1)
                        sl = psb[i][:, (s - s0g) * 128:(s - s0g + 1) * 128]
                        t.matmul(po[:, 0:128], sl, vh[hb][:, s * 128:(s + 1) * 128],
                                 start=first, stop=False)
                        m.i(t.matmul(po[:, 128:129], sl, m.sbc["onesb"][:],
                                     start=False, stop=last), "pe")
                    _mark(m, f"fl_psb{i}", "pe")
                blk.tensor(tv)
            tok_pv = ("pe", m.counts["pe"])

            def vep(v, qt=qt, po=po, tok_pv=tok_pv):
                m.wt(v, tok_pv)
                _gate(m, v, f"fl_atb{qt % 2}")
                m.i(v.reciprocal(rec[qt % 2][:], po[:, 128:129]), "dve")
                m.w(v, "dve")
                m.i(v.tensor_single_scalar(atb[qt % 2][:], po[:, 0:128],
                                           rec[qt % 2][:], ALU.mult), "dve")
                _mark(m, f"fl_pO{qt % 2}", "dve")
            blk.vector(vep)
            tokEp = ("dve", m.counts["dve"])

            def tat(t, qt=qt, tokEp=tokEp):
                m.wt(t, tokEp)
                _gate(m, t, f"fl_pA{qt % 2}")
                m.i(t.transpose(pA[qt % 2][:], atb[qt % 2][:], m.sbc["ident"][:]), "pe")
                _mark(m, f"fl_atb{qt % 2}", "pe")
            blk.tensor(tat)
            tokA = ("pe", m.counts["pe"])

            def vae(v, hd=hd, qt=qt, hb=hb, tokA=tokA):
                m.wt(v, tokA)
                if qt == 0:
                    _gate(m, v, f"fl_afm{hb}")
                m.i(v.tensor_copy(afm[hb][:, qt * 128:(qt + 1) * 128], pA[qt % 2][:]), "dve")
                _mark(m, f"fl_pA{qt % 2}", "dve")
            blk.vector(vae)

        def sa(q, hd=hd, hb=hb):
            m.w(q, "dve")
            t = m.dma("qsy", q, out=m.dram["atfm"][hd * 128:(hd + 1) * 128, :], in_=afm[hb][:],
                      fence=False, stream=2 + hb)
            _markt(m, f"fl_afm{hb}", t)
            _markt(m, f"fl_qh{hb}", t)
        blk.sync(sa)


PHASES["flash"] = dict(statics=[], reads=["qfm1", "kfm1", "vtm1"],
                       writes=["atfm"], name="flash", emit=emit_flash)


def emit_s1proj(m, blk, ps):
    nc, c = m.nc, m.c
    D, S, SC, AD, ADC = c["D"], c["S"], c["SC"], c["AD"], c["ADC"]
    wo = ps.enter_context(nc.sbuf_tensor("s1_wo", [128, ADC * D], BF16))
    att = ps.enter_context(nc.sbuf_tensor("s1_at", [128, ADC * 128], BF16))
    ses = ps.enter_context(nc.sbuf_tensor("s1_ses", [128, D], F32))
    pE = ps.enter_context(nc.psum_tensor("s1_pE", [128, min(2048, D)], F32))

    def wl(g):
        m.dma("qgp", g, out=sb3(wo[:], ADC), in_=fm_ap(m.dram["woTs"][:]))
    blk.gpsimd(wl)

    for tt in range(SC):
        def lt(q, tt=tt):
            _gate(m, q, "s1_at")
            m.dma("qsy", q, out=sb3(att[:], ADC),
                  in_=fm_ap(m.dram["atfm"][:, tt * 128:(tt + 1) * 128]))
        blk.sync(lt)

        def tb(t, tt=tt):
            m.w(t, "qsy"); m.w(t, "qgp")
            _gate(m, t, "s1_pE")
            for dg in range(D // 512):
                for cc in range(ADC):
                    inst = t.matmul(pE[:, dg * 512:(dg + 1) * 512],
                                    att[:, cc * 128:(cc + 1) * 128],
                                    wo[:, cc * D + dg * 512: cc * D + (dg + 1) * 512],
                                    start=(cc == 0), stop=(cc == ADC - 1))
            m.i(inst, "pe")
            _mark(m, "s1_at", "pe")
        blk.tensor(tb)

        def vb(v, tt=tt):
            m.w(v, "pe")
            _gate(m, v, "s1_ses")
            m.i(v.tensor_copy(ses[:], pE[:]), "dve")
            _mark(m, "s1_pE", "dve")
        blk.vector(vb)

        def st(q, tt=tt):
            m.w(q, "dve")
            m.dma("qsy", q, out=m.dram["s1p"][tt * 128:(tt + 1) * 128, :], in_=ses[:])
            _mark(m, "s1_ses", "qsy")
        blk.sync(st)


PHASES["s1proj"] = dict(statics=["woTs"], reads=["atfm"], writes=["s1p"],
                        name="s1proj", emit=emit_s1proj)


def emit_comb1(m, blk, ps):
    nc, c = m.nc, m.c
    D, T, TT = c["D"], c["T"], c["TT"]
    x1t = ps.enter_context(nc.sbuf_tensor("cb_x1t", [128, D], F32))
    g1t = ps.enter_context(nc.sbuf_tensor("cb_g1t", [128, D], F32))
    s1t = ps.enter_context(nc.sbuf_tensor("cb_s1t", [128, D], F32))
    s1b = ps.enter_context(nc.sbuf_tensor("cb_s1b", [128, D], F32))
    ot = ps.enter_context(nc.sbuf_tensor("cb_ot", [128, D], F32))

    for tt in range(TT):
        def lt(q, tt=tt):
            _gate(m, q, "cb_x1t")
            m.dma("qsy", q, out=x1t[:], in_=m.dram["x1"][tt * 128:(tt + 1) * 128, :])
            _gate(m, q, "cb_g1t")
            m.dma("qsy", q, out=g1t[:], in_=m.dram["g1tm"][tt * 128:(tt + 1) * 128, :])
            _gate(m, q, "cb_s1t")
            m.dma("qsy", q, out=s1t[:], in_=m.dram["s1pA"][tt * 128:(tt + 1) * 128, :])
            m.dma("qsy", q, out=s1b[:], in_=m.dram["s1pB"][tt * 128:(tt + 1) * 128, :])
        blk.sync(lt)

        def vb(v, tt=tt):
            m.w(v, "qsy")
            m.i(v.tensor_add(s1t[:], s1t[:], s1b[:]), "dve")
            m.w(v, "dve")
            m.i(v.tensor_mul(g1t[:], s1t[:], g1t[:]), "dve")
            _mark(m, "cb_s1t", "dve")
            m.w(v, "dve")
            _gate(m, v, "cb_ot")
            m.i(v.tensor_add(ot[:], x1t[:], g1t[:]), "dve")
            _mark(m, "cb_x1t", "dve")
            _mark(m, "cb_g1t", "dve")
        blk.vector(vb)

        def st(q, tt=tt):
            m.w(q, "dve")
            m.dma("qsy", q, out=m.dram["out"][tt * 128:(tt + 1) * 128, :], in_=ot[:])
            _mark(m, "cb_ot", "qsy")
        blk.sync(st)


def _comb1_reads(fused):
    return ["x1", "g1tm"] + (["c5out"] if fused else ["s1pA", "s1pB"])


PHASES["comb1"] = dict(statics=[], reads=["x1", "g1tm", "s1pA", "s1pB"],
                       writes=["out"], name="comb1", emit=emit_comb1)


# version 17 (final)
# speedup vs baseline: 1.2114x; 1.2114x over previous
import os
import numpy as np
import ml_dtypes

DBG_OUT = [s for s in os.environ.get("KDBG", "").split(",") if s]
from contextlib import ExitStack

import concourse.bass as bass
import concourse.mybir as mybir
from concourse.bass_utils import run_bass_kernel_spmd
from concourse.ap import AP as RawAP

RACE_DETECT = False
I32 = mybir.dt.int32


def sym_rows(handle, row_sv, nrows, ncols):
    return RawAP(tensor=handle, offset=row_sv * ncols, ap=[[1, nrows * ncols]])


def sym_elem(handle, row_sv, col, ncols):
    return RawAP(tensor=handle, offset=row_sv * ncols + col, ap=[[1, 1], [1, 1]])

F32 = mybir.dt.float32
BF16 = mybir.dt.bfloat16
I8 = mybir.dt.int8
AF = mybir.ActivationFunctionType
ALU = mybir.AluOpType
AX = mybir.AxisListType

MAGIC = 12582912.0
ATANH_HALF = 0.5493061443340549
LN_EPS = 1e-5
RMS_EPS = 1e-8
AQ_CLIP = 1e-4
T_EPS = 1e-8


def full_cfg():
    return derive(dict(B=4, S=4096, D=2048, NH0=32, NKV0=8, HD0=64, INTER=8192,
                       NH1=16, HD1=128, rope_theta=10000.0))


def derive(c):
    c = dict(c)
    c["T"] = c["S"] // 2
    c["GR"] = c["NH0"] // c["NKV0"]
    c["KVD"] = c["NKV0"] * c["HD0"]
    c["DC"] = c["D"] // 128
    c["KVC"] = max(1, c["KVD"] // 128)
    c["TT"] = c["T"] // 128
    c["NG"] = min(512, c["T"])
    c["NTG"] = c["T"] // c["NG"]
    c["IC"] = c["INTER"] // 128
    c["JG"] = c["INTER"] // 512
    c["HC"] = c["NH1"] // 2
    c["AD"] = c["HC"] * c["HD1"]
    c["ADC"] = c["AD"] // 128
    c["KC"] = min(512, c["S"])
    c["SC"] = c["S"] // 128
    c["DG"] = c["D"] // 512
    c["GSZ"] = min(2, c["TT"])
    c["HPC"] = 2 * c["GR"]
    assert c["T"] % c["NG"] == 0 and c["INTER"] % 512 == 0 and c["D"] % 512 == 0
    return c



def prep_per_core(ins, cfg):
    c = cfg
    B, S, D, I = c["B"], c["S"], c["D"], c["INTER"]
    T, AD = c["T"], c["AD"]
    A = np.ascontiguousarray

    x = np.asarray(ins["L0_attn_hidden_states"])
    wq0 = np.asarray(ins["L0_attn_Wq"]); wk0 = np.asarray(ins["L0_attn_Wk"])
    wv0 = np.asarray(ins["L0_attn_Wv"]); wphi0 = np.asarray(ins["L0_attn_Wphi"])
    wo0 = np.asarray(ins["L0_attn_Wo"])
    wg = np.asarray(ins["L0_ffn_w_gate"]); wu = np.asarray(ins["L0_ffn_w_up"])
    wd = np.asarray(ins["L0_ffn_w_down"])
    sg = np.asarray(ins["L0_ffn_s_gate"]); su = np.asarray(ins["L0_ffn_s_up"])
    sd = np.asarray(ins["L0_ffn_s_down"])
    wq1 = np.asarray(ins["L1_attn_wq"]); wk1 = np.asarray(ins["L1_attn_wk"])
    wv1 = np.asarray(ins["L1_attn_wv"]); wo1 = np.asarray(ins["L1_attn_wo"])
    w1 = np.asarray(ins["L1_ffn_w1"]); g1 = np.asarray(ins["L1_ffn_g1"])
    w2 = np.asarray(ins["L1_ffn_w2"]); g2 = np.asarray(ins["L1_ffn_g2"])
    w3 = np.asarray(ins["L1_ffn_w3"]); g3 = np.asarray(ins["L1_ffn_g3"])

    IC, DC = c["IC"], c["DC"]
    shared = dict(
        wq0=A(wq0), wk0=A(wk0), wv0=A(wv0), wphi0=A(wphi0), wo0=A(wo0),
        wgt=A(wg.reshape(D, IC, 128).transpose(1, 0, 2)),
        wut=A(wu.reshape(D, IC, 128).transpose(1, 0, 2)),
        wdt=A(wd.reshape(I, DC, 128).transpose(1, 0, 2)),
        sgs=A(sg.reshape(IC, 128).T), sus=A(su.reshape(IC, 128).T),
        sds=A(sd.reshape(DC, 128).T),
        w1T=A(w1.T), w2T=A(w2.T), w3T=A(w3.T),
        g1v=A(g1), g2v=A(g2), g3v=A(g3),
    )

    HD0 = c["HD0"]
    inv = (1.0 / cfg["rope_theta"] ** (np.arange(0, HD0, 2) / HD0)).astype(np.float64)
    pos = np.arange(S, dtype=np.float64)
    ang = pos[None, :] * inv[:, None]
    cos64 = np.cos(np.concatenate([ang, ang], 0))
    sin64 = np.sin(np.concatenate([ang, ang], 0))
    cos128 = np.concatenate([cos64, cos64], 0).astype(np.float32)
    sin128 = np.concatenate([sin64, sin64], 0).astype(np.float32)

    per_core = []
    for core in range(2 * B):
        b, role = core // 2, core % 2
        hs = slice(role * AD, role * AD + AD)
        d = dict(shared)
        d["x_in"] = A(x[b, role * T:(role + 1) * T, :].astype(np.float32))
        d["cosfm"] = A(cos128[:, role * T:(role + 1) * T])
        d["sinfm"] = A(sin128[:, role * T:(role + 1) * T])
        d["wqTs"] = A(wq1.T[:, hs]); d["wkTs"] = A(wk1.T[:, hs])
        d["wvTs"] = A(wv1.T[:, hs]); d["woTs"] = A(wo1.T[hs, :])
        per_core.append(d)
    return per_core


def inline_consts(nc, cfg):
    KC = cfg["KC"]
    ident = np.eye(128, dtype=ml_dtypes.bfloat16)
    identf = np.eye(128, dtype=np.float32)
    pr = np.zeros((128, 128), np.float32)
    for dd in range(128):
        if dd % 64 < 32:
            pr[dd, dd + 32] = -1.0
        else:
            pr[dd, dd - 32] = 1.0
    protT = pr.T.astype(ml_dtypes.bfloat16)
    masks = np.zeros((128, 4 * KC), np.float32)
    for m in range(4):
        i = np.arange(128)[:, None]
        j = np.arange(KC)[None, :]
        masks[:, m * KC:(m + 1) * KC] = np.where(j <= 128 * m + i, 0.0, -1e9)
    cvals = np.zeros((128, 8), np.float32)
    cvals[:, 0] = LN_EPS
    cvals[:, 1] = 1.0
    masks_t = np.where(np.arange(128)[None, :] >= np.arange(128)[:, None], 0.0, -1e9).astype(np.float32)
    onesb = np.ones((128, 1), ml_dtypes.bfloat16)
    onesf_row = np.ones((1, 128), np.float32)
    onesf_col = np.ones((128, 1), np.float32)
    return dict(
        ident=nc.inline_tensor(ident, name="c_ident"),
        identf=nc.inline_tensor(identf, name="c_identf"),
        protT=nc.inline_tensor(protT, name="c_protT"),
        masks=nc.inline_tensor(masks, name="c_masks"),
        maskt=nc.inline_tensor(masks_t, name="c_maskt"),
        onesb=nc.inline_tensor(onesb, name="c_onesb"),
        onesf_row=nc.inline_tensor(onesf_row, name="c_onesfr"),
        cvals=nc.inline_tensor(cvals, name="c_cvals"),
        onesf_col=nc.inline_tensor(onesf_col, name="c_onesfc"),
    )



def dyn_tensors(c):
    T, D, I, S, AD = c["T"], c["D"], c["INTER"], c["S"], c["AD"]
    NH0, TT = c["NH0"], c["TT"]
    return {
        "xnfm0": ([D, T], BF16),
        "qkfm": ([D, T], BF16),
        "kkfm": ([c["KVC"] * 128, T], BF16),
        "kktm": ([T, c["KVC"] * 128], BF16),
        "vtm0": ([T, c["KVC"] * 128], BF16),
        "qgp": ([128, c["DC"]], F32),
        "c1out": ([256, c["DC"]], F32),
        "lhm": ([NH0, T], F32),
        "mcp": ([NH0, 1], F32),
        "c2out": ([2 * NH0, 1], F32),
        "zp": ([NH0, 1], F32),
        "c3aout": ([2 * NH0, 1], F32),
        "up": ([64, NH0 * 64], F32),
        "c3bout": ([128, NH0 * 64], F32),
        "fuse": ([I, T], BF16),
        "g0tm": ([T, D], F32),
        "ctx": ([D, T], BF16),
        "x1": ([T, D], F32),
        "t1i": ([c["JG"], D, 512], I8),
        "t2i": ([c["JG"], D, 512], I8),
        "t3i": ([c["DG"], I, 512], I8),
        "terns": ([1, 8], F32),
        "xn1fm": ([D, T], BF16),
        "ln1st": ([128, 2 * TT], F32),
        "c4out": ([2 * D, T], BF16),
        "xq1fm": ([D, T], BF16),
        "xq2fm": ([D, T], BF16),
        "k12": ([128, 2 * TT], F32),
        "hdr": ([T, I], F32),
        "g1tm": ([T, D], F32),
        "qfm1": ([AD, S], BF16),
        "kfm1": ([AD, S], BF16),
        "vtm1": ([S, AD], BF16),
        "atfm": ([AD, S], BF16),
        "s1p": ([S, D], F32),
        "c5out": ([T, D], F32),
        "s1pA": ([T, D], F32),
        "s1pB": ([T, D], F32),
        "out": ([T, D], F32),
    }


def static_tensors(c):
    T, D, I, AD = c["T"], c["D"], c["INTER"], c["AD"]
    return {
        "nonce": ([2, 32], I32),
        "xw_c1": ([128, 1], I32), "xw_c2": ([128, 1], I32), "xw_c3a": ([128, 1], I32),
        "xw_c3b": ([128, 1], I32),
        "xw_c4": ([128, (c["D"] + 127) // 128], I32),
        "xw_c5": ([128, (c["T"] + 127) // 128], I32),
        "xr_c5": ([128, (c["T"] + 127) // 128], I32),
        "xf_w": ([2, 16], I32), "xf_r": ([2, 16], I32),
        "x_in": ([T, D], F32),
        "wq0": ([D, D], F32), "wk0": ([D, c["KVD"]], F32), "wv0": ([D, c["KVD"]], F32),
        "wphi0": ([D, D], F32), "wo0": ([D, D], F32),
        "wgt": ([c["IC"], D, 128], I8), "wut": ([c["IC"], D, 128], I8),
        "wdt": ([c["DC"], I, 128], I8),
        "sgs": ([128, c["IC"]], F32), "sus": ([128, c["IC"]], F32),
        "sds": ([128, c["DC"]], F32),
        "w1T": ([D, I], F32), "w2T": ([D, I], F32), "w3T": ([I, D], F32),
        "g1v": ([D], F32), "g2v": ([D], F32), "g3v": ([I], F32),
        "cosfm": ([128, T], F32), "sinfm": ([128, T], F32),
        "wqTs": ([D, AD], F32), "wkTs": ([D, AD], F32), "wvTs": ([D, AD], F32),
        "woTs": ([AD, D], F32),
    }



class M:

    def __init__(self, nc, cfg, fused):
        self.nc = nc
        self.c = cfg
        self.fused = fused
        self.counts = {}
        self.sems = {}
        self.gstack = ExitStack()
        self.consts = None
        self.dram = {}
        self._const_sb = None
        self.eids = {}
        self.polreg = None
        self.ixcache = {}
        self.xbounce = None

    def sem(self, name):
        if name not in self.sems:
            self.sems[name] = self.gstack.enter_context(self.nc.semaphore(name))
            self.counts[name] = 0
        return self.sems[name]

    def i(self, inst, s, n=1):
        h = self.sem(s)
        inst.then_inc(h, n)
        self.counts[s] += n
        return self.counts[s]

    def w(self, eng, s, tok=None):
        h = self.sem(s)
        eng.wait_ge(h, self.counts[s] if tok is None else tok)

    def dma(self, q, eng, out, in_, fence=True, stream=None, **kw):
        s = q if stream is None else f"{q}s{stream}"
        tok = self.i(eng.dma_start(out=out, in_=in_, **kw), s, 16)
        if fence:
            self.w(eng, s, tok)
        return (s, tok)

    def wt(self, eng, tok):
        self.w(eng, tok[0], tok[1])


PHASES = {}


def phase(name, statics=(), reads=(), writes=()):
    def deco(fn):
        PHASES[name] = dict(statics=list(statics), reads=list(reads),
                            writes=list(writes), emit=fn, name=name)
        return fn
    return deco


COLLS = {}


def coll(name, src, dst, kind, op, host):
    COLLS[name] = dict(name=name, src=src, dst=dst, kind=kind, op=op, host=host)



def _gate(m, eng, key):
    if key in m.free:
        s, tok = m.free[key]
        m.w(eng, s, tok)


def _mark(m, key, s):
    m.free[key] = (s, m.counts[s])


def _markt(m, key, tok):
    m.free[key] = tok


def fm_ap(ap, p=128):
    return ap.rearrange("(c p) n -> p c n", p=p)


def sb3(ap, nch):
    return ap.rearrange("p (c n) -> p c n", c=nch)



def emit_ln(m, blk, ps, src, dstfm, dstst=None):
    nc, c = m.nc, m.c
    D, DC, TT = c["D"], c["DC"], c["TT"]
    xt = [ps.enter_context(nc.sbuf_tensor(f"ln_x{i}", [128, D], F32)) for i in range(2)]
    sq = ps.enter_context(nc.sbuf_tensor("ln_sq", [128, D], F32))
    st = ps.enter_context(nc.sbuf_tensor("ln_st", [128, 8], F32))
    xnt = [ps.enter_context(nc.sbuf_tensor(f"ln_xnt{i}", [128, D], BF16)) for i in range(2)]
    stg = [ps.enter_context(nc.sbuf_tensor(f"ln_stg{i}", [128, D], BF16)) for i in range(2)]
    stats = ps.enter_context(nc.sbuf_tensor("ln_stats", [128, 2 * TT], F32))
    pbf = [ps.enter_context(nc.psum_tensor(f"ln_p{i}", [128, 128], BF16)) for i in range(2)]
    xsrc = m.dram[src]
    xdst = m.dram[dstfm]

    for tt in range(TT):
        bx = xt[tt % 2]
        bn = xnt[tt % 2]
        bs = stg[tt % 2]

        def qload(q, tt=tt, bx=bx):
            _gate(m, q, f"ln_x{tt % 2}")
            m.dma("qsy", q, out=bx[:], in_=xsrc[tt * 128:(tt + 1) * 128, :])
        blk.sync(qload)

        def v1(v, tt=tt, bx=bx):
            m.w(v, "qsy")
            m.i(v.reduce_sum(st[:, 0:1], bx[:], axis=AX.X), "dve")
        blk.vector(v1)

        def s1(s, tt=tt, bx=bx):
            m.w(s, "dve")
            _gate(m, s, "ln_sq")
            m.i(s.activation(sq[:], bx[:], AF.Square, accum_out=st[:, 1:2]), "act")
        blk.scalar(s1)

        def v2(v, tt=tt):
            m.w(v, "act")
            mcol = stats[:, 2 * tt:2 * tt + 1]
            m.i(v.tensor_scalar(mcol, st[:, 0:1], 1.0 / D, None, ALU.mult), "dve")
            m.w(v, "dve")
            m.i(v.tensor_tensor(st[:, 3:4], mcol, mcol, ALU.mult), "dve")
            m.w(v, "dve")
            m.i(v.scalar_tensor_tensor(st[:, 4:5], st[:, 1:2], 1.0 / D, st[:, 3:4],
                                       ALU.mult, ALU.subtract), "dve")
            _mark(m, "ln_sq", "dve")
        blk.vector(v2)

        def s2(s, tt=tt):
            m.w(s, "dve")
            m.i(s.activation(st[:, 5:6], st[:, 4:5], AF.Sqrt,
                             bias=m.sbc["cvals"][:, 0:1]), "act")
        blk.scalar(s2)

        def v3(v, tt=tt, bx=bx, bn=bn):
            m.w(v, "act")
            rcol = stats[:, 2 * tt + 1:2 * tt + 2]
            mcol = stats[:, 2 * tt:2 * tt + 1]
            m.i(v.reciprocal(rcol, st[:, 5:6]), "dve")
            m.w(v, "dve")
            _gate(m, v, f"ln_xnt{tt % 2}")
            m.i(v.tensor_scalar(bn[:], bx[:], mcol, rcol, ALU.subtract, ALU.mult), "dve")
            _mark(m, f"ln_x{tt % 2}", "dve")
        blk.vector(v3)

        for cc in range(DC):
            def tb(t, tt=tt, cc=cc, bn=bn):
                m.w(t, "dve")
                _gate(m, t, f"ln_p{cc % 2}")
                m.i(t.transpose(pbf[cc % 2][:], bn[:, cc * 128:(cc + 1) * 128],
                                m.sbc["ident"][:]), "pe")
            blk.tensor(tb)

            def ve(v, tt=tt, cc=cc, bs=bs):
                m.w(v, "pe")
                if cc == 0:
                    _gate(m, v, f"ln_stg{tt % 2}")
                m.i(v.tensor_copy(bs[:, cc * 128:(cc + 1) * 128], pbf[cc % 2][:]), "dve")
                _mark(m, f"ln_p{cc % 2}", "dve")
                if cc == DC - 1:
                    _mark(m, f"ln_xnt{tt % 2}", "dve")
            blk.vector(ve)

        def qst(q, tt=tt, bs=bs):
            m.w(q, "dve")
            m.dma("qsy", q, out=fm_ap(xdst[:, tt * 128:(tt + 1) * 128]),
                  in_=sb3(bs[:], DC))
            _mark(m, f"ln_stg{tt % 2}", "qsy")
        blk.sync(qst)

    if dstst is not None:
        def qst2(q):
            m.w(q, "dve")
            m.dma("qsy", q, out=m.dram[dstst][:], in_=stats[:])
        blk.sync(qst2)


PHASES["ln0"] = dict(statics=["x_in"], reads=[], writes=["xnfm0"], name="ln0",
                     emit=lambda m, blk, ps: emit_ln(m, blk, ps, "x_in", "xnfm0"))
PHASES["ln1"] = dict(statics=[], reads=["x1"], writes=["xn1fm", "ln1st"], name="ln1",
                     emit=lambda m, blk, ps: emit_ln(m, blk, ps, "x1", "xn1fm", "ln1st"))



COLLS["c1"] = dict(src="qgp", dst="c1out", kind="AllGather", op=ALU.bypass)
COLLS["c2"] = dict(src="mcp", dst="c2out", kind="AllGather", op=ALU.bypass)
COLLS["c3a"] = dict(src="zp", dst="c3aout", kind="AllGather", op=ALU.bypass)
COLLS["c3b"] = dict(src="up", dst="c3bout", kind="AllGather", op=ALU.bypass)
COLLS["c4"] = dict(src="xn1fm", dst="c4out", kind="AllGather", op=ALU.bypass)
COLLS["c5"] = dict(src="s1p", dst="c5out", kind="ReduceScatter", op=ALU.add)


def coll_host(name, env, cfg):
    cl = COLLS[name]
    ncores = 2 * cfg["B"]
    for c in range(ncores):
        p0, p1 = (c // 2) * 2, (c // 2) * 2 + 1
        if cl["kind"] == "AllGather":
            env[c][cl["dst"]] = np.concatenate(
                [env[p0][cl["src"]], env[p1][cl["src"]]], axis=0)
        else:
            T = cfg["T"]
            role = c % 2
            env[c]["s1pA"] = np.ascontiguousarray(env[p0]["s1p"][role * T:(role + 1) * T])
            env[c]["s1pB"] = np.ascontiguousarray(env[p1]["s1p"][role * T:(role + 1) * T])


def _idx_sb(m, ps, name, cols, rows=128):
    nc = m.nc
    t = ps.enter_context(nc.sbuf_tensor(f"ix_{name}", [rows, cols], I32))
    with nc.Block() as blk:
        def ld(g):
            m.dma("qgp", g, out=t[:], in_=m.dram[name][:])
        blk.gpsimd(ld)
    return t


def _xbounce(m, ps, cols, dt):
    if m.xbounce is None:
        m.xbounce = ps.enter_context(m.nc.sbuf_tensor("xbounce", [128, 2048], F32))
    if dt == F32:
        return m.xbounce[:, 0:cols]
    return m.xbounce[:].bitcast(dt)[:, 0:cols]


def _scatter_rows(m, g, dst_h, src_ap, rows, cols, idx_t):
    IOA = bass.IndirectOffsetOnAxis
    bt = _xbounce(m, m.collps, cols, src_ap.dtype)
    nch = (rows + 127) // 128
    for ch in range(nch):
        r0 = ch * 128
        r1 = min(rows, r0 + 128)
        n = r1 - r0
        m.dma("qgp", g, out=bt[0:n, :], in_=src_ap[r0:r1, :])
        inst = g.indirect_dma_start(
            out=dst_h[:], out_offset=IOA(idx_t[0:n, ch:ch + 1], 0),
            in_=bt[0:n, :], in_offset=None)
        m.i(inst, "qgp", 16)
        m.w(g, "qgp")


def _gather_rows(m, g, dst_ap, src_h, rows, cols, idx_t):
    IOA = bass.IndirectOffsetOnAxis
    bt = _xbounce(m, m.collps, cols, dst_ap.dtype)
    nch = (rows + 127) // 128
    for ch in range(nch):
        r0 = ch * 128
        r1 = min(rows, r0 + 128)
        n = r1 - r0
        inst = g.indirect_dma_start(
            out=bt[0:n, :], out_offset=None,
            in_=src_h[:], in_offset=IOA(idx_t[0:n, ch:ch + 1], 0))
        m.i(inst, "qgp", 16)
        m.w(g, "qgp")
        m.dma("qgp", g, out=dst_ap[r0:r1, :], in_=bt[0:n, :])


def emit_coll(m, name):
    with ExitStack() as collps:
        _emit_coll_inner(m, name, collps)


def _emit_coll_inner(m, name, collps):
    cl = COLLS[name]
    m.collps = collps
    m.xbounce = None
    nc, c = m.nc, m.c
    Pool = mybir.EngineType.Pool
    dyn = dyn_tensors(c)
    eid = m.eids.setdefault(name, len(m.eids))
    src_h, dst_h = m.dram[cl["src"]], m.dram[cl["dst"] if cl["kind"] == "AllGather" else "s1pA"]

    if "shflags" not in m.dram:
        m.dram["shflags"] = nc.dram_tensor("shflags", [32, 2], I32, addr_space="Shared")
        m.sh_nonce = m.gstack.enter_context(nc.sbuf_tensor("sh_nonce", [2, 32], I32))
        m.sh_fsb = m.gstack.enter_context(nc.sbuf_tensor("sh_fsb", [2, 32], I32))
        with nc.Block() as blk0:
            def ld0(g):
                m.dma("qgp", g, out=m.sh_nonce[:], in_=m.dram["nonce"][:])
            blk0.gpsimd(ld0)
    shf = m.dram["shflags"]
    fw = _idx_sb(m, collps, "xf_w", 16, rows=2)
    fr = _idx_sb(m, collps, "xf_r", 16, rows=2)
    m.fr_tile = fr

    if cl["kind"] == "AllGather":
        rows, cols = dyn[cl["src"]][0]
        sh = nc.dram_tensor(f"shx_{name}", [2 * rows, cols], dyn[cl["src"]][1],
                            addr_space="Shared")
        wix = _idx_sb(m, collps, "xw_" + name, (rows + 127) // 128)
        with nc.Block() as blk:
            def gw(g):
                m.w(g, "qsy"); m.w(g, "qsc"); m.w(g, "qgp")
                _scatter_rows(m, g, sh, src_h[:], rows, cols, wix)
                inst = g.indirect_dma_start(
                    out=shf[:],
                    out_offset=bass.IndirectOffsetOnAxis(fw[0:2, eid:eid + 1], 0),
                    in_=m.sh_nonce[0:2, 2 * eid:2 * eid + 2], in_offset=None)
                m.i(inst, "qgp", 16)
                m.w(g, "qgp")
            blk.gpsimd(gw)
        _emit_poll(m, name, eid)
        with nc.Block() as blk2:
            def gr(g):
                m.dma("qgp", g, out=dst_h[:], in_=sh[:])
            blk2.gpsimd(gr)
    else:
        T, D = c["T"], c["D"]
        sh = nc.dram_tensor(f"shx_{name}", [2 * T, D], F32, addr_space="Shared")
        wix = _idx_sb(m, collps, "xw_c5", (T + 127) // 128)
        rix = _idx_sb(m, collps, "xr_c5", (T + 127) // 128)
        bteid = nc.dram_tensor("c5_bounce", [T, D], F32, kind="Internal")
        with nc.Block() as blkb:
            def gb(g):
                m.w(g, "qsy"); m.w(g, "qsc"); m.w(g, "qgp")
                _gather_rows(m, g, bteid[:], m.dram["s1p"], T, D, wix)
                _scatter_rows(m, g, sh, bteid[:], T, D, wix)
                inst = g.indirect_dma_start(
                    out=shf[:],
                    out_offset=bass.IndirectOffsetOnAxis(fw[0:2, eid:eid + 1], 0),
                    in_=m.sh_nonce[0:2, 2 * eid:2 * eid + 2], in_offset=None)
                m.i(inst, "qgp", 16)
                m.w(g, "qgp")
            blkb.gpsimd(gb)
        _emit_poll(m, name, eid)
        with nc.Block() as blk2:
            def gr(g):
                _gather_rows(m, g, m.dram["s1pA"][:], m.dram["s1p"], T, D, rix)
                _gather_rows(m, g, m.dram["s1pB"][:], sh, T, D, rix)
            blk2.gpsimd(gr)
    nc.all_engine_barrier()


def _emit_poll(m, name, eid):
    nc = m.nc
    Pool = mybir.EngineType.Pool
    eng = [Pool]
    psem = m.sem(f"poll_{name}")
    qtot = m.counts["qgp"]
    fr = m.fr_tile
    if m.polreg is None:
        m.polreg = nc.alloc_registers("pollreg", engines=eng)
        m.flagreg = nc.alloc_registers("flagreg", engines=eng)
        m.noncereg = nc.alloc_registers("noncereg", engines=eng)
    nc.regs_mov(m.polreg, 0)
    nc.regs_load(m.noncereg, m.sh_nonce[0:1, 2 * eid:2 * eid + 1])
    loop, done = f"poll_{name}_loop", f"poll_{name}_done"
    nc.br(loop, engines=eng)
    with nc.body(loop, valid_engines=eng):
        gp = nc.gpsimd
        gp.wait_ge(m.sems["qgp"], qtot)
        gp.indirect_dma_start(
            out=m.sh_fsb[0:2, 0:2], out_offset=None,
            in_=m.dram["shflags"][:],
            in_offset=bass.IndirectOffsetOnAxis(fr[0:2, eid:eid + 1], 0)).then_inc(psem, 16)
        nc.regs_alu(m.polreg, m.polreg, 16, op=ALU.add)
        gp.wait_ge(psem, m.polreg)
        nc.regs_load(m.flagreg, m.sh_fsb[0:1, 0:1])
        nc.br_cmp(m.flagreg, m.noncereg, on_true=done, on_false=loop,
                  comp_op="IS_EQ", engines=eng)
    nc.switch_bb(done)



def build_launch(cfg, group, fused):
    nc = bass.Bass("TRN2", target_bir_lowering=False, debug=False,
                   detect_race_conditions=RACE_DETECT)
    _orig_sb, _orig_ps = nc.sbuf_tensor, nc.psum_tensor
    def _sb(*a, **kw):
        kw.setdefault("allow_name_mangling", True)
        return _orig_sb(*a, **kw)
    def _pp(*a, **kw):
        kw.setdefault("allow_name_mangling", True)
        return _orig_ps(*a, **kw)
    nc.sbuf_tensor, nc.psum_tensor = _sb, _pp
    m = M(nc, cfg, fused)
    m.free = {}
    stat_reg = static_tensors(cfg)
    dyn_reg = dyn_tensors(cfg)
    statics, reads, writes = [], [], []
    on_dev_coll = fused or len(group) > 1
    if on_dev_coll and any(nm in COLLS for nm in group):
        statics.extend(["nonce", "xw_c1", "xw_c2", "xw_c3a", "xw_c3b", "xw_c4",
                        "xw_c5", "xr_c5", "xf_w", "xf_r"])
    for nm in group:
        if nm in COLLS:
            cl = COLLS[nm]
            if on_dev_coll:
                if cl["src"] not in writes and cl["src"] not in reads:
                    reads.append(cl["src"])
                dsts = ["s1pA", "s1pB"] if nm == "c5" else [cl["dst"]]
                for dd in dsts:
                    if dd not in writes:
                        writes.append(dd)
            continue
        ph = PHASES[nm]
        for s in ph["statics"]:
            if s not in statics:
                statics.append(s)
        for r in ph["reads"]:
            if r not in writes and r not in reads:
                reads.append(r)
        for wn in ph["writes"]:
            if wn not in writes:
                writes.append(wn)
    for s in statics:
        shp, dt = stat_reg[s]
        m.dram[s] = nc.dram_tensor(s, shp, dt, kind="ExternalInput")
    if fused:
        reads = [r for r in reads if r not in ("s1pA", "s1pB")]
    for r in reads:
        shp, dt = dyn_reg[r]
        m.dram[r] = nc.dram_tensor(r, shp, dt, kind="ExternalInput")
    for wn in writes:
        shp, dt = dyn_reg[wn]
        kind = "ExternalOutput" if (not fused or wn == "out" or wn in DBG_OUT) else "Internal"
        m.dram[wn] = nc.dram_tensor(wn, shp, dt, kind=kind)
    for k, h in inline_consts(nc, cfg).items():
        m.dram[k] = h

    KC = cfg["KC"]
    m.sbc = {
        "ident": m.gstack.enter_context(nc.sbuf_tensor("sb_ident", [128, 128], BF16)),
        "identf": m.gstack.enter_context(nc.sbuf_tensor("sb_identf", [128, 128], F32)),
        "protT": m.gstack.enter_context(nc.sbuf_tensor("sb_protT", [128, 128], BF16)),
        "masks": m.gstack.enter_context(nc.sbuf_tensor("sb_masks", [128, 4 * KC], F32)),
        "maskt": m.gstack.enter_context(nc.sbuf_tensor("sb_maskt", [128, 128], F32)),
        "onesb": m.gstack.enter_context(nc.sbuf_tensor("sb_onesb", [128, 1], BF16)),
        "onesf_row": m.gstack.enter_context(nc.sbuf_tensor("sb_onesfr", [1, 128], F32)),
        "cvals": m.gstack.enter_context(nc.sbuf_tensor("sb_cvals", [128, 8], F32)),
        "onesf_col": m.gstack.enter_context(nc.sbuf_tensor("sb_onesfc", [128, 1], F32)),
    }
    with nc.Block() as blk:
        def pre(q):
            for k in m.sbc:
                m.dma("qsy", q, out=m.sbc[k][:], in_=m.dram[k][:])
            m.w(q, "qsy")
        blk.sync(pre)
    nc.all_engine_barrier()

    def _dma_drain():
        with nc.Block() as dblk:
            def dsy(q):
                for s in list(m.sems):
                    if s.startswith("qsy"):
                        m.w(q, s)
            dblk.sync(dsy)
            def dsc(s_):
                for s in list(m.sems):
                    if s.startswith("qsc"):
                        m.w(s_, s)
            dblk.scalar(dsc)
            def dgp(g):
                for s in list(m.sems):
                    if s.startswith("qgp"):
                        m.w(g, s)
            dblk.gpsimd(dgp)

    for nm in group:
        if nm in COLLS:
            if on_dev_coll:
                with nc.named_scope("X_" + nm):
                    emit_coll(m, nm)
            continue
        with nc.named_scope("P_" + nm):
            with ExitStack() as ps:
                with nc.Block() as blk:
                    PHASES[nm]["emit"](m, blk, ps)
            _dma_drain()
            nc.all_engine_barrier()

    with nc.Block() as blk:
        def fsy(q):
            for s in list(m.sems):
                if s.startswith("qsy"):
                    m.w(q, s)
        blk.sync(fsy)
        def fsc(s_):
            for s in list(m.sems):
                if s.startswith("qsc"):
                    m.w(s_, s)
        blk.scalar(fsc)
        def fgp(g):
            for s in list(m.sems):
                if s.startswith("qgp") or s.startswith("poll") or s == "qcl":
                    m.w(g, s)
        blk.gpsimd(fgp)
    m.gstack.close()
    io = dict(statics=statics, reads=reads,
              outputs=[wn for wn in writes if (not fused or wn == "out" or wn in DBG_OUT)])
    return nc, io



PHASE_ORDER = [
    "tern", "ln0", "qkv0", "c1", "logits", "c2", "zrow", "c3a", "uacc", "c3b",
    "ffn0a", "ffn0b", "ctx0", "out0", "ln1", "c4", "quant1", "uh1", "hdown",
    "proj1", "flash", "s1proj", "c5", "comb1",
]
LAUNCH_PLAN_UNFUSED = [[p] for p in PHASE_ORDER]
LAUNCH_PLAN_FUSED = [PHASE_ORDER]
LAUNCH_PLAN = LAUNCH_PLAN_FUSED


def kernel_impl(inputs, cfg, runner, plan=None, fused=None, env0=None):
    plan = plan if plan is not None else LAUNCH_PLAN
    per_core = prep_per_core(inputs, cfg)
    base = int(np.random.default_rng().integers(1, 2 ** 30))
    nonce = np.repeat(base + np.arange(16, dtype=np.int32), 2).reshape(1, 32).astype(np.int32)
    nonce = np.concatenate([nonce, nonce], axis=0)

    def colmajor(vals):
        n = len(vals)
        nch = (n + 127) // 128
        a = np.zeros((128, nch), np.int32)
        for i, v in enumerate(vals):
            a[i % 128, i // 128] = v
        return a

    dynr = dyn_tensors(cfg)
    for ci, d in enumerate(per_core):
        rank = ci % 2
        prank = 1 - rank
        d["nonce"] = nonce
        for nm2 in ("c1", "c2", "c3a", "c3b", "c4"):
            rows = dynr[COLLS[nm2]["src"]][0][0]
            d["xw_" + nm2] = colmajor([rank * rows + i for i in range(rows)])
        T = cfg["T"]
        d["xw_c5"] = colmajor([prank * T + i for i in range(T)])
        d["xr_c5"] = colmajor([rank * T + i for i in range(T)])
        fwv = np.array([2 * e + rank for e in range(16)], np.int32)
        frv = np.array([2 * e + prank for e in range(16)], np.int32)
        d["xf_w"] = np.stack([fwv, fwv])
        d["xf_r"] = np.stack([frv, frv])
    ncores = 2 * cfg["B"]
    env = [dict(e) for e in env0] if env0 is not None else [dict() for _ in range(ncores)]
    if fused is None:
        fused = len(plan) == 1 and len(plan[0]) > 4
    for group in plan:
        dev = [g for g in group if g not in COLLS]
        if not dev and not fused:
            for g in group:
                coll_host(g, env, cfg)
            continue
        nc, io = build_launch(cfg, group, fused)
        in_maps = []
        for c in range(ncores):
            im = {k: per_core[c][k] for k in io["statics"]}
            im.update({k: env[c][k] for k in io["reads"]})
            in_maps.append(im)
        res = runner(nc, in_maps, core_ids=list(range(ncores)))
        for c in range(ncores):
            for k in io["outputs"]:
                env[c][k] = np.asarray(res.results[c][k])
        if not fused and len(group) == 1:
            for g in group:
                if g in COLLS:
                    coll_host(g, env, cfg)
    B, T, D, S = cfg["B"], cfg["T"], cfg["D"], cfg["S"]
    if not all("out" in e for e in env):
        return None, env
    out = np.zeros((B, S, D), np.float32)
    for c in range(ncores):
        b, role = c // 2, c % 2
        out[b, role * T:(role + 1) * T, :] = env[c]["out"]
    return out, env


def kernel(**inputs):
    out, _ = kernel_impl(inputs, full_cfg(), run_bass_kernel_spmd)
    return out



def emit_qkv0(m, blk, ps):
    nc, c = m.nc, m.c
    D, DC, T, NG, NTG, TT = c["D"], c["DC"], c["T"], c["NG"], c["NTG"], c["TT"]
    KVD, KVC = c["KVD"], c["KVC"]
    xn = ps.enter_context(nc.sbuf_tensor("qk_xn", [128, DC * T], BF16))
    wsb = ps.enter_context(nc.sbuf_tensor("qk_w", [128, DC * D], BF16))
    cosb = ps.enter_context(nc.sbuf_tensor("qk_cos", [128, T], F32))
    sinb = ps.enter_context(nc.sbuf_tensor("qk_sin", [128, T], F32))
    eA = ps.enter_context(nc.sbuf_tensor("qk_eA", [128, NG], F32))
    eB = ps.enter_context(nc.sbuf_tensor("qk_eB", [128, NG], F32))
    eC = ps.enter_context(nc.sbuf_tensor("qk_eC", [128, NG], F32))
    qpre = ps.enter_context(nc.sbuf_tensor("qk_qpre", [128, NG], BF16))
    stg = [ps.enter_context(nc.sbuf_tensor(f"qk_stg{i}", [128, T], BF16)) for i in range(2)]
    tstg = ps.enter_context(nc.sbuf_tensor("qk_tstg", [128, 128], BF16))
    qg = ps.enter_context(nc.sbuf_tensor("qk_qg", [128, DC], F32))
    qgt = ps.enter_context(nc.sbuf_tensor("qk_qgt", [128, 1], F32))
    vev = ps.enter_context(nc.sbuf_tensor("qk_vev", [128, max(1, m.c["KVD"])], BF16))
    pA = [ps.enter_context(nc.psum_tensor(f"qk_pA{i}", [128, NG], F32)) for i in range(2)]
    pR = [ps.enter_context(nc.psum_tensor(f"qk_pR{i}", [128, NG], F32)) for i in range(2)]
    pT = [ps.enter_context(nc.psum_tensor(f"qk_pT{i}", [128, 128], BF16)) for i in range(2)]

    def pre(q):
        m.dma("qsy", q, out=sb3(xn[:], DC), in_=fm_ap(m.dram["xnfm0"][:]))
        m.dma("qsy", q, out=cosb[:], in_=m.dram["cosfm"][:])
        m.dma("qsy", q, out=sinb[:], in_=m.dram["sinfm"][:])
    blk.sync(pre)

    def wload(g, wname, cols):
        def f(gp):
            m.w(gp, "pe")
            m.dma("qgp", gp, out=sb3(wsb[:, 0:DC * cols], DC),
                  in_=fm_ap(m.dram[wname][:]))
        blk.gpsimd(f)

    def proj_rope(outdram, nchunks, cols, do_qg):
        for co in range(nchunks):
            bs = stg[co % 2]
            for tg in range(NTG):
                i = (co * NTG + tg) % 2

                def b1(t, co=co, tg=tg, i=i):
                    m.w(t, "qsy"); m.w(t, "qgp")
                    _gate(m, t, f"qk_pA{i}")
                    for cc in range(DC):
                        inst = t.matmul(pA[i][:], wsb[:, cc * cols + co * 128: cc * cols + co * 128 + 128],
                                        xn[:, cc * T + tg * NG: cc * T + (tg + 1) * NG],
                                        start=(cc == 0), stop=(cc == DC - 1))
                    m.i(inst, "pe")
                blk.tensor(b1)

                def b2(v, co=co, tg=tg, i=i):
                    m.w(v, "pe")
                    _gate(m, v, "qk_qpre")
                    m.i(v.tensor_copy(qpre[:], pA[i][:]), "dve")
                    _gate(m, v, "qk_eA")
                    m.i(v.tensor_tensor(eA[:], pA[i][:], cosb[:, tg * NG:(tg + 1) * NG], ALU.mult), "dve")
                    _mark(m, f"qk_pA{i}", "dve")
                blk.vector(b2)

                def b3(t, co=co, tg=tg, i=i):
                    m.w(t, "dve")
                    _gate(m, t, f"qk_pR{i}")
                    m.i(t.matmul(pR[i][:], m.sbc["protT"][:], qpre[:], start=True, stop=True), "pe")
                blk.tensor(b3)

                def b4(v, co=co, tg=tg, i=i):
                    m.w(v, "pe")
                    _mark(m, "qk_qpre", "pe")
                    _gate(m, v, "qk_eB")
                    m.i(v.tensor_tensor(eB[:], pR[i][:], sinb[:, tg * NG:(tg + 1) * NG], ALU.mult), "dve")
                    _mark(m, f"qk_pR{i}", "dve")
                    m.w(v, "dve")
                    _gate(m, v, "qk_eC")
                    m.i(v.tensor_add(eC[:], eA[:], eB[:]), "dve")
                    _mark(m, "qk_eA", "dve")
                    m.w(v, "dve")
                    m.i(v.tensor_scalar_min(eB[:], eC[:], 0.0), "dve")
                blk.vector(b4)

                def b5(s, co=co, tg=tg):
                    m.w(s, "dve")
                    m.i(s.activation(eA[:], eB[:], AF.Exp), "act")
                blk.scalar(b5)

                def b6(v, co=co, tg=tg, bs=bs, do_qg=do_qg):
                    m.w(v, "act")
                    _mark(m, "qk_eB", "act")
                    if tg == 0:
                        _gate(m, v, f"qk_stg{co % 2}")
                    m.i(v.scalar_tensor_tensor(bs[:, tg * NG:(tg + 1) * NG], eC[:], 0.0, eA[:],
                                               ALU.max, ALU.add), "dve")
                    _mark(m, "qk_eC", "dve")
                    _mark(m, "qk_eA", "dve")
                    if do_qg:
                        m.w(v, "dve")
                        if tg == 0:
                            m.i(v.reduce_sum(qg[:, co:co + 1], bs[:, 0:NG], axis=AX.X), "dve")
                        else:
                            m.i(v.reduce_sum(qgt[:], bs[:, tg * NG:(tg + 1) * NG], axis=AX.X), "dve")
                            m.w(v, "dve")
                            m.i(v.tensor_add(qg[:, co:co + 1], qg[:, co:co + 1], qgt[:]), "dve")
                blk.vector(b6)

            def bst(q, co=co, bs=bs, outdram=outdram):
                m.w(q, "dve")
                m.dma("qsy", q, out=m.dram[outdram][co * 128:(co + 1) * 128, :], in_=bs[:])
                _mark(m, f"qk_stg{co % 2}", "qsy")
            blk.sync(bst)

            if outdram == "kkfm":
                for tt in range(TT):
                    def btr(t, co=co, tt=tt, bs=bs):
                        m.w(t, "dve")
                        _gate(m, t, f"qk_pT{tt % 2}")
                        m.i(t.transpose(pT[tt % 2][:], bs[:, tt * 128:(tt + 1) * 128],
                                        m.sbc["ident"][:]), "pe")
                    blk.tensor(btr)

                    def bev(v, co=co, tt=tt):
                        m.w(v, "pe")
                        _gate(m, v, "qk_tstg")
                        m.i(v.tensor_copy(tstg[:], pT[tt % 2][:]), "dve")
                        _mark(m, f"qk_pT{tt % 2}", "dve")
                    blk.vector(bev)

                    def bsv(q, co=co, tt=tt):
                        m.w(q, "dve")
                        m.dma("qsy", q, out=m.dram["kktm"][tt * 128:(tt + 1) * 128,
                                                          co * 128:(co + 1) * 128], in_=tstg[:])
                        _mark(m, "qk_tstg", "qsy")
                    blk.sync(bsv)
                _mark(m, f"qk_stg{co % 2}", "pe")

    wload(blk, "wq0", D)
    proj_rope("qkfm", DC, D, True)
    wload(blk, "wk0", KVD)
    proj_rope("kkfm", KVC, KVD, False)
    wload(blk, "wv0", KVD)
    for tt in range(TT):
        i = tt % 2

        def bv1(t, tt=tt, i=i):
            m.w(t, "qgp"); m.w(t, "qsy")
            _gate(m, t, f"qk_pA{i}")
            for cc in range(DC):
                inst = t.matmul(pA[i][:, 0:KVD], xn[:, cc * T + tt * 128: cc * T + tt * 128 + 128],
                                wsb[:, cc * KVD:(cc + 1) * KVD],
                                start=(cc == 0), stop=(cc == DC - 1))
            m.i(inst, "pe")
        blk.tensor(bv1)

        def bv2(v, tt=tt, i=i):
            m.w(v, "pe")
            _gate(m, v, "qk_vev")
            m.i(v.tensor_copy(vev[:], pA[i][:, 0:KVD]), "dve")
            _mark(m, f"qk_pA{i}", "dve")
        blk.vector(bv2)

        def bv3(q, tt=tt):
            m.w(q, "dve")
            m.dma("qsy", q, out=m.dram["vtm0"][tt * 128:(tt + 1) * 128, :], in_=vev[:])
            _mark(m, "qk_vev", "qsy")
        blk.sync(bv3)

    def bqg(q):
        m.w(q, "dve")
        m.dma("qsy", q, out=m.dram["qgp"][:], in_=qg[:])
    blk.sync(bqg)


PHASES["qkv0"] = dict(statics=["wq0", "wk0", "wv0", "cosfm", "sinfm"],
                      reads=["xnfm0"], writes=["qkfm", "kkfm", "kktm", "vtm0", "qgp"],
                      name="qkv0", emit=emit_qkv0)



def emit_logits(m, blk, ps):
    nc, c = m.nc, m.c
    DC, T, NG, NTG = c["DC"], c["T"], c["NG"], c["NTG"]
    NH0, KVC, GR, S = c["NH0"], c["KVC"], c["GR"], c["S"]
    qgt = ps.enter_context(nc.sbuf_tensor("lg_qgt", [128, DC], F32))
    qg2 = ps.enter_context(nc.sbuf_tensor("lg_qg2", [128, DC], F32))
    bsb = ps.enter_context(nc.sbuf_tensor("lg_b", [128, KVC * NH0], BF16))
    kk = ps.enter_context(nc.sbuf_tensor("lg_kk", [128, KVC * T], BF16))
    lg = ps.enter_context(nc.sbuf_tensor("lg_lg", [NH0, T], F32))
    mc = ps.enter_context(nc.sbuf_tensor("lg_mc", [NH0, 1], F32))
    pL = ps.enter_context(nc.psum_tensor("lg_pL", [NH0, NG], F32))

    def pre(q):
        m.dma("qsy", q, out=qgt[:], in_=m.dram["c1out"][0:128, :])
        m.dma("qsy", q, out=qg2[:], in_=m.dram["c1out"][128:256, :])
        m.dma("qsy", q, out=sb3(kk[:], KVC), in_=fm_ap(m.dram["kkfm"][:]))
    blk.sync(pre)

    def v1(v):
        m.w(v, "qsy")
        m.i(v.tensor_add(qgt[:], qgt[:], qg2[:]), "dve")
        m.w(v, "dve")
        m.i(v.tensor_scalar_mul(qgt[:], qgt[:], 1.0 / S), "dve")
        m.w(v, "dve")
        m.i(v.memset(bsb[:], 0.0), "dve")
        for n in range(NH0):
            m.w(v, "dve")
            cb = (n // GR) // 2
            base = 64 * ((n // GR) % 2)
            m.i(v.tensor_copy(bsb[base:base + 64, cb * NH0 + n: cb * NH0 + n + 1],
                              qgt[64 * (n % 2):64 * (n % 2) + 64, n // 2:n // 2 + 1]), "dve")
    blk.vector(v1)

    for tg in range(NTG):
        def t1(t, tg=tg):
            m.w(t, "dve")
            _gate(m, t, "lg_pL")
            for cb in range(KVC):
                inst = t.matmul(pL[:], bsb[:, cb * NH0:(cb + 1) * NH0],
                                kk[:, cb * T + tg * NG: cb * T + (tg + 1) * NG],
                                start=(cb == 0), stop=(cb == KVC - 1))
            m.i(inst, "pe")
        blk.tensor(t1)

        def v2(v, tg=tg):
            m.w(v, "pe")
            m.i(v.tensor_copy(lg[:, tg * NG:(tg + 1) * NG], pL[:]), "dve")
            _mark(m, "lg_pL", "dve")
        blk.vector(v2)

    def v3(v):
        m.w(v, "dve")
        m.i(v.reduce_max(mc[:], lg[:], axis=AX.X), "dve")
    blk.vector(v3)

    def st(q):
        m.w(q, "dve")
        m.dma("qsy", q, out=m.dram["lhm"][:], in_=lg[:])
        m.dma("qsy", q, out=m.dram["mcp"][:], in_=mc[:])
    blk.sync(st)


PHASES["logits"] = dict(statics=[], reads=["c1out", "kkfm"], writes=["lhm", "mcp"],
                        name="logits", emit=emit_logits)


def emit_zrow(m, blk, ps):
    nc, c = m.nc, m.c
    NH0, T = c["NH0"], c["T"]
    ma = ps.enter_context(nc.sbuf_tensor("zr_ma", [NH0, 1], F32))
    mb = ps.enter_context(nc.sbuf_tensor("zr_mb", [NH0, 1], F32))
    lg = ps.enter_context(nc.sbuf_tensor("zr_lg", [NH0, T], F32))
    ed = ps.enter_context(nc.sbuf_tensor("zr_ed", [NH0, T], F32))
    z = ps.enter_context(nc.sbuf_tensor("zr_z", [NH0, 1], F32))

    def pre(q):
        m.dma("qsy", q, out=ma[:], in_=m.dram["c2out"][0:NH0, :])
        m.dma("qsy", q, out=mb[:], in_=m.dram["c2out"][NH0:2 * NH0, :])
        m.dma("qsy", q, out=lg[:], in_=m.dram["lhm"][:])
    blk.sync(pre)

    def v1(v):
        m.w(v, "qsy")
        m.i(v.tensor_max(ma[:], ma[:], mb[:]), "dve")
        m.w(v, "dve")
        m.i(v.tensor_scalar_mul(mb[:], ma[:], -1.0), "dve")
    blk.vector(v1)

    def s1(s):
        m.w(s, "dve")
        m.i(s.activation(ed[:], lg[:], AF.Exp, bias=mb[:], accum_out=z[:]), "act")
    blk.scalar(s1)

    def st(q):
        m.w(q, "act")
        m.dma("qsy", q, out=m.dram["zp"][:], in_=z[:])
    blk.sync(st)


PHASES["zrow"] = dict(statics=[], reads=["c2out", "lhm"], writes=["zp"],
                      name="zrow", emit=emit_zrow)


def emit_uacc(m, blk, ps):
    nc, c = m.nc, m.c
    NH0, T, TT, KVC, KVD, GR, S = c["NH0"], c["T"], c["TT"], c["KVC"], c["KVD"], c["GR"], c["S"]
    ma = ps.enter_context(nc.sbuf_tensor("ua_ma", [NH0, 1], F32))
    mb = ps.enter_context(nc.sbuf_tensor("ua_mb", [NH0, 1], F32))
    za = ps.enter_context(nc.sbuf_tensor("ua_za", [NH0, 1], F32))
    zb = ps.enter_context(nc.sbuf_tensor("ua_zb", [NH0, 1], F32))
    cv = ps.enter_context(nc.sbuf_tensor("ua_cv", [NH0, 1], F32))
    lg = ps.enter_context(nc.sbuf_tensor("ua_lg", [NH0, T], F32))
    ed = ps.enter_context(nc.sbuf_tensor("ua_ed", [NH0, T], F32))
    ebf = ps.enter_context(nc.sbuf_tensor("ua_ebf", [NH0, T], BF16))
    etm = ps.enter_context(nc.sbuf_tensor("ua_etm", [128, TT * NH0], F32))
    kt = ps.enter_context(nc.sbuf_tensor("ua_kt", [128, TT * KVD], BF16))
    vt = ps.enter_context(nc.sbuf_tensor("ua_vt", [128, TT * KVD], BF16))
    skb = [ps.enter_context(nc.sbuf_tensor(f"ua_skb{i}", [128, 64], BF16)) for i in range(2)]
    usb = ps.enter_context(nc.sbuf_tensor("ua_usb", [64, NH0 * 64], F32))
    pE = [ps.enter_context(nc.psum_tensor(f"ua_pE{i}", [128, NH0], BF16)) for i in range(2)]
    pU = ps.enter_context(nc.psum_tensor("ua_pU", [64, NH0 * 64], F32))

    def pre(q):
        m.dma("qsy", q, out=ma[:], in_=m.dram["c2out"][0:NH0, :])
        m.dma("qsy", q, out=mb[:], in_=m.dram["c2out"][NH0:2 * NH0, :])
        m.dma("qsy", q, out=za[:], in_=m.dram["c3aout"][0:NH0, :])
        m.dma("qsy", q, out=zb[:], in_=m.dram["c3aout"][NH0:2 * NH0, :])
        m.dma("qsy", q, out=lg[:], in_=m.dram["lhm"][:])
        m.dma("qsy", q, out=sb3(kt[:], TT), in_=m.dram["kktm"][:].rearrange("(t p) n -> p t n", p=128))
        m.dma("qsy", q, out=sb3(vt[:], TT), in_=m.dram["vtm0"][:].rearrange("(t p) n -> p t n", p=128))
    blk.sync(pre)

    def v1(v):
        m.w(v, "qsy")
        m.i(v.tensor_max(ma[:], ma[:], mb[:]), "dve")
        m.w(v, "dve")
        m.i(v.tensor_scalar_mul(mb[:], ma[:], -1.0), "dve")
        m.w(v, "dve")
        m.i(v.tensor_add(za[:], za[:], zb[:]), "dve")
        m.w(v, "dve")
        m.i(v.reciprocal(cv[:], za[:]), "dve")
        m.w(v, "dve")
        m.i(v.tensor_scalar_mul(cv[:], cv[:], float(S)), "dve")
    blk.vector(v1)

    def s1(s):
        m.w(s, "dve")
        m.i(s.activation(ed[:], lg[:], AF.Exp, bias=mb[:]), "act")
    blk.scalar(s1)

    def v2(v):
        m.w(v, "act")
        m.i(v.tensor_single_scalar(ebf[:], ed[:], cv[:], ALU.mult), "dve")
    blk.vector(v2)

    for tt in range(TT):
        def t1(t, tt=tt):
            m.w(t, "dve")
            _gate(m, t, f"ua_pE{tt % 2}")
            m.i(t.transpose(pE[tt % 2][:, 0:NH0], ebf[:, tt * 128:(tt + 1) * 128],
                            m.sbc["ident"][0:NH0, 0:NH0]), "pe")
        blk.tensor(t1)

        def v3(v, tt=tt):
            m.w(v, "pe")
            m.i(v.tensor_copy(etm[:, tt * NH0:(tt + 1) * NH0], pE[tt % 2][:, 0:NH0]), "dve")
            _mark(m, f"ua_pE{tt % 2}", "dve")
        blk.vector(v3)

    for n in range(NH0):
        pe_base = m.counts["pe"]
        dtoks = []

        def vb(v, n=n, pe_base=pe_base):
            kv = n // GR
            for tc in range(TT):
                if pe_base + tc - 1 > 0:
                    m.w(v, "pe", pe_base + tc - 1)
                m.w(v, "dve")
                dtoks.append(m.i(v.tensor_single_scalar(
                    skb[tc % 2][:], kt[:, tc * KVD + 64 * kv: tc * KVD + 64 * kv + 64],
                    etm[:, tc * NH0 + n: tc * NH0 + n + 1], ALU.mult), "dve"))
        blk.vector(vb)

        def tb(t, n=n):
            kv = n // GR
            for tc in range(TT):
                m.w(t, "dve", dtoks[tc])
                m.i(t.matmul(pU[:, n * 64:(n + 1) * 64], skb[tc % 2][:],
                             vt[:, tc * KVD + 64 * kv: tc * KVD + 64 * kv + 64],
                             start=(tc == 0), stop=(tc == TT - 1)), "pe")
        blk.tensor(tb)

    def v4(v):
        m.w(v, "pe")
        m.i(v.tensor_copy(usb[:], pU[:]), "dve")
    blk.vector(v4)

    def st(q):
        m.w(q, "dve")
        m.dma("qsy", q, out=m.dram["up"][:], in_=usb[:])
    blk.sync(st)


PHASES["uacc"] = dict(statics=[], reads=["c2out", "c3aout", "lhm", "kktm", "vtm0"],
                      writes=["up"], name="uacc", emit=emit_uacc)



def emit_ffn0a(m, blk, ps):
    nc, c = m.nc, m.c
    D, DC, T, NG, NTG, IC = c["D"], c["DC"], c["T"], c["NG"], c["NTG"], c["IC"]
    xn = ps.enter_context(nc.sbuf_tensor("fa_xn", [128, DC * T], BF16))
    sgs = ps.enter_context(nc.sbuf_tensor("fa_sgs", [128, IC], F32))
    sus = ps.enter_context(nc.sbuf_tensor("fa_sus", [128, IC], F32))
    wgj = [ps.enter_context(nc.sbuf_tensor(f"fa_wg{i}", [128, DC * 128], BF16)) for i in range(2)]
    wuj = [ps.enter_context(nc.sbuf_tensor(f"fa_wu{i}", [128, DC * 128], BF16)) for i in range(2)]
    ga = ps.enter_context(nc.sbuf_tensor("fa_ga", [128, T], F32))
    fu = ps.enter_context(nc.sbuf_tensor("fa_fu", [128, T], BF16))
    pG = ps.enter_context(nc.psum_tensor("fa_pG", [128, T], F32))
    pU = ps.enter_context(nc.psum_tensor("fa_pU", [128, T], F32))

    def pre(q):
        m.dma("qsy", q, out=sb3(xn[:], DC), in_=fm_ap(m.dram["xnfm0"][:]))
        m.dma("qsy", q, out=sgs[:], in_=m.dram["sgs"][:])
        m.dma("qsy", q, out=sus[:], in_=m.dram["sus"][:])
    blk.sync(pre)

    for j in range(IC):
        i = j % 2

        def ld(g, j=j, i=i):
            _gate(m, g, f"fa_wg{i}")
            m.dma("qgp", g, out=sb3(wgj[i][:], DC), in_=fm_ap(m.dram["wgt"][j]),
                  fence=False, stream=i)
            t = m.dma("qgp", g, out=sb3(wuj[i][:], DC), in_=fm_ap(m.dram["wut"][j]),
                      fence=False, stream=i)
            _markt(m, f"fa_w{i}_rdy", t)
        blk.gpsimd(ld)

        def tb(t, j=j, i=i):
            if j == 0:
                m.w(t, "qsy")
            _gate(m, t, f"fa_w{i}_rdy")
            _gate(m, t, "fa_pG")
            for cc in range(DC):
                for tg in range(NTG):
                    t.matmul(pG[:, tg * NG:(tg + 1) * NG],
                             wgj[i][:, cc * 128:(cc + 1) * 128],
                             xn[:, cc * T + tg * NG: cc * T + (tg + 1) * NG],
                             start=(cc == 0), stop=(cc == DC - 1))
                for tg in range(NTG):
                    inst = t.matmul(pU[:, tg * NG:(tg + 1) * NG],
                                    wuj[i][:, cc * 128:(cc + 1) * 128],
                                    xn[:, cc * T + tg * NG: cc * T + (tg + 1) * NG],
                                    start=(cc == 0), stop=(cc == DC - 1))
            m.i(inst, "pe")
            _mark(m, f"fa_wg{i}", "pe")
        blk.tensor(tb)

        tokM0 = ("pe", m.counts["pe"])

        def sb_(s, j=j, tokM0=None):
            m.wt(s, tokM0 or ("pe", m.counts["pe"]))
            _gate(m, s, "fa_ga")
            m.i(s.activation(ga[:], pG[:], AF.Silu, scale=sgs[:, j:j + 1]), "act")
        blk.scalar(sb_)

        def vb(v, j=j):
            m.w(v, "act")
            _gate(m, v, "fa_fu")
            m.i(v.scalar_tensor_tensor(fu[:], pU[:], sus[:, j:j + 1], ga[:],
                                       ALU.mult, ALU.mult), "dve")
            _mark(m, "fa_pG", "dve")
            _mark(m, "fa_ga", "dve")
        blk.vector(vb)

        def st(q, j=j):
            m.w(q, "dve")
            t = m.dma("qsy", q, out=m.dram["fuse"][j * 128:(j + 1) * 128, :], in_=fu[:],
                      fence=False, stream=j % 2)
            _markt(m, "fa_fu", t)
        blk.sync(st)


PHASES["ffn0a"] = dict(statics=["wgt", "wut", "sgs", "sus"], reads=["xnfm0"],
                       writes=["fuse"], name="ffn0a", emit=emit_ffn0a)


def emit_ffn0b(m, blk, ps):
    nc, c = m.nc, m.c
    D, DC, T, NG, NTG, IC = c["D"], c["DC"], c["T"], c["NG"], c["NTG"], c["IC"]
    NSUB = NG // 128
    fus = ps.enter_context(nc.sbuf_tensor("fb_fu", [128, IC * NG], BF16))
    sds = ps.enter_context(nc.sbuf_tensor("fb_sds", [128, DC], F32))
    wdj = [ps.enter_context(nc.sbuf_tensor(f"fb_wd{i}", [128, IC * 128], BF16)) for i in range(2)]
    g0f = ps.enter_context(nc.sbuf_tensor("fb_g0f", [128, NG], F32))
    g0s = ps.enter_context(nc.sbuf_tensor("fb_g0s", [128, NSUB * D], F32))
    pC = [ps.enter_context(nc.psum_tensor(f"fb_pC{i}", [128, NG], F32)) for i in range(2)]
    pT = [ps.enter_context(nc.psum_tensor(f"fb_pT{i}", [128, 128], F32)) for i in range(2)]

    def pre(q):
        m.dma("qsy", q, out=sds[:], in_=m.dram["sds"][:])
    blk.sync(pre)

    for tgrp in range(NTG):
        def lf(q, tgrp=tgrp):
            _gate(m, q, "fb_fu")
            t = m.dma("qsy", q, out=sb3(fus[:], IC),
                      in_=fm_ap(m.dram["fuse"][:, tgrp * NG:(tgrp + 1) * NG]),
                      fence=False, stream=3)
            _markt(m, "fb_fu_rdy", t)
        blk.sync(lf)

        for j in range(DC):
            i = j % 2

            def ld(g, j=j, i=i):
                _gate(m, g, f"fb_wd{i}")
                t = m.dma("qgp", g, out=sb3(wdj[i][:], IC), in_=fm_ap(m.dram["wdt"][j]),
                          fence=False, stream=4 + i)
                _markt(m, f"fb_wd{i}_rdy", t)
            blk.gpsimd(ld)

            def tb(t, j=j, i=i, tgrp=tgrp):
                _gate(m, t, f"fb_wd{i}_rdy")
                _gate(m, t, "fb_fu_rdy")
                _gate(m, t, f"fb_pC{i}")
                for cc in range(IC):
                    inst = t.matmul(pC[i][:], wdj[i][:, cc * 128:(cc + 1) * 128],
                                    fus[:, cc * NG:(cc + 1) * NG],
                                    start=(cc == 0), stop=(cc == IC - 1))
                m.i(inst, "pe")
                _mark(m, f"fb_wd{i}", "pe")
                if j == DC - 1:
                    _mark(m, "fb_fu", "pe")
            blk.tensor(tb)

            tokC = ("pe", m.counts["pe"])

            def sb_(s, j=j, i=i, tokC=tokC):
                m.wt(s, tokC)
                _gate(m, s, "fb_g0f")
                m.i(s.activation(g0f[:], pC[i][:], AF.Silu, scale=sds[:, j:j + 1]), "act")
            blk.scalar(sb_)

            def vr(v, j=j, i=i):
                m.w(v, "act")
                m.i(v.engine_nop(), "dve")
                _mark(m, f"fb_pC{i}", "dve")
            blk.vector(vr)

            for ts in range(NSUB):
                def tt_(t, j=j, ts=ts):
                    m.w(t, "dve")
                    _gate(m, t, f"fb_pT{ts % 2}")
                    m.i(t.transpose(pT[ts % 2][:], g0f[:, ts * 128:(ts + 1) * 128],
                                    m.sbc["identf"][:]), "pe")
                blk.tensor(tt_)

                def ve(v, j=j, ts=ts):
                    m.w(v, "pe")
                    if j == 0 and ts == 0:
                        _gate(m, v, "fb_g0s")
                    m.i(v.tensor_copy(g0s[:, ts * D + j * 128: ts * D + (j + 1) * 128],
                                      pT[ts % 2][:]), "dve")
                    _mark(m, f"fb_pT{ts % 2}", "dve")
                    if ts == NSUB - 1:
                        _mark(m, "fb_g0f", "dve")
                blk.vector(ve)
        def st(q, tgrp=tgrp):
            m.w(q, "dve")
            for ts in range(NSUB):
                t = m.dma("qsy", q, out=m.dram["g0tm"][tgrp * NG + ts * 128: tgrp * NG + (ts + 1) * 128, :],
                          in_=g0s[:, ts * D:(ts + 1) * D], fence=False, stream=6)
            _markt(m, "fb_g0s", t)
        blk.sync(st)


PHASES["ffn0b"] = dict(statics=["wdt", "sds"], reads=["fuse"], writes=["g0tm"],
                       name="ffn0b", emit=emit_ffn0b)



def emit_ctx0(m, blk, ps):
    nc, c = m.nc, m.c
    D, DC, T, NG, NTG, NH0 = c["D"], c["DC"], c["T"], c["NG"], c["NTG"], c["NH0"]
    ua = ps.enter_context(nc.sbuf_tensor("cx_ua", [64, NH0 * 64], F32))
    ub = ps.enter_context(nc.sbuf_tensor("cx_ub", [64, NH0 * 64], F32))
    bd = ps.enter_context(nc.sbuf_tensor("cx_bd", [128, DC * 128], BF16))
    xn = ps.enter_context(nc.sbuf_tensor("cx_xn", [128, DC * T], BF16))
    wp = ps.enter_context(nc.sbuf_tensor("cx_wp", [128, DC * D], BF16))
    qkg = ps.enter_context(nc.sbuf_tensor("cx_qkg", [128, DC * NG], BF16))
    xph = ps.enter_context(nc.sbuf_tensor("cx_xph", [128, NG], F32))
    ctxp = ps.enter_context(nc.sbuf_tensor("cx_ctxp", [128, NG], BF16))
    pA = [ps.enter_context(nc.psum_tensor(f"cx_pA{i}", [128, NG], F32)) for i in range(2)]
    pB = [ps.enter_context(nc.psum_tensor(f"cx_pB{i}", [128, NG], F32)) for i in range(2)]

    def pre(q):
        m.dma("qsy", q, out=ua[:], in_=m.dram["c3bout"][0:64, :])
        m.dma("qsy", q, out=ub[:], in_=m.dram["c3bout"][64:128, :])
        m.dma("qsy", q, out=sb3(xn[:], DC), in_=fm_ap(m.dram["xnfm0"][:]))
    blk.sync(pre)

    def wl(g):
        m.dma("qgp", g, out=sb3(wp[:], DC), in_=fm_ap(m.dram["wphi0"][:]))
    blk.gpsimd(wl)

    def v1(v):
        m.w(v, "qsy")
        m.i(v.tensor_add(ua[:], ua[:], ub[:]), "dve")
        m.w(v, "dve")
        m.i(v.memset(bd[:], 0.0), "dve")
        for cb in range(DC):
            m.w(v, "dve")
            m.i(v.tensor_copy(bd[0:64, cb * 128: cb * 128 + 64],
                              ua[:, (2 * cb) * 64:(2 * cb + 1) * 64]), "dve")
            m.w(v, "dve")
            m.i(v.tensor_copy(bd[64:128, cb * 128 + 64: cb * 128 + 128],
                              ua[:, (2 * cb + 1) * 64:(2 * cb + 2) * 64]), "dve")
    blk.vector(v1)

    for tgrp in range(NTG):
        def lq(q, tgrp=tgrp):
            m.w(q, "dve")
            _gate(m, q, "cx_qkg")
            m.dma("qsy", q, out=sb3(qkg[:], DC),
                  in_=fm_ap(m.dram["qkfm"][:, tgrp * NG:(tgrp + 1) * NG]))
        blk.sync(lq)

        for co in range(DC):
            i = co % 2

            def tb(t, tgrp=tgrp, co=co, i=i):
                m.w(t, "qsy"); m.w(t, "qgp")
                _gate(m, t, f"cx_pA{i}")
                for cc in range(DC):
                    inst = t.matmul(pA[i][:], wp[:, cc * D + co * 128: cc * D + co * 128 + 128],
                                    xn[:, cc * T + tgrp * NG: cc * T + (tgrp + 1) * NG],
                                    start=(cc == 0), stop=(cc == DC - 1))
                m.i(inst, "pe")
                _gate(m, t, f"cx_pB{i}")
                m.i(t.matmul(pB[i][:], bd[:, co * 128:(co + 1) * 128],
                             qkg[:, co * NG:(co + 1) * NG], start=True, stop=True), "pe")
                if co == DC - 1:
                    _mark(m, "cx_qkg", "pe")
            blk.tensor(tb)

            def sb_(s, co=co, i=i):
                m.w(s, "pe")
                _gate(m, s, "cx_xph")
                m.i(s.copy(xph[:], pA[i][:]), "act")
            blk.scalar(sb_)

            def vb(v, co=co, i=i):
                m.w(v, "act")
                _gate(m, v, "cx_ctxp")
                m.i(v.tensor_tensor(ctxp[:], pB[i][:], xph[:], ALU.mult), "dve")
                _mark(m, f"cx_pA{i}", "dve")
                _mark(m, f"cx_pB{i}", "dve")
                _mark(m, "cx_xph", "dve")
            blk.vector(vb)

            def st(q, tgrp=tgrp, co=co):
                m.w(q, "dve")
                m.dma("qsy", q, out=m.dram["ctx"][co * 128:(co + 1) * 128,
                                                  tgrp * NG:(tgrp + 1) * NG], in_=ctxp[:])
                _mark(m, "cx_ctxp", "qsy")
            blk.sync(st)


PHASES["ctx0"] = dict(statics=["wphi0"], reads=["c3bout", "xnfm0", "qkfm"],
                      writes=["ctx"], name="ctx0", emit=emit_ctx0)


def emit_out0(m, blk, ps):
    nc, c = m.nc, m.c
    D, DC, T, NG, TT = c["D"], c["DC"], c["T"], c["NG"], c["TT"]
    wo = ps.enter_context(nc.sbuf_tensor("o0_wo", [128, DC * D], BF16))
    ctxt = ps.enter_context(nc.sbuf_tensor("o0_ctxt", [128, DC * 128], BF16))
    xt = ps.enter_context(nc.sbuf_tensor("o0_xt", [128, D], F32))
    g0t = ps.enter_context(nc.sbuf_tensor("o0_g0t", [128, D], F32))
    t1 = ps.enter_context(nc.sbuf_tensor("o0_t1", [128, D], F32))
    x1t = ps.enter_context(nc.sbuf_tensor("o0_x1t", [128, D], F32))
    pD = ps.enter_context(nc.psum_tensor("o0_pD", [128, min(2048, D)], F32))

    def wl(g):
        m.dma("qgp", g, out=sb3(wo[:], DC), in_=fm_ap(m.dram["wo0"][:]))
    blk.gpsimd(wl)

    for tt in range(TT):
        def lt(q, tt=tt):
            _gate(m, q, "o0_ctxt")
            m.dma("qsy", q, out=sb3(ctxt[:], DC),
                  in_=fm_ap(m.dram["ctx"][:, tt * 128:(tt + 1) * 128]))
            _gate(m, q, "o0_xt")
            m.dma("qsy", q, out=xt[:], in_=m.dram["x_in"][tt * 128:(tt + 1) * 128, :])
            _gate(m, q, "o0_g0t")
            m.dma("qsy", q, out=g0t[:], in_=m.dram["g0tm"][tt * 128:(tt + 1) * 128, :])
        blk.sync(lt)

        def tb(t, tt=tt):
            m.w(t, "qsy"); m.w(t, "qgp")
            _gate(m, t, "o0_pD")
            for dg in range(D // NG):
                for cc in range(DC):
                    inst = t.matmul(pD[:, dg * NG:(dg + 1) * NG],
                                    ctxt[:, cc * 128:(cc + 1) * 128],
                                    wo[:, cc * D + dg * NG: cc * D + (dg + 1) * NG],
                                    start=(cc == 0), stop=(cc == DC - 1))
            m.i(inst, "pe")
            _mark(m, "o0_ctxt", "pe")
        blk.tensor(tb)

        def vb(v, tt=tt):
            m.w(v, "pe")
            _gate(m, v, "o0_t1")
            m.i(v.tensor_tensor(t1[:], pD[:], g0t[:], ALU.mult), "dve")
            _mark(m, "o0_pD", "dve")
            _mark(m, "o0_g0t", "dve")
            m.w(v, "dve")
            _gate(m, v, "o0_x1t")
            m.i(v.tensor_add(x1t[:], xt[:], t1[:]), "dve")
            _mark(m, "o0_t1", "dve")
            _mark(m, "o0_xt", "dve")
        blk.vector(vb)

        def st(q, tt=tt):
            m.w(q, "dve")
            m.dma("qsy", q, out=m.dram["x1"][tt * 128:(tt + 1) * 128, :], in_=x1t[:])
            _mark(m, "o0_x1t", "qsy")
        blk.sync(st)


PHASES["out0"] = dict(statics=["wo0", "x_in"], reads=["ctx", "g0tm"],
                      writes=["x1"], name="out0", emit=emit_out0)



def emit_tern(m, blk, ps):
    nc, c = m.nc, m.c
    D, I, DC, IC, JG, DG = c["D"], c["INTER"], c["DC"], c["IC"], c["JG"], c["DG"]
    W = max(D, I)
    slab = [ps.enter_context(nc.sbuf_tensor(f"tn_sl{i}", [128, W], F32)) for i in range(2)]
    t8 = ps.enter_context(nc.sbuf_tensor("tn_t8", [128, W], I8))
    t8b = ps.enter_context(nc.sbuf_tensor("tn_t8b", [128, W], I8))
    btmp = ps.enter_context(nc.sbuf_tensor("tn_bt", [128, W], F32))
    btmpb = ps.enter_context(nc.sbuf_tensor("tn_btb", [128, W], F32))
    acc = ps.enter_context(nc.sbuf_tensor("tn_acc", [128, max(DC, IC)], F32))
    accs = ps.enter_context(nc.sbuf_tensor("tn_accs", [128, 1], F32))
    sm = ps.enter_context(nc.sbuf_tensor("tn_sm", [1, 8], F32))
    tauv = ps.enter_context(nc.sbuf_tensor("tn_tauv", [128, 1], F32))
    ntv = ps.enter_context(nc.sbuf_tensor("tn_ntv", [128, 1], F32))
    asb = ps.enter_context(nc.sbuf_tensor("tn_asb", [1, 8], F32))
    p1 = ps.enter_context(nc.psum_tensor("tn_p1", [128, 8], F32))

    def zero_asb(v):
        m.i(v.memset(asb[:], 0.0), "dve")
    blk.vector(zero_asb)

    for mi_, (wname, tname, aslot, rows, cols, ntile) in enumerate([
            ("w1T", "t1i", 0, D, I, JG), ("w2T", "t2i", 1, D, I, JG),
            ("w3T", "t3i", 2, I, D, DG)]):
        nsl = rows // 128
        numel = rows * cols

        if "pool" in m.counts:
            def wsync(v):
                m.w(v, "pool")
            blk.vector(wsync)
        for sl in range(nsl):
            i = sl % 2

            def ld(q, sl=sl, i=i, wname=wname, cols=cols):
                _gate(m, q, f"tn_sl{i}")
                t = m.dma("qsy", q, out=slab[i][:, 0:cols],
                          in_=m.dram[wname][sl * 128:(sl + 1) * 128, :],
                          fence=False, stream=i)
                _markt(m, f"tn_sl{i}_rdy", t)
            blk.sync(ld)

            def vr(v, sl=sl, i=i, cols=cols):
                _gate(m, v, f"tn_sl{i}_rdy")
                m.i(v.reduce_sum(acc[:, sl:sl + 1], slab[i][:, 0:cols], axis=AX.X,
                                 apply_absolute_value=True), "dve")
                _mark(m, f"tn_sl{i}", "dve")
            blk.vector(vr)

        def vtot(v, nsl=nsl):
            m.w(v, "dve")
            if "pool" in m.counts:
                m.w(v, "pool")
            m.i(v.reduce_sum(accs[:], acc[:, 0:nsl], axis=AX.X), "dve")
        blk.vector(vtot)

        def ttot(t):
            m.w(t, "dve")
            _gate(m, t, "tn_p1")
            m.i(t.matmul(p1[0:1, 0:1], accs[:], m.sbc["onesf_col"][:],
                         start=True, stop=True), "pe")
        blk.tensor(ttot)

        def vsc(v, numel=numel, aslot=aslot):
            m.w(v, "pe")
            m.i(v.tensor_scalar_mul(sm[0:1, 0:1], p1[0:1, 0:1], 1.0 / numel), "dve")
            _mark(m, "tn_p1", "dve")
            m.w(v, "dve")
            m.i(v.tensor_scalar(sm[0:1, 1:2], sm[0:1, 0:1], T_EPS, ATANH_HALF,
                                ALU.add, ALU.mult), "dve")
        blk.vector(vsc)

        def sa(s):
            m.w(s, "dve")
            m.i(s.activation(sm[0:1, 2:3], sm[0:1, 0:1], AF.Ln,
                             bias=m.sbc["cvals"][0:1, 1:2]), "act")
            m.w(s, "act")
            m.i(s.activation(sm[0:1, 3:4], sm[0:1, 0:1], AF.Ln,
                             bias=m.sbc["cvals"][0:1, 1:2], scale=-1.0), "act")
        blk.scalar(sa)

        def va(v, aslot=aslot):
            m.w(v, "act")
            m.i(v.tensor_sub(sm[0:1, 4:5], sm[0:1, 2:3], sm[0:1, 3:4]), "dve")
            m.w(v, "dve")
            m.i(v.tensor_scalar_mul(asb[0:1, aslot:aslot + 1], sm[0:1, 4:5], 0.5), "dve")
        blk.vector(va)

        def tbc(t):
            m.w(t, "dve")
            _gate(m, t, "tn_p1")
            m.i(t.matmul(p1[:, 0:1], m.sbc["onesf_row"][:], sm[0:1, 1:2],
                         start=True, stop=True), "pe")
        blk.tensor(tbc)

        def vbc(v):
            m.w(v, "pe")
            m.i(v.tensor_copy(tauv[:], p1[:, 0:1]), "dve")
            _mark(m, "tn_p1", "dve")
            m.w(v, "dve")
            m.i(v.tensor_scalar_mul(ntv[:], tauv[:], -1.0), "dve")
        blk.vector(vbc)

        for sl in range(nsl):
            i = sl % 2

            def ld2(q, sl=sl, i=i, wname=wname, cols=cols):
                _gate(m, q, f"tn_sl{i}")
                t = m.dma("qsy", q, out=slab[i][:, 0:cols],
                          in_=m.dram[wname][sl * 128:(sl + 1) * 128, :],
                          fence=False, stream=i)
                _markt(m, f"tn_sl{i}_rdy", t)
            blk.sync(ld2)

            eng_s = "dve"
            t82 = t8 if sl % 2 == 0 else t8b

            def vq(v, sl=sl, i=i, cols=cols, t82=t82):
                _gate(m, v, f"tn_sl{i}_rdy"); m.w(v, "dve")
                m.i(v.tensor_single_scalar(btmp[:, 0:cols], slab[i][:, 0:cols], ntv[:],
                                           ALU.is_le), "dve")
                m.w(v, "dve")
                _gate(m, v, f"tn_t8{sl % 2}")
                m.i(v.scalar_tensor_tensor(t82[:, 0:cols], slab[i][:, 0:cols], tauv[:],
                                           btmp[:, 0:cols], ALU.is_ge, ALU.subtract), "dve")
                _mark(m, f"tn_sl{i}", "dve")
            blk.vector(vq)

            def st(q, sl=sl, tname=tname, ntile=ntile, cols=cols, eng_s=eng_s, t82=t82):
                m.w(q, eng_s)
                t = m.dma("qsy", q, out=m.dram[tname][:, sl * 128:(sl + 1) * 128, :]
                          .rearrange("j p n -> p j n"),
                          in_=sb3(t82[:, 0:cols], ntile), fence=False, stream=2 + sl % 2)
                _markt(m, f"tn_t8{sl % 2}", t)
            blk.sync(st)

    def sta(q):
        m.w(q, "dve")
        m.dma("qsy", q, out=m.dram["terns"][:], in_=asb[:])
    blk.sync(sta)


PHASES["tern"] = dict(statics=["w1T", "w2T", "w3T"], reads=[],
                      writes=["t1i", "t2i", "t3i", "terns"], name="tern", emit=emit_tern)



def emit_quant1(m, blk, ps):
    nc, c = m.nc, m.c
    D, DC, T, TT = c["D"], c["DC"], c["T"], c["TT"]
    x1t = ps.enter_context(nc.sbuf_tensor("q1_x1t", [128, D], F32))
    xn1 = ps.enter_context(nc.sbuf_tensor("q1_xn1", [128, D], F32))
    sq = ps.enter_context(nc.sbuf_tensor("q1_sq", [128, D], F32))
    xr = ps.enter_context(nc.sbuf_tensor("q1_xr", [128, D], F32))
    q8 = ps.enter_context(nc.sbuf_tensor("q1_q8", [128, D], I8))
    qb = ps.enter_context(nc.sbuf_tensor("q1_qb", [128, D], BF16))
    stg = ps.enter_context(nc.sbuf_tensor("q1_stg", [128, DC * 128], BF16))
    stats = ps.enter_context(nc.sbuf_tensor("q1_stats", [128, 2 * TT], F32))
    k12 = ps.enter_context(nc.sbuf_tensor("q1_k12", [128, 2 * TT], F32))
    sm = ps.enter_context(nc.sbuf_tensor("q1_sm", [128, 8], F32))
    grow = ps.enter_context(nc.sbuf_tensor("q1_grow", [1, D], F32))
    g1rep = ps.enter_context(nc.sbuf_tensor("q1_g1rep", [128, D], F32))
    g2rep = ps.enter_context(nc.sbuf_tensor("q1_g2rep", [128, D], F32))
    arow = ps.enter_context(nc.sbuf_tensor("q1_arow", [1, 8], F32))
    avec = ps.enter_context(nc.sbuf_tensor("q1_avec", [128, 2], F32))
    pb = ps.enter_context(nc.psum_tensor("q1_pb", [128, 512], F32))
    pt = [ps.enter_context(nc.psum_tensor(f"q1_pt{i}", [128, 128], BF16)) for i in range(2)]

    for gi, (gname, dst) in enumerate([("g1v", g1rep), ("g2v", g2rep)]):
        def lg(q, gname=gname):
            _gate(m, q, "q1_grow")
            m.dma("qsy", q, out=grow[0:1, :], in_=m.dram[gname][:].rearrange("(a d) -> a d", a=1))
        blk.sync(lg)
        for gg in range(D // 512):
            def tb(t, gg=gg):
                m.w(t, "qsy")
                _gate(m, t, "q1_pb")
                m.i(t.matmul(pb[:], m.sbc["onesf_row"][:], grow[0:1, gg * 512:(gg + 1) * 512],
                             start=True, stop=True), "pe")
                _mark(m, "q1_grow", "pe")
            blk.tensor(tb)

            def vb(v, gg=gg, dst=dst):
                m.w(v, "pe")
                m.i(v.tensor_copy(dst[:, gg * 512:(gg + 1) * 512], pb[:]), "dve")
                _mark(m, "q1_pb", "dve")
            blk.vector(vb)

    def la(q):
        m.dma("qsy", q, out=arow[:], in_=m.dram["terns"][:])
        m.dma("qsy", q, out=stats[:], in_=m.dram["ln1st"][:])
    blk.sync(la)
    for gi in range(2):
        def tb2(t, gi=gi):
            m.w(t, "qsy")
            _gate(m, t, "q1_pb")
            m.i(t.matmul(pb[:, 0:1], m.sbc["onesf_row"][:], arow[0:1, gi:gi + 1],
                         start=True, stop=True), "pe")
        blk.tensor(tb2)

        def vb2(v, gi=gi):
            m.w(v, "pe")
            m.i(v.tensor_copy(avec[:, gi:gi + 1], pb[:, 0:1]), "dve")
            _mark(m, "q1_pb", "dve")
        blk.vector(vb2)

    for tt in range(TT):
        def lx(q, tt=tt):
            _gate(m, q, "q1_x1t")
            m.dma("qsy", q, out=x1t[:], in_=m.dram["x1"][tt * 128:(tt + 1) * 128, :])
        blk.sync(lx)

        def v0(v, tt=tt):
            m.w(v, "qsy")
            _gate(m, v, "q1_xn1")
            m.i(v.tensor_scalar(xn1[:], x1t[:], stats[:, 2 * tt:2 * tt + 1],
                                stats[:, 2 * tt + 1:2 * tt + 2], ALU.subtract, ALU.mult), "dve")
            _mark(m, "q1_x1t", "dve")
        blk.vector(v0)

        for G in (0, 1):
            grep = g1rep if G == 0 else g2rep
            dstfm = "xq1fm" if G == 0 else "xq2fm"

            def s0(s, tt=tt, G=G):
                m.w(s, "dve")
                _gate(m, s, "q1_sq")
                m.i(s.activation(sq[:], xn1[:], AF.Square, accum_out=sm[:, 0:1]), "act")
            blk.scalar(s0)

            def v1(v, tt=tt, G=G):
                m.w(v, "act")
                m.i(v.tensor_scalar(sm[:, 1:2], sm[:, 0:1], 1.0 / D, RMS_EPS,
                                    ALU.mult, ALU.add), "dve")
            blk.vector(v1)

            def s1(s, tt=tt, G=G):
                m.w(s, "dve")
                m.i(s.activation(sm[:, 2:3], sm[:, 1:2], AF.Sqrt), "act")
            blk.scalar(s1)

            def v2(v, tt=tt, G=G, grep=grep):
                m.w(v, "act")
                m.i(v.reciprocal(sm[:, 3:4], sm[:, 2:3]), "dve")
                m.w(v, "dve")
                _gate(m, v, "q1_xr")
                m.i(v.scalar_tensor_tensor(xr[:], xn1[:], sm[:, 3:4], grep[:],
                                           ALU.mult, ALU.mult), "dve")
                m.w(v, "dve")
                m.i(v.reduce_max(sm[:, 4:5], xr[:], axis=AX.X,
                                 apply_absolute_value=True), "dve")
                m.w(v, "dve")
                m.i(v.tensor_scalar_max(sm[:, 4:5], sm[:, 4:5], AQ_CLIP), "dve")
                m.w(v, "dve")
                m.i(v.reciprocal(sm[:, 5:6], sm[:, 4:5]), "dve")
                m.w(v, "dve")
                m.i(v.tensor_scalar_mul(sm[:, 5:6], sm[:, 5:6], 127.0), "dve")
                m.w(v, "dve")
                _gate(m, v, "q1_sq2")
                m.i(v.tensor_scalar(sq[:], xr[:], sm[:, 5:6], MAGIC, ALU.mult, ALU.add), "dve")
                _mark(m, "q1_xr", "dve")
                m.w(v, "dve")
                _gate(m, v, "q1_q8")
                m.i(v.tensor_scalar_sub(q8[:], sq[:], MAGIC), "dve")
                _mark(m, "q1_sq", "dve")
                _mark(m, "q1_sq2", "dve")
                m.w(v, "dve")
                m.i(v.reciprocal(sm[:, 6:7], sm[:, 5:6]), "dve")
                m.w(v, "dve")
                m.i(v.tensor_tensor(k12[:, G * TT + tt:G * TT + tt + 1], sm[:, 6:7],
                                    avec[:, G:G + 1], ALU.mult), "dve")
            blk.vector(v2)

            def s2(s, tt=tt, G=G):
                m.w(s, "dve")
                _gate(m, s, "q1_qb")
                m.i(s.copy(qb[:], q8[:]), "act")
                _mark(m, "q1_q8", "act")
            blk.scalar(s2)

            def vr(v):
                m.w(v, "act")
                m.i(v.engine_nop(), "dve")
            blk.vector(vr)

            for cc in range(DC):
                def tb3(t, cc=cc):
                    m.w(t, "dve")
                    _gate(m, t, f"q1_pt{cc % 2}")
                    m.i(t.transpose(pt[cc % 2][:], qb[:, cc * 128:(cc + 1) * 128],
                                    m.sbc["ident"][:]), "pe")
                blk.tensor(tb3)

                def ve(v, cc=cc):
                    m.w(v, "pe")
                    if cc == 0:
                        _gate(m, v, "q1_stg")
                    m.i(v.tensor_copy(stg[:, cc * 128:(cc + 1) * 128], pt[cc % 2][:]), "dve")
                    _mark(m, f"q1_pt{cc % 2}", "dve")
                    if cc == DC - 1:
                        _mark(m, "q1_qb", "dve")
                blk.vector(ve)

            def st(q, tt=tt, dstfm=dstfm, G=G):
                m.w(q, "dve")
                m.dma("qsy", q, out=fm_ap(m.dram[dstfm][:, tt * 128:(tt + 1) * 128]),
                      in_=sb3(stg[:], DC))
                _mark(m, "q1_stg", "qsy")
                if G == 1:
                    _mark(m, "q1_xn1", "qsy")
            blk.sync(st)

    def stk(q):
        m.w(q, "dve")
        m.dma("qsy", q, out=m.dram["k12"][:], in_=k12[:])
    blk.sync(stk)


PHASES["quant1"] = dict(statics=["g1v", "g2v"], reads=["x1", "ln1st", "terns"],
                        writes=["xq1fm", "xq2fm", "k12"], name="quant1", emit=emit_quant1)



def emit_uh1(m, blk, ps):
    nc, c = m.nc, m.c
    D, DC, T, TT, JG = c["D"], c["DC"], c["T"], c["TT"], c["JG"]
    xq1 = ps.enter_context(nc.sbuf_tensor("uh_xq1", [128, DC * T], BF16))
    xq2 = ps.enter_context(nc.sbuf_tensor("uh_xq2", [128, DC * T], BF16))
    k12 = ps.enter_context(nc.sbuf_tensor("uh_k12", [128, 2 * TT], F32))
    t1s = [ps.enter_context(nc.sbuf_tensor(f"uh_t1{i}", [128, DC * 512], BF16)) for i in range(2)]
    t2s = [ps.enter_context(nc.sbuf_tensor(f"uh_t2{i}", [128, DC * 512], BF16)) for i in range(2)]
    ga = [ps.enter_context(nc.sbuf_tensor("uh_ga0", [128, 512], F32))] * 2
    hb = [ps.enter_context(nc.sbuf_tensor("uh_hb0", [128, 512], F32))] * 2
    pU1 = [ps.enter_context(nc.psum_tensor(f"uh_p1{i}", [128, 512], F32)) for i in range(2)]
    pU2 = [ps.enter_context(nc.psum_tensor(f"uh_p2{i}", [128, 512], F32)) for i in range(2)]

    def pre(q):
        m.dma("qsy", q, out=sb3(xq1[:], DC), in_=fm_ap(m.dram["xq1fm"][:]))
        m.dma("qsy", q, out=sb3(xq2[:], DC), in_=fm_ap(m.dram["xq2fm"][:]))
        m.dma("qsy", q, out=k12[:], in_=m.dram["k12"][:])
    blk.sync(pre)
    pre_tok = ("qsy", m.counts["qsy"])

    for jg in range(JG):
        i = jg % 2

        def ld(g, jg=jg, i=i):
            _gate(m, g, f"uh_t1{i}")
            m.dma("qgp", g, out=sb3(t1s[i][:], DC),
                  in_=m.dram["t1i"][jg].rearrange("(c p) n -> p c n", p=128),
                  fence=False, stream=2 * i)
            t = m.dma("qgp", g, out=sb3(t2s[i][:], DC),
                      in_=m.dram["t2i"][jg].rearrange("(c p) n -> p c n", p=128),
                      fence=False, stream=2 * i)
            _markt(m, f"uh_t{i}_rdy", t)
        blk.gpsimd(ld)

        for tt in range(TT):
            ii = tt % 2

            def tb(t, jg=jg, i=i, tt=tt, ii=ii):
                if jg == 0 and tt == 0:
                    m.wt(t, pre_tok)
                _gate(m, t, f"uh_t{i}_rdy")
                _gate(m, t, f"uh_p1{ii}")
                _gate(m, t, f"uh_p2{ii}")
                for cc in range(DC):
                    t.matmul(pU1[ii][:], xq1[:, cc * T + tt * 128: cc * T + (tt + 1) * 128],
                             t1s[i][:, cc * 512:(cc + 1) * 512],
                             start=(cc == 0), stop=(cc == DC - 1))
                    inst = t.matmul(pU2[ii][:], xq2[:, cc * T + tt * 128: cc * T + (tt + 1) * 128],
                                    t2s[i][:, cc * 512:(cc + 1) * 512],
                                    start=(cc == 0), stop=(cc == DC - 1))
                m.i(inst, "pe")
                if tt == TT - 1:
                    _mark(m, f"uh_t1{i}", "pe")
            blk.tensor(tb)
            tokM = ("pe", m.counts["pe"])

            def sb_(s, jg=jg, tt=tt, ii=ii, tokM=tokM):
                m.wt(s, tokM)
                _gate(m, s, "uh_ga0")
                m.i(s.activation(ga[ii][:], pU1[ii][:], AF.Silu,
                                 scale=k12[:, tt:tt + 1]), "act")
                _mark(m, f"uh_p1{ii}", "act")
            blk.scalar(sb_)
            tokA = ("act", m.counts["act"])

            def vb(v, jg=jg, tt=tt, ii=ii, tokA=tokA):
                m.wt(v, tokA)
                _gate(m, v, "uh_hb0")
                m.i(v.scalar_tensor_tensor(hb[ii][:], pU2[ii][:], k12[:, TT + tt:TT + tt + 1],
                                           ga[ii][:], ALU.mult, ALU.mult), "dve")
                _mark(m, f"uh_p2{ii}", "dve")
                _mark(m, "uh_ga0", "dve")
            blk.vector(vb)
            tokV = ("dve", m.counts["dve"])

            def st(q, jg=jg, tt=tt, ii=ii, tokV=tokV):
                m.wt(q, tokV)
                t = m.dma("qsy", q, out=m.dram["hdr"][tt * 128:(tt + 1) * 128,
                                                      jg * 512:(jg + 1) * 512], in_=hb[ii][:],
                          fence=False, stream=ii)
                _markt(m, "uh_hb0", t)
            blk.sync(st)


PHASES["uh1"] = dict(statics=[], reads=["xq1fm", "xq2fm", "k12", "t1i", "t2i"],
                     writes=["hdr"], name="uh1", emit=emit_uh1)



def emit_hdown(m, blk, ps):
    nc, c = m.nc, m.c
    D, T, TT, I, IC, DG, GSZ = c["D"], c["T"], c["TT"], c["INTER"], c["IC"], c["DG"], c["GSZ"]
    GT = GSZ * 128
    ICH = IC // 2
    hbuf2 = [ps.enter_context(nc.sbuf_tensor(f"hd_h{i}", [128, I], F32)) for i in range(2)]
    q8 = ps.enter_context(nc.sbuf_tensor("hd_q8", [128, I], I8))
    qb = ps.enter_context(nc.sbuf_tensor("hd_qb", [128, I], BF16))
    hq = ps.enter_context(nc.sbuf_tensor("hd_hq", [128, IC * GT], BF16))
    t3h = ps.enter_context(nc.sbuf_tensor("hd_t3h", [128, ICH * 512], BF16))
    gout = ps.enter_context(nc.sbuf_tensor("hd_gout", [128, 512], F32))
    gsc = ps.enter_context(nc.sbuf_tensor("hd_gsc", [128, 512], F32))
    gdn = [ps.enter_context(nc.sbuf_tensor(f"hd_gdn{i}", [128, 512], F32)) for i in range(2)]
    sm = ps.enter_context(nc.sbuf_tensor("hd_sm", [128, 24], F32))
    k3 = ps.enter_context(nc.sbuf_tensor("hd_k3", [128, TT], F32))
    grow = ps.enter_context(nc.sbuf_tensor("hd_grow", [1, 512], F32))
    g3rep = ps.enter_context(nc.sbuf_tensor("hd_g3rep", [128, I], F32))
    arow = ps.enter_context(nc.sbuf_tensor("hd_arow", [1, 8], F32))
    avec = ps.enter_context(nc.sbuf_tensor("hd_avec", [128, 1], F32))
    pb = ps.enter_context(nc.psum_tensor("hd_pb", [128, 512], F32))
    pt = [ps.enter_context(nc.psum_tensor(f"hd_pt{i}", [128, 512], BF16)) for i in range(2)]
    pD = [ps.enter_context(nc.psum_tensor(f"hd_pD{i}", [128, 512], F32)) for i in range(2)]

    for gg in range(I // 512):
        def lg(q, gg=gg):
            _gate(m, q, "hd_grow")
            m.dma("qsy", q, out=grow[0:1, :],
                  in_=m.dram["g3v"][gg * 512:(gg + 1) * 512].rearrange("(a d) -> a d", a=1))
        blk.sync(lg)

        def tb(t, gg=gg):
            m.w(t, "qsy")
            _gate(m, t, "hd_pb")
            m.i(t.matmul(pb[:], m.sbc["onesf_row"][:], grow[0:1, :],
                         start=True, stop=True), "pe")
            _mark(m, "hd_grow", "pe")
        blk.tensor(tb)

        def vb(v, gg=gg):
            m.w(v, "pe")
            m.i(v.tensor_copy(g3rep[:, gg * 512:(gg + 1) * 512], pb[:]), "dve")
            _mark(m, "hd_pb", "dve")
        blk.vector(vb)

    def la(q):
        m.dma("qsy", q, out=arow[:], in_=m.dram["terns"][:])
    blk.sync(la)

    def tb2(t):
        m.w(t, "qsy")
        _gate(m, t, "hd_pb")
        m.i(t.matmul(pb[:, 0:1], m.sbc["onesf_row"][:], arow[0:1, 2:3],
                     start=True, stop=True), "pe")
    blk.tensor(tb2)

    def vb2(v):
        m.w(v, "pe")
        m.i(v.tensor_copy(avec[:], pb[:, 0:1]), "dve")
        _mark(m, "hd_pb", "dve")
    blk.vector(vb2)

    def emit_quant(grp):
        tokQ = {}
        for lt in range(GSZ):
            tt = grp * GSZ + lt
            hbuf = hbuf2[tt % 2]

            def lh(q, tt=tt, hbuf=hbuf):
                _gate(m, q, f"hd_h{tt % 2}")
                t = m.dma("qsy", q, out=hbuf[:], in_=m.dram["hdr"][tt * 128:(tt + 1) * 128, :],
                          fence=False, stream=tt % 2)
                _markt(m, f"hd_h{tt % 2}_rdy", t)
            blk.sync(lh)

            for ch in range(I // 512):
                def sA(s, ch=ch, tt=tt):
                    if ch == 0:
                        _gate(m, s, f"hd_h{tt % 2}_rdy")
                        _gate(m, s, "hd_smA")
                    m.i(s.activation(gsc[:], hbuf[:, ch * 512:(ch + 1) * 512], AF.Square,
                                     accum_out=sm[:, ch:ch + 1]), "act")
                blk.scalar(sA)

            for ch in range(I // 512):
                def vA(v, ch=ch, tt=tt):
                    if ch == 0:
                        _gate(m, v, f"hd_h{tt % 2}_rdy")
                    m.i(v.tensor_tensor(gout[:], hbuf[:, ch * 512:(ch + 1) * 512],
                                        g3rep[:, ch * 512:(ch + 1) * 512], ALU.mult), "dve")
                    m.w(v, "dve")
                    m.i(v.reduce_max(sm[:, 18:19], gout[:], axis=AX.X,
                                     apply_absolute_value=True), "dve")
                    m.w(v, "dve")
                    if ch == 0:
                        m.i(v.tensor_copy(sm[:, 17:18], sm[:, 18:19]), "dve")
                    else:
                        m.i(v.tensor_max(sm[:, 17:18], sm[:, 17:18], sm[:, 18:19]), "dve")
                blk.vector(vA)

            def v1(v, tt=tt):
                m.w(v, "act")
                m.i(v.reduce_sum(sm[:, 16:17], sm[:, 0:I // 512], axis=AX.X), "dve")
                m.w(v, "dve")
                m.i(v.tensor_scalar(sm[:, 19:20], sm[:, 16:17], 1.0 / I, RMS_EPS,
                                    ALU.mult, ALU.add), "dve")
                _mark(m, "hd_smA", "dve")
            blk.vector(v1)

            def s1(s, tt=tt):
                m.w(s, "dve")
                m.i(s.activation(sm[:, 20:21], sm[:, 19:20], AF.Sqrt), "act")
            blk.scalar(s1)

            def v2(v, tt=tt):
                m.w(v, "act")
                m.i(v.reciprocal(sm[:, 19:20], sm[:, 20:21]), "dve")
                m.w(v, "dve")
                m.i(v.tensor_tensor(sm[:, 21:22], sm[:, 19:20], sm[:, 17:18], ALU.mult), "dve")
                m.w(v, "dve")
                m.i(v.tensor_scalar_max(sm[:, 21:22], sm[:, 21:22], AQ_CLIP), "dve")
                m.w(v, "dve")
                m.i(v.reciprocal(sm[:, 22:23], sm[:, 21:22]), "dve")
                m.w(v, "dve")
                m.i(v.tensor_scalar_mul(sm[:, 22:23], sm[:, 22:23], 127.0), "dve")
                m.w(v, "dve")
                m.i(v.tensor_tensor(sm[:, 20:21], sm[:, 19:20], sm[:, 22:23], ALU.mult), "dve")
                m.w(v, "dve")
                m.i(v.reciprocal(sm[:, 21:22], sm[:, 22:23]), "dve")
                m.w(v, "dve")
                m.i(v.tensor_tensor(k3[:, tt:tt + 1], sm[:, 21:22], avec[:], ALU.mult), "dve")
            blk.vector(v2)

            for ch in range(I // 512):
                def vB(v, ch=ch, tt=tt, hbuf=hbuf):
                    if ch == 0:
                        _gate(m, v, "hd_q8")
                    m.i(v.scalar_tensor_tensor(gout[:], hbuf[:, ch * 512:(ch + 1) * 512],
                                               sm[:, 20:21], g3rep[:, ch * 512:(ch + 1) * 512],
                                               ALU.mult, ALU.mult), "dve")
                    m.w(v, "dve")
                    m.i(v.tensor_scalar(q8[:, ch * 512:(ch + 1) * 512], gout[:],
                                        MAGIC, MAGIC, ALU.add, ALU.subtract), "dve")
                    if ch == I // 512 - 1:
                        _mark(m, f"hd_h{tt % 2}", "dve")
                blk.vector(vB)

            if lt == 0:
                def s2(s):
                    m.w(s, "dve")
                    _gate(m, s, "hd_qb")
                    m.i(s.copy(qb[:], q8[:]), "act")
                    _mark(m, "hd_q8", "act")
                blk.scalar(s2)

                def vr(v):
                    m.w(v, "act")
                    m.i(v.engine_nop(), "dve")
                blk.vector(vr)
                tokQ[lt] = ("dve", m.counts["dve"])
        return tokQ

    def emit_transp(grp, tokQ):
        for lt in range(GSZ):
            if lt > 0:
                def s2b(s):
                    m.w(s, "dve")
                    _gate(m, s, "hd_qb")
                    m.i(s.copy(qb[:], q8[:]), "act")
                    _mark(m, "hd_q8", "act")
                blk.scalar(s2b)

                def vrb(v):
                    m.w(v, "act")
                    m.i(v.engine_nop(), "dve")
                blk.vector(vrb)
                tokQ[lt] = ("dve", m.counts["dve"])
            for icg in range(0, IC, 4):
                g2 = (icg // 4) % 2

                def tb3(t, icg=icg, g2=g2, lt=lt):
                    if icg == 0:
                        m.wt(t, tokQ[lt])
                    _gate(m, t, f"hd_pt{g2}")
                    for s in range(4):
                        m.i(t.transpose(pt[g2][:, s * 128:(s + 1) * 128],
                                        qb[:, (icg + s) * 128:(icg + s + 1) * 128],
                                        m.sbc["ident"][:]), "pe")
                blk.tensor(tb3)

                def ve(v, icg=icg, g2=g2, lt=lt):
                    m.w(v, "pe")
                    if lt == 0 and icg == 0:
                        _gate(m, v, "hd_hq")
                    for s in range(4):
                        ic = icg + s
                        m.i(v.tensor_copy(hq[:, ic * GT + lt * 128: ic * GT + (lt + 1) * 128],
                                          pt[g2][:, s * 128:(s + 1) * 128]), "dve")
                    _mark(m, f"hd_pt{g2}", "dve")
                    if icg + 4 == IC:
                        _mark(m, "hd_qb", "dve")
                blk.vector(ve)
        return ("dve", m.counts["dve"])

    def emit_down(grp, tokHQ):
        for dg in range(DG):
            for half in range(2):
                def ld(g, dg=dg, half=half):
                    _gate(m, g, "hd_t3h")
                    t = m.dma("qgp", g, out=sb3(t3h[:], ICH),
                              in_=m.dram["t3i"][dg, half * (I // 2):(half + 1) * (I // 2), :]
                              .rearrange("(c p) n -> p c n", p=128),
                              fence=False, stream=7)
                    _markt(m, "hd_t3h_rdy", t)
                blk.gpsimd(ld)

                for lt in range(GSZ):
                    tt = grp * GSZ + lt
                    ii = lt % 2

                    def tb4(t, dg=dg, half=half, lt=lt, ii=ii):
                        _gate(m, t, "hd_t3h_rdy"); m.wt(t, tokHQ)
                        if half == 0:
                            _gate(m, t, f"hd_pD{ii}")
                        for icl in range(ICH):
                            ic = half * ICH + icl
                            inst = t.matmul(pD[ii][:], hq[:, ic * GT + lt * 128: ic * GT + (lt + 1) * 128],
                                            t3h[:, icl * 512:(icl + 1) * 512],
                                            start=(ic == 0), stop=(ic == IC - 1))
                        m.i(inst, "pe")
                        if lt == GSZ - 1:
                            _mark(m, "hd_t3h", "pe")
                        if dg == DG - 1 and half == 1 and lt == GSZ - 1:
                            _mark(m, "hd_hq", "pe")
                    blk.tensor(tb4)

                    if half == 1:
                        gdni = gdn[ii][:]
                        tokD = ("pe", m.counts["pe"])

                        def sb_(s, dg=dg, tt=tt, ii=ii, gdni=gdni, tokD=tokD):
                            m.wt(s, tokD)
                            _gate(m, s, f"hd_gdn{ii}")
                            m.i(s.activation(gdni, pD[ii][:], AF.Silu,
                                             scale=k3[:, tt:tt + 1]), "act")
                        blk.scalar(sb_)

                        def vrd(v, ii=ii):
                            m.w(v, "act")
                            m.i(v.engine_nop(), "dve")
                            _mark(m, f"hd_pD{ii}", "dve")
                        blk.vector(vrd)
                        tokR = ("act", m.counts["act"])

                        def st(q, dg=dg, tt=tt, ii=ii, gdni=gdni, tokR=tokR):
                            m.wt(q, tokR)
                            t = m.dma("qsy", q, out=m.dram["g1tm"][tt * 128:(tt + 1) * 128,
                                                                   dg * 512:(dg + 1) * 512],
                                      in_=gdni, fence=False, stream=5 + ii)
                            _markt(m, f"hd_gdn{ii}", t)
                        blk.sync(st)

    NGRP = TT // GSZ
    tokHQ = None
    for grp in range(NGRP):
        tokQ = emit_quant(grp)
        if grp > 0:
            emit_down(grp - 1, tokHQ)
        tokHQ = emit_transp(grp, tokQ)
    emit_down(NGRP - 1, tokHQ)


PHASES["hdown"] = dict(statics=["g3v"], reads=["hdr", "t3i", "terns"],
                       writes=["g1tm"], name="hdown", emit=emit_hdown)



def emit_proj1(m, blk, ps):
    nc, c = m.nc, m.c
    D, DC, T, S, AD, ADC, NG, NTG = c["D"], c["DC"], c["T"], c["S"], c["AD"], c["ADC"], c["NG"], c["NTG"]
    wq = ps.enter_context(nc.sbuf_tensor("p1_wq", [128, DC * AD], BF16))
    wk = ps.enter_context(nc.sbuf_tensor("p1_wk", [128, DC * AD], BF16))
    wv = ps.enter_context(nc.sbuf_tensor("p1_wv", [128, DC * AD], BF16))
    xb = ps.enter_context(nc.sbuf_tensor("p1_xb", [128, DC * T], BF16))
    stg = ps.enter_context(nc.sbuf_tensor("p1_stg", [128, T], BF16))
    vst = ps.enter_context(nc.sbuf_tensor("p1_vst", [128, AD], BF16))
    pA = [ps.enter_context(nc.psum_tensor(f"p1_pA{i}", [128, NG], F32)) for i in range(2)]

    def wl(g):
        m.dma("qgp", g, out=sb3(wq[:], DC), in_=fm_ap(m.dram["wqTs"][:]))
        m.dma("qgp", g, out=sb3(wk[:], DC), in_=fm_ap(m.dram["wkTs"][:]))
        m.dma("qgp", g, out=sb3(wv[:], DC), in_=fm_ap(m.dram["wvTs"][:]))
    blk.gpsimd(wl)

    for h in range(2):
        def lx(q, h=h):
            _gate(m, q, "p1_xb")
            m.dma("qsy", q, out=sb3(xb[:], DC),
                  in_=fm_ap(m.dram["c4out"][h * D:(h + 1) * D, :]))
        blk.sync(lx)

        for wsb, dst in ((wq, "qfm1"), (wk, "kfm1")):
            for co in range(ADC):
                for tg in range(NTG):
                    i = (co * NTG + tg) % 2

                    def tb(t, h=h, wsb=wsb, co=co, tg=tg, i=i):
                        m.w(t, "qsy"); m.w(t, "qgp")
                        _gate(m, t, f"p1_pA{i}")
                        for cc in range(DC):
                            inst = t.matmul(pA[i][:], wsb[:, cc * AD + co * 128: cc * AD + co * 128 + 128],
                                            xb[:, cc * T + tg * NG: cc * T + (tg + 1) * NG],
                                            start=(cc == 0), stop=(cc == DC - 1))
                        m.i(inst, "pe")
                        if wsb is wk and co == ADC - 1 and tg == NTG - 1:
                            pass
                    blk.tensor(tb)

                    def vb(v, co=co, tg=tg, i=i):
                        m.w(v, "pe")
                        if tg == 0:
                            _gate(m, v, "p1_stg")
                        m.i(v.tensor_copy(stg[:, tg * NG:(tg + 1) * NG], pA[i][:]), "dve")
                        _mark(m, f"p1_pA{i}", "dve")
                    blk.vector(vb)

                def st(q, h=h, dst=dst, co=co):
                    m.w(q, "dve")
                    m.dma("qsy", q, out=m.dram[dst][co * 128:(co + 1) * 128,
                                                    h * T:(h + 1) * T], in_=stg[:])
                    _mark(m, "p1_stg", "qsy")
                blk.sync(st)

        for tt in range(c["TT"]):
            for vg in range(max(1, AD // 512)):
                ncols = min(512, AD)
                i = (tt + vg) % 2

                def tv(t, h=h, tt=tt, vg=vg, i=i, ncols=ncols):
                    m.w(t, "qsy"); m.w(t, "qgp")
                    _gate(m, t, f"p1_pA{i}")
                    for cc in range(DC):
                        inst = t.matmul(pA[i][:, 0:ncols],
                                        xb[:, cc * T + tt * 128: cc * T + (tt + 1) * 128],
                                        wv[:, cc * AD + vg * 512: cc * AD + vg * 512 + ncols],
                                        start=(cc == 0), stop=(cc == DC - 1))
                    m.i(inst, "pe")
                blk.tensor(tv)

                def vv(v, tt=tt, vg=vg, i=i, ncols=ncols):
                    m.w(v, "pe")
                    if vg == 0:
                        _gate(m, v, "p1_vst")
                    m.i(v.tensor_copy(vst[:, vg * 512:vg * 512 + ncols], pA[i][:, 0:ncols]), "dve")
                    _mark(m, f"p1_pA{i}", "dve")
                blk.vector(vv)

            def sv(q, h=h, tt=tt):
                m.w(q, "dve")
                m.dma("qsy", q, out=m.dram["vtm1"][h * T + tt * 128: h * T + (tt + 1) * 128, :],
                      in_=vst[:])
                _mark(m, "p1_vst", "qsy")
            blk.sync(sv)
        if h == 0:
            def gx(v):
                m.w(v, "pe")
                m.i(v.engine_nop(), "dve")
                _mark(m, "p1_xb", "dve")
            blk.vector(gx)


PHASES["proj1"] = dict(statics=["wqTs", "wkTs", "wvTs"], reads=["c4out"],
                       writes=["qfm1", "kfm1", "vtm1"], name="proj1", emit=emit_proj1)


def emit_flash(m, blk, ps):
    nc, c = m.nc, m.c
    S, SC, AD, ADC, HC, HD1 = c["S"], c["SC"], c["AD"], c["ADC"], c["HC"], c["HD1"]
    iscale = 1.0 / float(np.sqrt(HD1))
    kf = ps.enter_context(nc.sbuf_tensor("fl_kf", [128, ADC * S], BF16))
    qh = [ps.enter_context(nc.sbuf_tensor(f"fl_qh{i}", [128, S], BF16)) for i in range(2)]
    vh = [ps.enter_context(nc.sbuf_tensor(f"fl_vh{i}", [128, SC * 128], BF16)) for i in range(2)]
    ssb = ps.enter_context(nc.sbuf_tensor("fl_ssb", [128, 128], F32))
    psb = [ps.enter_context(nc.sbuf_tensor(f"fl_psb{i}", [128, 1024], BF16)) for i in range(2)]
    atb = [ps.enter_context(nc.sbuf_tensor(f"fl_atb{i}", [128, 128], BF16)) for i in range(2)]
    rec = [ps.enter_context(nc.sbuf_tensor(f"fl_rec{i}", [128, 1], F32)) for i in range(2)]
    afm = [ps.enter_context(nc.sbuf_tensor(f"fl_afm{i}", [128, S], BF16)) for i in range(2)]
    pS = [ps.enter_context(nc.psum_tensor(f"fl_pS{i}", [128, 1024], F32)) for i in range(2)]
    pA = [ps.enter_context(nc.psum_tensor(f"fl_pA{i}", [128, 128], BF16)) for i in range(2)]
    pO = [ps.enter_context(nc.psum_tensor(f"fl_pO{i}", [128, 132], F32)) for i in range(2)]

    def lk(q):
        m.dma("qsy", q, out=sb3(kf[:], ADC), in_=fm_ap(m.dram["kfm1"][:]))
    blk.sync(lk)

    for hd in range(HC):
        hb = hd % 2

        def lq(q, hd=hd, hb=hb):
            _gate(m, q, f"fl_qh{hb}")
            m.dma("qsy", q, out=qh[hb][:], in_=m.dram["qfm1"][hd * 128:(hd + 1) * 128, :],
                  fence=False, stream=hb)
            t2 = m.dma("qsy", q, out=sb3(vh[hb][:], SC),
                       in_=m.dram["vtm1"][:, hd * 128:(hd + 1) * 128]
                       .rearrange("(t p) n -> p t n", p=128), fence=False, stream=hb)
            _markt(m, f"fl_qv{hb}_rdy", t2)
        blk.sync(lq)

        for qt in range(SC):
            po = pO[qt % 2]
            ep_tok = m.free.get(f"fl_pO{qt % 2}", None)
            nsub = qt + 1
            ngrp = (nsub + 7) // 8
            for grp in range(ngrp):
                s0g = grp * 8
                s1g = min(nsub, s0g + 8)
                nsg = s1g - s0g
                i = grp % 2

                def ts_(t, hd=hd, hb=hb, qt=qt, s0g=s0g, s1g=s1g, i=i):
                    _gate(m, t, f"fl_qv{hb}_rdy")
                    _gate(m, t, f"fl_pS{i}")
                    for s in range(s0g, s1g):
                        m.i(t.matmul(pS[i][:, (s - s0g) * 128:(s - s0g + 1) * 128],
                                     kf[:, hd * S + s * 128: hd * S + (s + 1) * 128],
                                     qh[hb][:, qt * 128:(qt + 1) * 128],
                                     start=True, stop=True), "pe")
                blk.tensor(ts_)
                tokS = ("pe", m.counts["pe"])

                has_diag = (s1g == nsub)
                if has_diag:
                    def vm(v, i=i, nsg=nsg, tokS=tokS):
                        m.wt(v, tokS)
                        _gate(m, v, "fl_ssb")
                        m.i(v.tensor_add(ssb[:], pS[i][:, (nsg - 1) * 128:nsg * 128],
                                         m.sbc["maskt"][:]), "dve")
                    blk.vector(vm)
                    tokM = ("dve", m.counts["dve"])

                    def se(s, i=i, nsg=nsg, tokM=tokM, tokS=tokS):
                        m.wt(s, tokS)
                        _gate(m, s, f"fl_psb{i}")
                        for off in range(0, (nsg - 1) * 128, 512):
                            ch = min(512, (nsg - 1) * 128 - off)
                            m.i(s.activation(psb[i][:, off:off + ch],
                                             pS[i][:, off:off + ch], AF.Exp,
                                             scale=iscale), "act")
                        m.wt(s, tokM)
                        m.i(s.activation(psb[i][:, (nsg - 1) * 128:nsg * 128], ssb[:],
                                         AF.Exp, scale=iscale), "act")
                        _mark(m, "fl_ssb", "act")
                        _mark(m, f"fl_pS{i}", "act")
                    blk.scalar(se)
                else:
                    def se2(s, i=i, nsg=nsg, tokS=tokS):
                        m.wt(s, tokS)
                        _gate(m, s, f"fl_psb{i}")
                        for off in range(0, nsg * 128, 512):
                            ch = min(512, nsg * 128 - off)
                            m.i(s.activation(psb[i][:, off:off + ch],
                                             pS[i][:, off:off + ch],
                                             AF.Exp, scale=iscale), "act")
                        _mark(m, f"fl_pS{i}", "act")
                    blk.scalar(se2)
                tokE = ("act", m.counts["act"])

                def vr(v, tokE=tokE):
                    m.wt(v, tokE)
                    m.i(v.engine_nop(), "dve")
                blk.vector(vr)
                tokP = ("dve", m.counts["dve"])

                def tv(t, hd=hd, hb=hb, qt=qt, s0g=s0g, s1g=s1g, i=i, nsub=nsub,
                       tokP=tokP, ep_tok=ep_tok, po=po):
                    m.wt(t, tokP)
                    if s0g == 0 and ep_tok is not None:
                        m.wt(t, ep_tok)
                    for s in range(s0g, s1g):
                        first = (s == 0)
                        last = (s == nsub - 1)
                        sl = psb[i][:, (s - s0g) * 128:(s - s0g + 1) * 128]
                        t.matmul(po[:, 0:128], sl, vh[hb][:, s * 128:(s + 1) * 128],
                                 start=first, stop=False)
                        m.i(t.matmul(po[:, 128:129], sl, m.sbc["onesb"][:],
                                     start=False, stop=last), "pe")
                    _mark(m, f"fl_psb{i}", "pe")
                blk.tensor(tv)
            tok_pv = ("pe", m.counts["pe"])

            def vep(v, qt=qt, po=po, tok_pv=tok_pv):
                m.wt(v, tok_pv)
                _gate(m, v, f"fl_atb{qt % 2}")
                m.i(v.reciprocal(rec[qt % 2][:], po[:, 128:129]), "dve")
                m.w(v, "dve")
                m.i(v.tensor_single_scalar(atb[qt % 2][:], po[:, 0:128],
                                           rec[qt % 2][:], ALU.mult), "dve")
                _mark(m, f"fl_pO{qt % 2}", "dve")
            blk.vector(vep)
            tokEp = ("dve", m.counts["dve"])

            def tat(t, qt=qt, tokEp=tokEp):
                m.wt(t, tokEp)
                _gate(m, t, f"fl_pA{qt % 2}")
                m.i(t.transpose(pA[qt % 2][:], atb[qt % 2][:], m.sbc["ident"][:]), "pe")
                _mark(m, f"fl_atb{qt % 2}", "pe")
            blk.tensor(tat)
            tokA = ("pe", m.counts["pe"])

            def vae(v, hd=hd, qt=qt, hb=hb, tokA=tokA):
                m.wt(v, tokA)
                if qt == 0:
                    _gate(m, v, f"fl_afm{hb}")
                m.i(v.tensor_copy(afm[hb][:, qt * 128:(qt + 1) * 128], pA[qt % 2][:]), "dve")
                _mark(m, f"fl_pA{qt % 2}", "dve")
            blk.vector(vae)

        def sa(q, hd=hd, hb=hb):
            m.w(q, "dve")
            t = m.dma("qsy", q, out=m.dram["atfm"][hd * 128:(hd + 1) * 128, :], in_=afm[hb][:],
                      fence=False, stream=2 + hb)
            _markt(m, f"fl_afm{hb}", t)
            _markt(m, f"fl_qh{hb}", t)
        blk.sync(sa)


PHASES["flash"] = dict(statics=[], reads=["qfm1", "kfm1", "vtm1"],
                       writes=["atfm"], name="flash", emit=emit_flash)


def emit_s1proj(m, blk, ps):
    nc, c = m.nc, m.c
    D, S, SC, AD, ADC = c["D"], c["S"], c["SC"], c["AD"], c["ADC"]
    wo = ps.enter_context(nc.sbuf_tensor("s1_wo", [128, ADC * D], BF16))
    att = ps.enter_context(nc.sbuf_tensor("s1_at", [128, ADC * 128], BF16))
    ses = ps.enter_context(nc.sbuf_tensor("s1_ses", [128, D], F32))
    pE = ps.enter_context(nc.psum_tensor("s1_pE", [128, min(2048, D)], F32))

    def wl(g):
        m.dma("qgp", g, out=sb3(wo[:], ADC), in_=fm_ap(m.dram["woTs"][:]))
    blk.gpsimd(wl)

    for tt in range(SC):
        def lt(q, tt=tt):
            _gate(m, q, "s1_at")
            m.dma("qsy", q, out=sb3(att[:], ADC),
                  in_=fm_ap(m.dram["atfm"][:, tt * 128:(tt + 1) * 128]))
        blk.sync(lt)

        def tb(t, tt=tt):
            m.w(t, "qsy"); m.w(t, "qgp")
            _gate(m, t, "s1_pE")
            for dg in range(D // 512):
                for cc in range(ADC):
                    inst = t.matmul(pE[:, dg * 512:(dg + 1) * 512],
                                    att[:, cc * 128:(cc + 1) * 128],
                                    wo[:, cc * D + dg * 512: cc * D + (dg + 1) * 512],
                                    start=(cc == 0), stop=(cc == ADC - 1))
            m.i(inst, "pe")
            _mark(m, "s1_at", "pe")
        blk.tensor(tb)

        def vb(v, tt=tt):
            m.w(v, "pe")
            _gate(m, v, "s1_ses")
            m.i(v.tensor_copy(ses[:], pE[:]), "dve")
            _mark(m, "s1_pE", "dve")
        blk.vector(vb)

        def st(q, tt=tt):
            m.w(q, "dve")
            m.dma("qsy", q, out=m.dram["s1p"][tt * 128:(tt + 1) * 128, :], in_=ses[:])
            _mark(m, "s1_ses", "qsy")
        blk.sync(st)


PHASES["s1proj"] = dict(statics=["woTs"], reads=["atfm"], writes=["s1p"],
                        name="s1proj", emit=emit_s1proj)


def emit_comb1(m, blk, ps):
    nc, c = m.nc, m.c
    D, T, TT = c["D"], c["T"], c["TT"]
    x1t = ps.enter_context(nc.sbuf_tensor("cb_x1t", [128, D], F32))
    g1t = ps.enter_context(nc.sbuf_tensor("cb_g1t", [128, D], F32))
    s1t = ps.enter_context(nc.sbuf_tensor("cb_s1t", [128, D], F32))
    s1b = ps.enter_context(nc.sbuf_tensor("cb_s1b", [128, D], F32))
    ot = ps.enter_context(nc.sbuf_tensor("cb_ot", [128, D], F32))

    for tt in range(TT):
        def lt(q, tt=tt):
            _gate(m, q, "cb_x1t")
            m.dma("qsy", q, out=x1t[:], in_=m.dram["x1"][tt * 128:(tt + 1) * 128, :])
            _gate(m, q, "cb_g1t")
            m.dma("qsy", q, out=g1t[:], in_=m.dram["g1tm"][tt * 128:(tt + 1) * 128, :])
            _gate(m, q, "cb_s1t")
            m.dma("qsy", q, out=s1t[:], in_=m.dram["s1pA"][tt * 128:(tt + 1) * 128, :])
            m.dma("qsy", q, out=s1b[:], in_=m.dram["s1pB"][tt * 128:(tt + 1) * 128, :])
        blk.sync(lt)

        def vb(v, tt=tt):
            m.w(v, "qsy")
            m.i(v.tensor_add(s1t[:], s1t[:], s1b[:]), "dve")
            m.w(v, "dve")
            m.i(v.tensor_mul(g1t[:], s1t[:], g1t[:]), "dve")
            _mark(m, "cb_s1t", "dve")
            m.w(v, "dve")
            _gate(m, v, "cb_ot")
            m.i(v.tensor_add(ot[:], x1t[:], g1t[:]), "dve")
            _mark(m, "cb_x1t", "dve")
            _mark(m, "cb_g1t", "dve")
        blk.vector(vb)

        def st(q, tt=tt):
            m.w(q, "dve")
            m.dma("qsy", q, out=m.dram["out"][tt * 128:(tt + 1) * 128, :], in_=ot[:])
            _mark(m, "cb_ot", "qsy")
        blk.sync(st)


def _comb1_reads(fused):
    return ["x1", "g1tm"] + (["c5out"] if fused else ["s1pA", "s1pB"])


PHASES["comb1"] = dict(statics=[], reads=["x1", "g1tm", "s1pA", "s1pB"],
                       writes=["out"], name="comb1", emit=emit_comb1)
